# version 07
# baseline (speedup 1.0000x reference)
import math
from contextlib import ExitStack

import numpy as np

import concourse.bass as bass
import concourse.mybir as mybir
from concourse.bass_utils import run_bass_kernel_spmd

F32 = mybir.dt.float32
BF16 = mybir.dt.bfloat16
I32 = mybir.dt.int32
AF = mybir.ActivationFunctionType
ALU = mybir.AluOpType
AX = mybir.AxisListType

S = 2048
TH = 1024
H = 4096
NH = 32
QR = 1536
KVR = 512
RD = 64
ND = 128
VD = 128
QK = ND + RD
SCALE = QK ** -0.5
NKC = H // 128
EPS_RMS = 1e-6
EPS_LN = 1e-5
TWO_PI = 2.0 * math.pi

ENGINES = ("sync", "gpsimd", "tensor", "vector", "scalar")

DEBUG_TAPS = []
DEBUG_LAST_PHASE = None
DEBUG_SUB = None


def ts2(nt):
    return slice(nt * 512, (nt + 1) * 512)


class Rec:

    def __init__(self, nc, stack):
        self.nc = nc
        self.ops = {e: [] for e in ENGINES}
        self.sems = {}
        self.cnt = {}
        self.wait_cache = {e: {} for e in ENGINES}
        for s in ("s_sms", "s_smg", "s_abf", "s_abfs", "s_strA", "s_strB", "s_strC", "s_xa", "s_xb",
                  "s_h0", "s_h1", "s_wA", "s_wB", "s_resA", "s_resB",
                  "s_dq", "s_wd", "s_oA", "s_oB", "s_oC", "s_oD",
                  "s_ca", "s_cb", "s_cc", "s_cd", "s_ra", "s_rb", "s_rc", "s_rd",
                  "s_pe", "s_dve", "s_act", "s_gp"):
            self.sems[s] = stack.enter_context(nc.semaphore(s))
            self.cnt[s] = 0

    def op(self, eng, fn, waits=(), inc=None, inc_by=1):
        w = tuple((s, v) for t in waits if t is not None for (s, v) in (t,) if v > 0)
        self.ops[eng].append((w, fn, inc, inc_by))
        if inc is not None:
            self.cnt[inc] += inc_by
            return (inc, self.cnt[inc])
        return None

    def dma(self, eng, out, in_, waits=(), sem=None):
        assert sem is not None, "explicit dma semaphore family required"
        return self.op(eng, lambda E, o=out, i=in_: E.dma_start(out=o, in_=i),
                       waits, inc=sem, inc_by=16)

    def fence(self, sems=("s_pe", "s_dve", "s_act")):
        return [(s, self.cnt[s]) for s in sems]

    def flush(self):
        nc = self.nc
        todo = {e: self.ops[e] for e in ENGINES if self.ops[e]}
        if not todo:
            return
        with nc.Block() as blk:
            for engname, oplist in todo.items():
                def make_cb(oplist=oplist, engname=engname):
                    def cb(E):
                        cache = self.wait_cache[engname]
                        for waits, fn, inc, n in oplist:
                            for sname, val in waits:
                                if val > cache.get(sname, 0):
                                    E.wait_ge(self.sems[sname], val)
                                    cache[sname] = val
                            inst = fn(E)
                            if inc is not None:
                                inst.then_inc(self.sems[inc], n)
                    return cb
                getattr(blk, engname)(make_cb())
        self.ops = {e: [] for e in ENGINES}


def build_nc(debug_taps=(), last_phase=None):
    nc = bass.Bass("TRN2", target_bir_lowering=False, detect_race_conditions=False)

    xT = nc.dram_tensor("xT", [H, S], F32, kind="ExternalInput").ap()
    w_qa = nc.dram_tensor("w_qa", [H, QR], F32, kind="ExternalInput").ap()
    g_qa = nc.dram_tensor("g_qa", [QR], F32, kind="ExternalInput").ap()
    w_qb = nc.dram_tensor("w_qb", [QR, NH * QK], F32, kind="ExternalInput").ap()
    w_kva = nc.dram_tensor("w_kva", [H, KVR + RD], F32, kind="ExternalInput").ap()
    g_kva = nc.dram_tensor("g_kva", [KVR], F32, kind="ExternalInput").ap()
    w_kvb = nc.dram_tensor("w_kvb", [KVR, NH * (ND + VD)], F32, kind="ExternalInput").ap()
    w_o = nc.dram_tensor("w_o", [NH * VD, H], F32, kind="ExternalInput").ap()
    qwT = nc.dram_tensor("qwT", [H, H], mybir.dt.uint8, kind="ExternalInput").ap()
    sbc = nc.dram_tensor("sbc", [16, H], F32, kind="ExternalInput").ap()
    zbc = nc.dram_tensor("zbc", [16, H], F32, kind="ExternalInput").ap()
    bias_d = nc.dram_tensor("bias", [H], F32, kind="ExternalInput").ap()
    rotq_d = nc.dram_tensor("rotq", [128, 128], F32, kind="ExternalInput").ap()
    rotk_d = nc.dram_tensor("rotk", [64, 64], F32, kind="ExternalInput").ap()
    outT = nc.dram_tensor("outT", [H, S], F32, kind="ExternalOutput").ap()

    QN = nc.dram_tensor("QN_sc", [NH * ND, S], BF16).ap()
    QRS = nc.dram_tensor("QR_sc", [NH * RD, S], BF16).ap()
    KN = nc.dram_tensor("KN_sc", [NH * ND, S], BF16).ap()
    VSC = nc.dram_tensor("V_sc", [S, NH * VD], BF16).ap()
    ASC = nc.dram_tensor("A_sc", [NH * VD, S], BF16).ap()
    R1 = nc.dram_tensor("R1_sc", [H, S], F32).ap()
    WD = nc.dram_tensor("WD_sc", [H, H], BF16).ap()
    R2 = nc.dram_tensor("R2_sc", [H, S], BF16).ap()
    BOUNCE = nc.dram_tensor("BN_sc", [8192], F32).ap()
    scratch = {"QN_sc": QN, "QR_sc": QRS, "KN_sc": KN, "V_sc": VSC, "A_sc": ASC,
               "R1_sc": R1, "WD_sc": WD, "R2_sc": R2}

    dbg = {}
    for name in debug_taps:
        src = scratch[name]
        dbg[name] = nc.dram_tensor("dbg_" + name, list(src.shape), src.dtype,
                                   kind="ExternalOutput").ap()

    stack = ExitStack()
    R = Rec(nc, stack)

    def sb(name, shape, dt):
        return stack.enter_context(nc.sbuf_tensor(name, shape, dt)).ap()

    ABF = sb("ABF", [128, 32 * TH], BF16)
    strA = sb("strA", [128, 32 * 128], BF16)
    strB = sb("strB", [128, 32 * 128], BF16)
    evA = sb("evA", [128, TH], F32)
    evB = sb("evB", [128, TH], F32)
    evC = sb("evC", [128, TH], F32)
    sqA = sb("sqA", [128, TH], BF16)
    sqB = sb("sqB", [128, TH], BF16)
    stgA = sb("stgA", [128, TH], BF16)
    stgB = sb("stgB", [128, TH], BF16)
    xfA = sb("xfA", [128, TH], F32)
    xfB = sb("xfB", [128, TH], F32)
    rowsA = sb("rowsA", [1, 2 * TH], F32)
    ones1f = sb("ones1f", [1, 128], F32)
    onesCb = sb("onesCb", [128, 1], BF16)
    ones128b = sb("ones128b", [128, 128], BF16)
    gqa_c = sb("gqa_c", [128, 12], F32)
    gkva_c = sb("gkva_c", [128, 4], F32)
    bias_c = sb("bias_c", [128, 32], F32)
    rotqb = sb("rotqb", [128, 128], BF16)
    rotkb = sb("rotkb", [64, 64], BF16)
    rkcols = sb("rkcols", [128, 16], F32)
    nmhat = sb("nmhat", [128, 32], F32)
    mrA = sb("mrA", [1, 32], F32)
    mrB = sb("mrB", [1, 32], F32)
    mrC = sb("mrC", [1, 32], F32)
    constT = sb("constT", [128, 8], F32)
    _consts = {}

    def creg(val):
        if val not in _consts:
            idx = len(_consts)
            ap = constT[:, idx:idx + 1]
            _consts[val] = ap
            nc.const_aps.register(F32, val, ap)
        return _consts[val]

    for v in (math.pi / 2.0, EPS_RMS, EPS_LN):
        creg(v)

    p12_stack = ExitStack()
    def sb12(name, shape, dt):
        return p12_stack.enter_context(nc.sbuf_tensor(name, shape, dt)).ap()
    krop = sb12("krop", [128, S], BF16)
    selN = sb12("selN", [128, 32 * 32], BF16)
    selR = sb12("selR", [128, 16 * 32], BF16)
    qn2max = sb12("qn2max", [32, 1], F32)
    kn2max = sb12("kn2max", [32, 1], F32)
    kr2max = sb12("kr2max", [1, 1], F32)
    n2tmp = sb12("n2tmp", [32, 1], F32)

    trig_stack = ExitStack()
    cosd = trig_stack.enter_context(nc.sbuf_tensor("cosd", [128, S], F32)).ap()
    sind = trig_stack.enter_context(nc.sbuf_tensor("sind", [128, S], F32)).ap()

    ROW = {"r0": rowsA[:, 0:TH], "r1": rowsA[:, TH:2 * TH],
           "t0": xfA[0:1, :], "t1": xfB[0:1, :]}

    def gemm(n_m, n_k, stripe_dma, rhs_of, ps_pair, epilogue,
             rhs_wait_of=None, mwid_of=None, extra_first_waits=(), stripes=None):
        if stripes is None:
            stripes = (strA, strB)
        nstr = len(stripes)
        stripe_tick = [None] * nstr
        ep_free = [(), ()]
        pe_tick = [None] * n_m
        pe_tails = [None] * n_m
        init_w = [("s_pe", R.cnt["s_pe"])]
        for m in range(min(nstr, n_m)):
            stripe_tick[m % nstr] = stripe_dma(m, stripes[m % nstr], m % nstr,
                                               extra_waits=init_w)
        for m in range(n_m):
            par = m % 2
            spar = m % nstr
            ps = ps_pair[par]
            buf = stripes[spar]
            mwid = 128 if mwid_of is None else mwid_of(m)
            if m >= 1 and pe_tails[m - 1] is not None:
                pe_tails[m - 1]()
            first = True
            last_t = None
            for k in range(n_k):
                for nt in range(2):
                    w = []
                    if first:
                        w = [stripe_tick[spar]] + list(ep_free[par]) + list(extra_first_waits)
                        first = False
                    if nt == 0 and rhs_wait_of is not None:
                        w.append(rhs_wait_of(k))
                    last_t = R.op("tensor", lambda E, ps=ps, buf=buf, k=k, nt=nt,
                                  mwid=mwid, n_k=n_k: E.matmul(
                                      ps[0:mwid, ts2(nt)],
                                      buf[:, k * mwid:(k + 1) * mwid],
                                      rhs_of(k, nt),
                                      start=(k == 0), stop=(k == n_k - 1)),
                                  waits=w,
                                  inc=("s_pe" if (k == n_k - 1 and nt == 1) else None))
            pe_tick[m] = last_t
            if m + nstr < n_m:
                stripe_tick[spar] = stripe_dma(m + nstr, stripes[spar], spar,
                                               extra_waits=[pe_tick[m]])
            free_waits, tail = epilogue(m, ps, pe_tick[m])
            ep_free[par] = free_waits
            pe_tails[m] = tail
        if pe_tails[n_m - 1] is not None:
            pe_tails[n_m - 1]()
        return pe_tick

    STR_SEMS = ("s_strA", "s_strB", "s_strC")

    def mk_stripe(wap, n_k, eng="gpsimd", mwid=128, kpar=128, static_waits=None):
        def f(m, buf, slot, extra_waits=()):
            w = list(extra_waits) + (static_waits() if static_waits else [])
            return R.dma(eng,
                         buf[:, 0:n_k * mwid].rearrange("p (k m) -> p k m", k=n_k),
                         wap[:, m * mwid:(m + 1) * mwid]
                         .rearrange("(k p) m -> p k m", p=kpar),
                         waits=w, sem=STR_SEMS[slot])
        return f

    def phase0():
        with (nc.sbuf_tensor("trig_u", [64, S], F32) as h1,
              nc.sbuf_tensor("trig_i", [64, S], I32) as h2,
              nc.sbuf_tensor("trig_y", [64, S], F32) as h3,
              nc.sbuf_tensor("trig_t", [64, S], F32) as h3b,
              nc.sbuf_tensor("ifrow", [1, 64], F32) as h4,
              nc.sbuf_tensor("itrow", [1, S], F32) as h5,
              nc.psum_tensor("ps_ang", [64, S], F32) as h6):
            trig_u, trig_i, trig_y = h1.ap(), h2.ap(), h3.ap()
            trig_t = h3b.ap()
            ifrow, itrow, ps_ang = h4.ap(), h5.ap(), h6.ap()

            for val, ap in _consts.items():
                R.op("vector", lambda E, ap=ap, val=val: E.memset(ap, val))
            R.op("vector", lambda E: E.memset(ones1f, 1.0))
            R.op("vector", lambda E: E.memset(onesCb, 1.0))
            R.op("vector", lambda E: E.memset(ones128b, 1.0))
            R.op("vector", lambda E: E.memset(qn2max, 0.0))
            R.op("vector", lambda E: E.memset(kn2max, 0.0))
            R.op("vector", lambda E: E.memset(kr2max, 0.0))
            R.op("vector", lambda E: E.memset(krop, 0.0))
            R.op("vector", lambda E: E.memset(selN, 0.0))
            R.op("vector", lambda E: E.memset(selR, 0.0))
            for h in range(32):
                R.op("vector", lambda E, h=h: E.memset(
                    selN[:, h * 32 + h:h * 32 + h + 1], 1.0))
            for j in range(16):
                R.op("vector", lambda E, j=j: E.memset(
                    selR[0:64, j * 32 + 2 * j:j * 32 + 2 * j + 1], 1.0))
                R.op("vector", lambda E, j=j: E.memset(
                    selR[64:128, j * 32 + 2 * j + 1:j * 32 + 2 * j + 2], 1.0))

            for c in range(12):
                R.dma("sync", gqa_c[:, c:c + 1],
                      g_qa[c * 128:(c + 1) * 128].rearrange("(p o) -> p o", o=1), sem="s_sms")
            for c in range(4):
                R.dma("sync", gkva_c[:, c:c + 1],
                      g_kva[c * 128:(c + 1) * 128].rearrange("(p o) -> p o", o=1), sem="s_sms")
            for c in range(32):
                R.dma("sync", bias_c[:, c:c + 1],
                      bias_d[c * 128:(c + 1) * 128].rearrange("(p o) -> p o", o=1), sem="s_sms")
            R.dma("gpsimd", rotqb, rotq_d, sem="s_smg")
            R.dma("gpsimd", rotkb, rotk_d, sem="s_smg")

            ti1 = R.op("gpsimd", lambda E: E.iota(
                trig_y[0:1, 0:32], [[1, 32]], base=0, channel_multiplier=0,
                allow_small_or_imprecise_dtypes=True), inc="s_gp")
            ti2 = R.op("gpsimd", lambda E: E.iota(
                itrow, [[1, S]], base=0, channel_multiplier=0,
                allow_small_or_imprecise_dtypes=True), inc="s_gp")
            a1 = R.op("scalar", lambda E: E.activation(
                ifrow[:, 0:32], trig_y[0:1, 0:32], AF.Exp,
                scale=-math.log(10000.0) / 32.0), waits=[ti1], inc="s_act")
            d0 = R.op("vector", lambda E: E.tensor_copy(ifrow[:, 32:64], ifrow[:, 0:32]),
                      waits=[a1], inc="s_dve")
            p_ang = None
            for n in range(4):
                p_ang = R.op("tensor", lambda E, n=n: E.matmul(
                    ps_ang[:, ts2(n)], ifrow[0:1, :], itrow[0:1, ts2(n)],
                    start=True, stop=True),
                    waits=[d0, ti2], inc=("s_pe" if n == 3 else None))
            def trig_table(out_ap, off, shift, w0):
                R.op("vector", lambda E: E.tensor_scalar(
                    trig_u, ps_ang, 1.0 / TWO_PI, off, op0=ALU.mult,
                    op1=(ALU.add if off else ALU.bypass)), waits=[w0])
                R.op("vector", lambda E: E.tensor_copy(trig_i, trig_u))
                R.op("vector", lambda E: E.tensor_copy(trig_u, trig_i))
                R.op("vector", lambda E: E.scalar_tensor_tensor(
                    trig_y, trig_u, TWO_PI, ps_ang, op0=ALU.mult, op1=ALU.subtract))
                if shift:
                    R.op("vector", lambda E: E.tensor_scalar(
                        trig_y, trig_y, shift, None, op0=ALU.subtract))
                R.op("vector", lambda E: E.tensor_scalar(
                    trig_u, trig_y, -math.pi, None, op0=ALU.is_lt))
                R.op("vector", lambda E: E.tensor_scalar(
                    trig_t, trig_y, math.pi, None, op0=ALU.is_gt))
                R.op("vector", lambda E: E.tensor_tensor(
                    trig_u, trig_u, trig_t, op=ALU.subtract))
                dd = R.op("vector", lambda E: E.scalar_tensor_tensor(
                    trig_t, trig_u, TWO_PI, trig_y, op0=ALU.mult, op1=ALU.add),
                    inc="s_dve")
                return R.op("scalar", lambda E, out_ap=out_ap: E.activation(
                    out_ap, trig_t, AF.Sin, scale=-1.0),
                    waits=[dd], inc="s_act")

            a3 = trig_table(sind[0:64, :], None, 0.0, p_ang)
            a4 = trig_table(cosd[0:64, :], 0.25, math.pi / 2.0, a3)
            R.dma("gpsimd", sind[64:128, :], sind[0:64, :], waits=[a4], sem="s_smg")
            R.dma("gpsimd", cosd[64:128, :], cosd[0:64, :], waits=[a4], sem="s_smg")
            R.flush()

    PH = {}

    def load_xbf(th, w0):
        thc = slice(th * TH, (th + 1) * TH)
        xt = []
        for q in range(4):
            xt.append(R.dma(
                "gpsimd",
                ABF[:, q * 8 * TH:(q + 1) * 8 * TH].rearrange("p (k t) -> p k t", k=8),
                xT[q * 1024:(q + 1) * 1024, thc].rearrange("(k p) t -> p k t", p=128),
                waits=w0 if q == 0 else (), sem="s_abf"))
        return xt

    def phase1(th):
        thc = slice(th * TH, (th + 1) * TH)
        with ExitStack() as pst:
            qag = pst.enter_context(nc.sbuf_tensor(f"qag{th}", [128, 12 * TH], BF16)).ap()
            kcg = pst.enter_context(nc.sbuf_tensor(f"kcg{th}", [128, 4 * TH], BF16)).ap()
            rqbc = pst.enter_context(nc.sbuf_tensor(f"rqbc{th}", [128, TH], F32)).ap()
            rkbc = pst.enter_context(nc.sbuf_tensor(f"rkbc{th}", [128, TH], F32)).ap()
            krbf = pst.enter_context(nc.sbuf_tensor(f"krbf{th}", [64, TH], BF16)).ap()
            qrbuf = pst.enter_context(nc.sbuf_tensor(f"qrbuf{th}", [128, 2 * TH], BF16)).ap()
            strC = pst.enter_context(nc.sbuf_tensor(f"strC{th}", [128, 32 * 128], BF16)).ap()
            str3 = (strA, strB, strC)

            with (nc.psum_tensor(f"ps_g0_{th}", [128, TH], F32) as hg0,
                  nc.psum_tensor(f"ps_g1_{th}", [128, TH], F32) as hg1,
                  nc.psum_tensor(f"ps_qs{th}", [1, TH], F32) as hqs,
                  nc.psum_tensor(f"ps_ks{th}", [1, TH], F32) as hks):
                ps_g = (hg0.ap(), hg1.ap())
                ps_qs, ps_ks = hqs.ap(), hks.ap()

                if th == 0:
                    xt = load_xbf(0, R.fence())
                else:
                    xt = PH["xt1"]

                def xbf_of(k, nt):
                    return ABF[:, k * TH + nt * 512: k * TH + nt * 512 + 512]

                if DEBUG_SUB == "x":
                    R.flush()
                    return

                st_q = {"first": True}

                def epi_qa_mm(m, ps, pe_w):
                    return [pe_w], None

                def epi_qa_act(m, ps, pe_w):
                    sq = (sqA, sqB)[m % 2]
                    a = R.op("scalar", lambda E, ps=ps, sq=sq: E.activation(
                        sq, ps, AF.Square), waits=[pe_w], inc="s_act")
                    return [a], None

                def epi_qa_dve(m, ps, pe_w):
                    d = R.op("vector", lambda E, ps=ps, m=m: E.tensor_scalar(
                        qag[:, m * TH:(m + 1) * TH], ps, gqa_c[:, m:m + 1], None,
                        op0=ALU.mult), waits=[pe_w], inc="s_dve")
                    return [d], None

                def epi_qa(m, ps, pe_w):
                    sq = (sqA, sqB)[m % 2]
                    a = R.op("scalar", lambda E, ps=ps, sq=sq: E.activation(
                        sq, ps, AF.Square), waits=[pe_w], inc="s_act")
                    d = R.op("vector", lambda E, ps=ps, m=m: E.tensor_scalar(
                        qag[:, m * TH:(m + 1) * TH], ps, gqa_c[:, m:m + 1], None,
                        op0=ALU.mult), waits=[pe_w, a], inc="s_dve")

                    def tail(m=m, sq=sq, a=a):
                        st = st_q["first"]
                        st_q["first"] = False
                        for nt in range(2):
                            R.op("tensor", lambda E, sq=sq, nt=nt, st=st, m=m: E.matmul(
                                ps_qs[:, ts2(nt)], onesCb, sq[:, ts2(nt)],
                                start=st, stop=(m == 11)),
                                waits=[a],
                                inc=("s_pe" if (m == 11 and nt == 1) else None))
                    return [a, d], tail

                epi_sel = {"qamm": epi_qa_mm, "qaact": epi_qa_act,
                           "qadve": epi_qa_dve}.get(DEBUG_SUB, epi_qa)
                gemm(12, NKC, mk_stripe(w_qa, NKC), xbf_of, ps_g, epi_sel,
                     rhs_wait_of=lambda k: xt[k // 8],
                     extra_first_waits=R.fence(("s_sms", "s_smg")), stripes=str3)
                if DEBUG_SUB in ("qa", "qamm", "qaact", "qadve"):
                    R.flush()
                    return

                def kva_stripe(m, buf, slot, extra_waits=()):
                    sem = STR_SEMS[slot]
                    if m < 4:
                        return R.dma("gpsimd",
                                     buf[:, 0:NKC * 128].rearrange("p (k m) -> p k m", k=NKC),
                                     w_kva[:, m * 128:(m + 1) * 128]
                                     .rearrange("(k p) m -> p k m", p=128),
                                     waits=list(extra_waits), sem=sem)
                    return R.dma("gpsimd",
                                 buf[:, 0:NKC * 64].rearrange("p (k m) -> p k m", k=NKC),
                                 w_kva[:, 512:576].rearrange("(k p) m -> p k m", p=128),
                                 waits=list(extra_waits), sem=sem)

                st_k = {"first": True}
                krope_tick = [None]

                def epi_kva(m, ps, pe_w):
                    if m < 4:
                        sq = (sqA, sqB)[m % 2]
                        a = R.op("scalar", lambda E, ps=ps, sq=sq: E.activation(
                            sq, ps, AF.Square), waits=[pe_w], inc="s_act")
                        d = R.op("vector", lambda E, ps=ps, m=m: E.tensor_scalar(
                            kcg[:, m * TH:(m + 1) * TH], ps, gkva_c[:, m:m + 1], None,
                            op0=ALU.mult), waits=[pe_w, a], inc="s_dve")

                        def tail(m=m, sq=sq, a=a):
                            st = st_k["first"]
                            st_k["first"] = False
                            for nt in range(2):
                                R.op("tensor", lambda E, sq=sq, nt=nt, st=st, m=m: E.matmul(
                                    ps_ks[:, ts2(nt)], onesCb, sq[:, ts2(nt)],
                                    start=st, stop=(m == 3)),
                                    waits=[a],
                                    inc=("s_pe" if (m == 3 and nt == 1) else None))
                        return [a, d], tail

                    a = R.op("scalar", lambda E, ps=ps: E.activation(
                        krbf, ps[0:64, :], AF.Copy), waits=[pe_w], inc="s_act")
                    other = ps_g[(m + 1) % 2]
                    pr = None
                    for nt in range(2):
                        pr = R.op("tensor", lambda E, other=other, nt=nt: E.matmul(
                            other[0:64, ts2(nt)], rotkb, krbf[:, ts2(nt)],
                            start=True, stop=True),
                            waits=[a] + R.fence(("s_act", "s_dve")),
                            inc=("s_pe" if nt == 1 else None))
                    R.op("vector", lambda E: E.tensor_tensor(
                        evA[0:64, :], krbf, cosd[0:64, thc], op=ALU.mult), waits=[a])
                    R.op("vector", lambda E, other=other: E.tensor_tensor(
                        evB[0:64, :], other[0:64, :], sind[0:64, thc], op=ALU.mult),
                        waits=[pr])
                    d3 = R.op("vector", lambda E: E.tensor_tensor(
                        krop[0:64, thc], evA[0:64, :], evB[0:64, :], op=ALU.add),
                        inc="s_dve")
                    krope_tick[0] = d3
                    return [d3, pr], None

                kva_pe = gemm(5, NKC, kva_stripe, xbf_of, ps_g, epi_kva,
                              rhs_wait_of=lambda k: xt[k // 8],
                              mwid_of=lambda m: 128 if m < 4 else 64, stripes=str3)
                if th == 0:
                    PH["xt1"] = load_xbf(1, [kva_pe[4]])
                if DEBUG_SUB == "kva":
                    R.flush()
                    return

                def recip_chain(ps_row, nfeat, out_row, w0):
                    R.op("scalar", lambda E, ps_row=ps_row, nfeat=nfeat: E.activation(
                        ROW["t0"], ps_row, AF.Ln, scale=1.0 / nfeat),
                        waits=[w0], inc="s_act")
                    R.op("scalar", lambda E: E.activation(
                        ROW["t1"], ROW["t0"], AF.Exp, scale=0.5), inc="s_act")
                    R.op("scalar", lambda E: E.activation(
                        ROW["t0"], ROW["t1"], AF.Ln, bias=EPS_RMS), inc="s_act")
                    return R.op("scalar", lambda E, o=out_row: E.activation(
                        o, ROW["t0"], AF.Exp, scale=-1.0), inc="s_act")

                aq = recip_chain(ps_qs, QR, ROW["r0"], ("s_pe", R.cnt["s_pe"]))
                ak = recip_chain(ps_ks, KVR, ROW["r1"], aq)
                for (row, psb, out_bc, aw) in ((ROW["r0"], ps_g[0], rqbc, aq),
                                               (ROW["r1"], ps_g[1], rkbc, ak)):
                    lastb = None
                    for nt in range(2):
                        lastb = R.op("tensor", lambda E, psb=psb, row=row, nt=nt: E.matmul(
                            psb[:, ts2(nt)], ones1f, row[:, ts2(nt)],
                            start=True, stop=True),
                            waits=[aw] + R.fence(("s_dve", "s_act")),
                            inc=("s_pe" if nt == 1 else None))
                    R.op("vector", lambda E, o=out_bc, psb=psb: E.tensor_copy(o, psb),
                         waits=[lastb], inc="s_dve")

                trow = R.dma("gpsimd",
                             BOUNCE[th * TH:(th + 1) * TH].rearrange("(o t) -> o t", o=1),
                             ROW["r1"], waits=[ak], sem="s_smg")
                rk_tick = None
                for tc in range(8):
                    rk_tick = R.dma("gpsimd", rkcols[:, th * 8 + tc:th * 8 + tc + 1],
                                    BOUNCE[th * TH + tc * 128: th * TH + (tc + 1) * 128]
                                    .rearrange("(p o) -> p o", o=1),
                                    waits=[trow], sem="s_smg")
                phase1.rk_tick = rk_tick

                dsq = R.op("vector", lambda E: E.tensor_tensor(
                    sqA[0:64, :], krop[0:64, thc], krop[0:64, thc], op=ALU.mult),
                    waits=[krope_tick[0]], inc="s_dve")
                lastm = None
                for nt in range(2):
                    lastm = R.op("tensor", lambda E, nt=nt: E.matmul(
                        ps_ks[:, ts2(nt)], onesCb[0:64, :], sqA[0:64, ts2(nt)],
                        start=True, stop=True),
                        waits=[dsq, ak], inc=("s_pe" if nt == 1 else None))
                R.op("vector", lambda E: E.tensor_reduce(
                    ROW["t0"][:, 0:1], ps_ks, axis=AX.X, op=ALU.max), waits=[lastm])
                R.op("vector", lambda E: E.tensor_tensor(
                    kr2max, kr2max, ROW["t0"][:, 0:1], op=ALU.max), inc="s_dve")
                R.flush()
            if DEBUG_SUB == "1a":
                return

            with (nc.psum_tensor(f"ps_g0b{th}", [128, TH], F32) as hg0,
                  nc.psum_tensor(f"ps_g1b{th}", [128, TH], F32) as hg1,
                  nc.psum_tensor(f"ps_R{th}", [128, TH], F32) as hR,
                  nc.psum_tensor(f"ps_n2{th}", [32, TH], F32) as hn2):
                ps_g = (hg0.ap(), hg1.ap())
                ps_R, ps_n2 = hR.ap(), hn2.ap()

                stg_tick = {0: None, 1: None}
                rope_state = {"prev_d2": None, "prev_pr": None}
                n2s = {"first": True, "final": None}

                def epi_qb(m, ps, pe_w):
                    par = m % 2
                    stg = (stgA, stgB)[par]
                    sq = (sqA, sqB)[par]
                    if m < 32:
                        d1 = R.op("vector", lambda E, ps=ps, stg=stg: E.tensor_tensor(
                            stg, ps, rqbc, op=ALU.mult),
                            waits=[pe_w, stg_tick[par]], inc="s_dve")
                        stg_tick[par] = R.dma("sync", QN[m * 128:(m + 1) * 128, thc],
                                              stg, waits=[d1], sem=("s_wA", "s_wB")[par])
                        d2 = R.op("vector", lambda E, stg=stg, sq=sq: E.tensor_tensor(
                            sq, stg, stg, op=ALU.mult), inc="s_dve")

                        def tail(m=m, sq=sq, d2=d2):
                            st = n2s["first"]
                            n2s["first"] = False
                            for nt in range(2):
                                t = R.op("tensor", lambda E, sq=sq, nt=nt, st=st, m=m: E.matmul(
                                    ps_n2[:, ts2(nt)], selN[:, m * 32:(m + 1) * 32],
                                    sq[:, ts2(nt)], start=st, stop=(m == 47)),
                                    waits=[d2],
                                    inc=("s_pe" if (m == 47 and nt == 1) else None))
                                if m == 47 and nt == 1:
                                    n2s["final"] = t
                        return [d1, d2], tail

                    j = m - 32
                    qrb = qrbuf[:, par * TH:(par + 1) * TH]
                    d0 = R.op("vector", lambda E, ps=ps, qrb=qrb: E.tensor_tensor(
                        qrb, ps, rqbc, op=ALU.mult),
                        waits=[pe_w, rope_state["prev_pr"]], inc="s_dve")
                    dsq = R.op("vector", lambda E, qrb=qrb, sq=sq: E.tensor_tensor(
                        sq, qrb, qrb, op=ALU.mult), inc="s_dve")

                    def tail(m=m, j=j, par=par, qrb=qrb, stg=stg, sq=sq, d0=d0, dsq=dsq):
                        pr = None
                        for nt in range(2):
                            pr = R.op("tensor", lambda E, qrb=qrb, nt=nt: E.matmul(
                                ps_R[:, ts2(nt)], rotqb, qrb[:, ts2(nt)],
                                start=True, stop=True),
                                waits=[d0, rope_state["prev_d2"]],
                                inc=("s_pe" if nt == 1 else None))
                        rope_state["prev_pr"] = pr
                        st = n2s["first"]
                        n2s["first"] = False
                        for nt in range(2):
                            t = R.op("tensor", lambda E, sq=sq, nt=nt, st=st, j=j, m=m: E.matmul(
                                ps_n2[:, ts2(nt)], selR[:, j * 32:(j + 1) * 32],
                                sq[:, ts2(nt)], start=st, stop=(m == 47)),
                                waits=[dsq],
                                inc=("s_pe" if (m == 47 and nt == 1) else None))
                            if m == 47 and nt == 1:
                                n2s["final"] = t
                        dd1 = R.op("vector", lambda E, qrb=qrb: E.tensor_tensor(
                            evA, qrb, cosd[:, thc], op=ALU.mult), waits=[dsq])
                        dd2 = R.op("vector", lambda E: E.tensor_tensor(
                            evB, ps_R, sind[:, thc], op=ALU.mult), waits=[pr])
                        rope_state["prev_d2"] = dd2
                        d3 = R.op("vector", lambda E, stg=stg: E.tensor_tensor(
                            stg, evA, evB, op=ALU.add),
                            waits=[stg_tick[par]], inc="s_dve")
                        stg_tick[par] = R.dma("sync", QRS[j * 64 * 2:(j + 1) * 64 * 2, thc],
                                              stg, waits=[d3], sem=("s_wA", "s_wB")[par])
                    return [d0], tail

                gemm(48, 12, mk_stripe(w_qb, 12),
                     lambda k, nt: qag[:, k * TH + nt * 512: k * TH + nt * 512 + 512],
                     ps_g, epi_qb, stripes=str3)

                R.op("vector", lambda E: E.tensor_reduce(
                    n2tmp, ps_n2, axis=AX.X, op=ALU.max), waits=[n2s["final"]])
                R.op("vector", lambda E: E.tensor_tensor(
                    qn2max, qn2max, n2tmp, op=ALU.max), inc="s_dve")
                R.flush()
            if DEBUG_SUB == "1b":
                return

            with (nc.psum_tensor(f"ps_g0c{th}", [128, TH], F32) as hg0,
                  nc.psum_tensor(f"ps_g1c{th}", [128, TH], F32) as hg1,
                  nc.psum_tensor(f"ps_n2c{th}", [32, TH], F32) as hn2):
                ps_g = (hg0.ap(), hg1.ap())
                ps_n2 = hn2.ap()
                stg_tick = {0: None, 1: None}
                n2s = {"first": True, "final": None}

                def epi_kvb(m, ps, pe_w):
                    par = m % 2
                    stg = (stgA, stgB)[par]
                    sq = (sqA, sqB)[par]
                    d1 = R.op("vector", lambda E, ps=ps, stg=stg: E.tensor_tensor(
                        stg, ps, rkbc, op=ALU.mult),
                        waits=[pe_w, stg_tick[par]], inc="s_dve")
                    stg_tick[par] = R.dma("sync", KN[m * 128:(m + 1) * 128, thc],
                                          stg, waits=[d1], sem=("s_wA", "s_wB")[par])
                    d2 = R.op("vector", lambda E, stg=stg, sq=sq: E.tensor_tensor(
                        sq, stg, stg, op=ALU.mult), inc="s_dve")

                    def tail(m=m, sq=sq, d2=d2):
                        st = n2s["first"]
                        n2s["first"] = False
                        for nt in range(2):
                            t = R.op("tensor", lambda E, sq=sq, nt=nt, st=st, m=m: E.matmul(
                                ps_n2[:, ts2(nt)], selN[:, m * 32:(m + 1) * 32],
                                sq[:, ts2(nt)], start=st, stop=(m == 31)),
                                waits=[d2],
                                inc=("s_pe" if (m == 31 and nt == 1) else None))
                            if m == 31 and nt == 1:
                                n2s["final"] = t
                    return [d1, d2], tail

                gemm(32, 4, mk_stripe(w_kvb, 4),
                     lambda k, nt: kcg[:, k * TH + nt * 512: k * TH + nt * 512 + 512],
                     ps_g, epi_kvb, stripes=str3)

                R.op("vector", lambda E: E.tensor_reduce(
                    n2tmp, ps_n2, axis=AX.X, op=ALU.max), waits=[n2s["final"]])
                R.op("vector", lambda E: E.tensor_tensor(
                    kn2max, kn2max, n2tmp, op=ALU.max), inc="s_dve")

                with nc.sbuf_tensor(f"wvb{th}", [128, 2 * 4 * 512], BF16) as hwv:
                    wvb = hwv.ap()
                    wv_tick = [None, None]
                    pe_done = [None, None]
                    dve_done = {0: None, 1: None}

                    def load_wv(vq, extra=()):
                        par = vq % 2
                        wv_tick[par] = R.dma(
                            "gpsimd",
                            wvb[:, par * 2048:(par + 1) * 2048]
                            .rearrange("p (k m) -> p k m", k=4),
                            w_kvb[:, 4096 + vq * 512: 4096 + (vq + 1) * 512]
                            .rearrange("(k p) m -> p k m", p=128),
                            waits=list(extra), sem=("s_strA", "s_strB")[par])

                    init_v = [("s_pe", R.cnt["s_pe"])]
                    load_wv(0, extra=init_v)
                    load_wv(1, extra=init_v)
                    for vq in range(8):
                        par = vq % 2
                        for tc in range(8):
                            pp = ps_g[tc % 2]
                            last = None
                            for kv in range(4):
                                w = []
                                if kv == 0:
                                    w = [wv_tick[par], dve_done[tc % 2]]
                                    if vq == 0 and tc < 2:
                                        w += R.fence(("s_dve", "s_act"))
                                last = R.op("tensor", lambda E, pp=pp, kv=kv, tc=tc,
                                            par=par: E.matmul(
                                                pp[:, 0:512],
                                                kcg[:, kv * TH + tc * 128: kv * TH + (tc + 1) * 128],
                                                wvb[:, par * 2048 + kv * 512: par * 2048 + (kv + 1) * 512],
                                                start=(kv == 0), stop=(kv == 3)),
                                            waits=w, inc=("s_pe" if kv == 3 else None))
                            pe_done[par] = last
                            stg = (stgA, stgB)[tc % 2]
                            dv = R.op("vector", lambda E, pp=pp, stg=stg, tc=tc: E.tensor_scalar(
                                stg[:, 0:512], pp[:, 0:512],
                                rkcols[:, th * 8 + tc:th * 8 + tc + 1], None, op0=ALU.mult),
                                waits=[last, stg_tick[tc % 2], phase1.rk_tick], inc="s_dve")
                            dve_done[tc % 2] = dv
                            stg_tick[tc % 2] = R.dma(
                                "sync",
                                VSC[(th * 8 + tc) * 128:(th * 8 + tc + 1) * 128,
                                    vq * 512:(vq + 1) * 512],
                                stg[:, 0:512], waits=[dv], sem=("s_wA", "s_wB")[tc % 2])
                        if vq + 2 < 8:
                            load_wv(vq + 2, extra=[pe_done[par]])
                R.flush()

    def phase2():
        with ExitStack() as pst:
            def psb(name, shape, dt):
                return pst.enter_context(nc.sbuf_tensor(name, shape, dt)).ap()
            qnb = psb("qnb", [128, 2 * S], BF16)
            qrb2 = psb("qrb2", [128, 2 * S], BF16)
            knb = psb("knb", [128, 2 * S], BF16)
            vsb = psb("vsb", [128, 2 * 16 * 128], BF16)
            ptb = psb("ptb", [128, 2 * TH], BF16)
            recl = psb("recl", [128, TH], F32)
            lnl = psb("lnl", [128, TH], F32)
            aostg = psb("aostg", [128, 2 * TH], BF16)
            qfb = psb("qfb", [128, TH], F32)
            R.op("vector", lambda E: E.memset(qrb2, 0.0))
            sbcb = psb("sbcb", [128, TH], F32)
            zbcb = psb("zbcb", [128, TH], F32)
            dqt = psb("dqt", [128, TH], F32)
            wdst = psb("wdst", [128, 2 * TH], BF16)

            with nc.psum_tensor("ps_m", [128, 32], F32) as hm:
                ps_m = hm.ap()
                t1 = R.dma("gpsimd", BOUNCE[4096:4128].rearrange("(p o) -> p o", o=1),
                           qn2max, waits=R.fence(("s_dve", "s_pe")), sem="s_smg")
                t2 = R.dma("gpsimd", BOUNCE[4128:4160].rearrange("(p o) -> p o", o=1),
                           kn2max, sem="s_smg")
                t3 = R.dma("gpsimd", mrA, BOUNCE[4096:4128].rearrange("(o t) -> o t", o=1),
                           waits=[t2], sem="s_smg")
                t4 = R.dma("gpsimd", mrB, BOUNCE[4128:4160].rearrange("(o t) -> o t", o=1),
                           waits=[t2], sem="s_smg")
                d1 = R.op("vector", lambda E: E.tensor_scalar(
                    mrC, mrB, kr2max[0:1, 0:1], None, op0=ALU.add),
                    waits=[t3, t4], inc="s_dve")
                R.op("scalar", lambda E: E.activation(mrB, mrC, AF.Sqrt),
                     waits=[d1], inc="s_act")
                a2 = R.op("scalar", lambda E: E.activation(mrC, mrA, AF.Sqrt),
                          inc="s_act")
                R.op("vector", lambda E: E.tensor_tensor(
                    mrA, mrB, mrC, op=ALU.mult), waits=[a2])
                R.op("vector", lambda E: E.tensor_scalar(
                    mrB, mrA, SCALE, 0.1, op0=ALU.mult, op1=ALU.add))
                d4 = R.op("vector", lambda E: E.tensor_scalar(
                    mrC, mrB, -1.0, None, op0=ALU.mult), inc="s_dve")
                pm = R.op("tensor", lambda E: E.matmul(
                    ps_m, ones1f, mrC, start=True, stop=True),
                    waits=[d4], inc="s_pe")
                nm_tick = R.op("vector", lambda E: E.tensor_copy(nmhat, ps_m),
                               waits=[pm], inc="s_dve")
                R.flush()

            with (nc.psum_tensor("ps_s0", [128, TH], F32) as hs0,
                  nc.psum_tensor("ps_s1", [128, TH], F32) as hs1,
                  nc.psum_tensor("ps_o", [128, TH], F32) as ho,
                  nc.psum_tensor("ps_l", [128, TH], F32) as hl):
                ps_s = (hs0.ap(), hs1.ap())
                ps_o, ps_l = ho.ap(), hl.ap()

                ld_tick = [None, None]
                head_pe_end = [None, None]
                ao_tick = {0: None, 1: None}
                ps_s_free = {0: nm_tick, 1: nm_tick}
                prev = {"ao_dve": None, "ln": None}
                dq = {"s": None, "z": None, "sub": None,
                      "mul": {0: None, 1: None}, "out": {0: None, 1: None}}

                def load_head(h):
                    par = h % 2
                    sem = ("s_h0", "s_h1")[par]
                    w = [head_pe_end[par]]
                    if h < 2:
                        w = w + R.fence(("s_wA", "s_wB"))
                    R.dma("sync", qnb[:, par * S:(par + 1) * S],
                          QN[h * 128:(h + 1) * 128, :], waits=w, sem=sem)
                    R.dma("sync", qrb2[0:64, par * S:(par + 1) * S],
                          QRS[h * 64:(h + 1) * 64, :], waits=w, sem=sem)
                    R.dma("sync", knb[:, par * S:(par + 1) * S],
                          KN[h * 128:(h + 1) * 128, :], waits=w, sem=sem)
                    ld_tick[par] = R.dma(
                        "sync",
                        vsb[:, par * 2048:(par + 1) * 2048]
                        .rearrange("p (c v) -> p c v", c=16),
                        VSC[:, h * 128:(h + 1) * 128]
                        .rearrange("(c p) v -> p c v", p=128), waits=w, sem=sem)

                def dequant_stripe(c):
                    gi = c // 2
                    for qv in range(4):
                        tq = R.dma("gpsimd", qfb,
                                   qwT[c * 128:(c + 1) * 128, qv * TH:(qv + 1) * TH],
                                   waits=[dq["sub"]], sem="s_dq")
                        dq["s"] = R.dma("gpsimd", sbcb,
                                        sbc[gi:gi + 1, qv * TH:(qv + 1) * TH]
                                        .partition_broadcast(128),
                                        waits=[dq["mul"][0], dq["mul"][1]], sem="s_dq")
                        dq["z"] = R.dma("gpsimd", zbcb,
                                        zbc[gi:gi + 1, qv * TH:(qv + 1) * TH]
                                        .partition_broadcast(128),
                                        waits=[dq["sub"]], sem="s_dq")
                        dsub = R.op("vector", lambda E: E.tensor_tensor(
                            dqt, qfb, zbcb, op=ALU.subtract),
                            waits=[tq, dq["z"], dq["s"]], inc="s_dve")
                        dq["sub"] = dsub
                        par = qv % 2
                        dmul = R.op("vector", lambda E, par=par: E.tensor_tensor(
                            wdst[:, par * TH:(par + 1) * TH], dqt, sbcb, op=ALU.mult),
                            waits=[dq["out"][par]], inc="s_dve")
                        dq["mul"][par] = dmul
                        dq["out"][par] = R.dma(
                            "gpsimd", WD[c * 128:(c + 1) * 128, qv * TH:(qv + 1) * TH],
                            wdst[:, par * TH:(par + 1) * TH], waits=[dmul], sem="s_wd")

                def emit_pv(h, qh, c, vs, exp_tick):
                    cp = c % 2
                    last = None
                    for nt in range(2):
                        w = []
                        if nt == 0:
                            w = [exp_tick[c]]
                            if c == 0:
                                w += [prev["ao_dve"], prev["ln"]]
                        R.op("tensor", lambda E, c=c, nt=nt, cp=cp, vs=vs: E.matmul(
                            ps_o[:, ts2(nt)], vs[:, c * 128:(c + 1) * 128],
                            ptb[:, cp * TH + nt * 512: cp * TH + nt * 512 + 512],
                            start=(c == 0), stop=(c == 15)), waits=w)
                        last = R.op("tensor", lambda E, c=c, nt=nt, cp=cp: E.matmul(
                            ps_l[:, ts2(nt)], ones128b,
                            ptb[:, cp * TH + nt * 512: cp * TH + nt * 512 + 512],
                            start=(c == 0), stop=(c == 15)),
                            inc=("s_pe" if (c == 15 and nt == 1) else None))
                    return last

                load_head(0)
                load_head(1)
                for h in range(NH):
                    par = h % 2
                    dequant_stripe(h)
                    qn = qnb[:, par * S:(par + 1) * S]
                    qr = qrb2[:, par * S:(par + 1) * S]
                    kn = knb[:, par * S:(par + 1) * S]
                    vs = vsb[:, par * 2048:(par + 1) * 2048]
                    last_pe = None
                    for qh in range(2):
                        exp_tick = [None] * 16
                        pvl_last = None
                        for c in range(16):
                            cp = c % 2
                            sc_last = None
                            for nt in range(2):
                                w = []
                                if nt == 0:
                                    w = [ld_tick[par], ps_s_free[cp]]
                                colsl = slice(qh * TH + nt * 512, qh * TH + nt * 512 + 512)
                                R.op("tensor", lambda E, cp=cp, c=c, kn=kn, qn=qn,
                                     colsl=colsl, nt=nt: E.matmul(
                                         ps_s[cp][:, ts2(nt)],
                                         kn[:, c * 128:(c + 1) * 128],
                                         qn[:, colsl], start=True, stop=False),
                                     waits=w)
                                sc_last = R.op("tensor", lambda E, cp=cp, c=c, qr=qr,
                                               colsl=colsl, nt=nt: E.matmul(
                                                   ps_s[cp][:, ts2(nt)],
                                                   krop[:, c * 128:(c + 1) * 128],
                                                   qr[:, colsl], start=False, stop=True),
                                               inc=("s_pe" if nt == 1 else None))
                            if c >= 1:
                                pvl_last = emit_pv(h, qh, c - 1, vs, exp_tick)
                            exp_tick[c] = R.op("scalar", lambda E, cp=cp, h=h: E.activation(
                                ptb[:, cp * TH:(cp + 1) * TH], ps_s[cp], AF.Exp,
                                bias=nmhat[:, h:h + 1], scale=SCALE),
                                waits=[sc_last], inc="s_act")
                            ps_s_free[cp] = exp_tick[c]
                        pvl_last = emit_pv(h, qh, 15, vs, exp_tick)
                        a1 = R.op("scalar", lambda E: E.activation(
                            lnl, ps_l, AF.Ln), waits=[pvl_last], inc="s_act")
                        a2 = R.op("scalar", lambda E: E.activation(
                            recl, lnl, AF.Exp, scale=-1.0), inc="s_act")
                        dv = R.op("vector", lambda E, qh=qh: E.tensor_tensor(
                            aostg[:, qh * TH:(qh + 1) * TH], ps_o, recl, op=ALU.mult),
                            waits=[a2, ao_tick[qh]], inc="s_dve")
                        ao_tick[qh] = R.dma(
                            "sync", ASC[h * 128:(h + 1) * 128, qh * TH:(qh + 1) * TH],
                            aostg[:, qh * TH:(qh + 1) * TH], waits=[dv],
                            sem=("s_wA", "s_wB")[qh])
                        prev["ao_dve"] = dv
                        prev["ln"] = a1
                        last_pe = pvl_last
                    head_pe_end[par] = last_pe
                    if h + 2 < NH:
                        load_head(h + 2)
                R.flush()

    def phase3():
        with ExitStack() as pst:
            def psb(name, shape, dt):
                return pst.enter_context(nc.sbuf_tensor(name, shape, dt)).ap()
            hg0 = pst.enter_context(nc.psum_tensor("ps_g0d", [128, TH], F32))
            hg1 = pst.enter_context(nc.psum_tensor("ps_g1d", [128, TH], F32))
            hsx = pst.enter_context(nc.psum_tensor("ps_sx", [1, TH], F32))
            hsq = pst.enter_context(nc.psum_tensor("ps_sq", [1, TH], F32))
            ps_g = (hg0.ap(), hg1.ap())
            ps_sx, ps_sq = hsx.ap(), hsq.ap()
            strC3 = psb("strC3", [128, 32 * 128], BF16)
            str3 = (strA, strB, strC3)
            cx4 = psb("cx4", [128, 2 * 4 * TH], F32)
            ox4 = psb("ox4", [128, 2 * 4 * TH], BF16)
            ouf = psb("ouf", [128, 2 * 2 * TH], F32)
            oev = evC
            mu1_bc = psb("mu1bc", [128, TH], F32)
            rs1_bc = psb("rs1bc", [128, TH], F32)
            mu2_bc = psb("mu2bc", [128, TH], F32)
            rs2_bc = psb("rs2bc", [128, TH], F32)

            xf_tick = [None, None]
            xf_free = [None, None]
            res_tick = [None, None]
            stg_free = [None, None]

            def ln_rows_and_bcast(final_tick, mu_bc, rstd_bc):
                R.op("vector", lambda E: E.tensor_scalar(
                    ROW["r0"], ps_sx, 1.0 / H, None, op0=ALU.mult), waits=[final_tick])
                R.op("vector", lambda E: E.tensor_scalar(
                    ROW["t1"], ps_sq, 1.0 / H, None, op0=ALU.mult))
                R.op("vector", lambda E: E.tensor_tensor(
                    ROW["t0"], ROW["r0"], ROW["r0"], op=ALU.mult))
                dv = R.op("vector", lambda E: E.tensor_tensor(
                    ROW["t1"], ROW["t1"], ROW["t0"], op=ALU.subtract), inc="s_dve")
                R.op("scalar", lambda E: E.activation(
                    ROW["t0"], ROW["t1"], AF.Ln, bias=EPS_LN), waits=[dv], inc="s_act")
                ar = R.op("scalar", lambda E: E.activation(
                    ROW["r1"], ROW["t0"], AF.Exp, scale=-0.5), inc="s_act")
                out = {}
                for (row, psb_, bc, aw) in ((ROW["r0"], ps_g[0], mu_bc, dv),
                                            (ROW["r1"], ps_g[1], rstd_bc, ar)):
                    lastb = None
                    for nt in range(2):
                        lastb = R.op("tensor", lambda E, psb_=psb_, row=row, nt=nt: E.matmul(
                            psb_[:, ts2(nt)], ones1f, row[:, ts2(nt)],
                            start=True, stop=True),
                            waits=[aw] + R.fence(("s_dve", "s_act")),
                            inc=("s_pe" if nt == 1 else None))
                    out[id(bc)] = R.op("vector", lambda E, bc=bc, psb_=psb_: E.tensor_copy(bc, psb_),
                                       waits=[lastb], inc="s_dve")
                return out[id(mu_bc)], out[id(rstd_bc)]

            def wo_fetch_x(m, thc):
                par = m % 2
                w = [xf_free[par]]
                xf_tick[par] = R.dma("sync", (xfA, xfB)[par],
                                     xT[m * 128:(m + 1) * 128, thc], waits=w,
                                     sem=("s_xa", "s_xb")[par])

            def run_wo(th, thc, at, prefetched=False):
                st = {"first": True, "final": None,
                      "tail_tick": [("s_pe", R.cnt["s_pe"])] * 2}

                def fetch_x(m):
                    wo_fetch_x(m, thc)

                if not prefetched:
                    fetch_x(0)
                    fetch_x(1)

                def epi(m, ps, pe_w):
                    par = m % 2
                    rr = (evA, evB)[par]
                    xf = (xfA, xfB)[par]
                    d1 = R.op("vector", lambda E, ps=ps, rr=rr, xf=xf: E.tensor_tensor(
                        rr, ps, xf, op=ALU.add),
                        waits=[pe_w, xf_tick[par], res_tick[par]], inc="s_dve")
                    xf_free[par] = d1
                    if m + 2 < 32:
                        fetch_x(m + 2)
                    res_tick[par] = R.dma("gpsimd", R1[m * 128:(m + 1) * 128, thc],
                                          rr, waits=[d1], sem=("s_resA", "s_resB")[par])
                    stg = (stgA, stgB)[par]
                    a1 = R.op("scalar", lambda E, rr=rr, stg=stg: E.activation(
                        stg, rr, AF.Copy),
                        waits=[d1, st["tail_tick"][par]], inc="s_act")
                    sq = (sqA, sqB)[par]
                    d2 = R.op("vector", lambda E, stg=stg, sq=sq: E.tensor_tensor(
                        sq, stg, stg, op=ALU.mult), waits=[a1], inc="s_dve")

                    def tail(m=m, stg=stg, sq=sq, a1=a1, d2=d2, par=par):
                        stt = st["first"]
                        st["first"] = False
                        w_extra = R.fence(("s_dve",)) if stt else []
                        for nt in range(2):
                            R.op("tensor", lambda E, stg=stg, nt=nt, stt=stt, m=m: E.matmul(
                                ps_sx[:, ts2(nt)], onesCb, stg[:, ts2(nt)],
                                start=stt, stop=(m == 31)),
                                waits=[a1] + w_extra)
                            t = R.op("tensor", lambda E, sq=sq, nt=nt, stt=stt, m=m: E.matmul(
                                ps_sq[:, ts2(nt)], onesCb, sq[:, ts2(nt)],
                                start=stt, stop=(m == 31)), waits=[d2],
                                inc="s_pe")
                            if nt == 1:
                                st["tail_tick"][par] = t
                                if m == 31:
                                    st["final"] = t
                    return [d1], tail

                gemm(32, 32, mk_stripe(w_o, 32),
                     lambda k, nt: ABF[:, k * TH + nt * 512: k * TH + nt * 512 + 512],
                     ps_g, epi, rhs_wait_of=lambda k: at[k // 8], stripes=str3)
                return st["final"]

            def corr_pass(th, thc, mu_tick):
                ticks = [None] * 32
                dv = [None, None]
                r1f = R.fence(("s_resA", "s_resB", "s_pe"))
                for b in range(8):
                    par = b % 2
                    cx = cx4[:, par * 4 * TH:(par + 1) * 4 * TH]
                    w = [dv[par]] + (list(r1f) if b < 2 else [])
                    ld = R.dma("sync", cx.rearrange("p (c t) -> p c t", c=4),
                               R1[b * 512:(b + 1) * 512, thc]
                               .rearrange("(c p) t -> p c t", p=128),
                               waits=w, sem=("s_ca", "s_cb")[par])
                    for j in range(4):
                        k = 4 * b + j
                        dv[par] = R.op("vector", lambda E, cx=cx, j=j, k=k: E.tensor_tensor(
                            ABF[:, k * TH:(k + 1) * TH],
                            cx[:, j * TH:(j + 1) * TH], mu1_bc, op=ALU.subtract),
                            waits=[ld, mu_tick], inc="s_dve")
                        ticks[k] = dv[par]
                return ticks

            def run_ffn(th, thc, corr):
                st = {"first": True, "final": None,
                      "tail_tick": [("s_pe", R.cnt["s_pe"])] * 2}

                def epi(m, ps, pe_w):
                    par = m % 2
                    rr = (evA, evB)[par]
                    t1 = R.op("vector", lambda E, ps=ps, m=m: E.tensor_tensor(
                        evC, ps, ABF[:, m * TH:(m + 1) * TH], op=ALU.add),
                        waits=[pe_w], inc="s_dve")
                    R.op("vector", lambda E: E.tensor_tensor(
                        evC, evC, rs1_bc, op=ALU.mult))
                    d1 = R.op("vector", lambda E, rr=rr, m=m: E.tensor_scalar(
                        rr, evC, bias_c[:, m:m + 1], None, op0=ALU.add), inc="s_dve")
                    rb = (stgA, stgB)[par]
                    a1 = R.op("scalar", lambda E, rr=rr, rb=rb: E.activation(
                        rb, rr, AF.Copy),
                        waits=[d1, st["tail_tick"][par], res_tick[par]], inc="s_act")
                    res_tick[par] = R.dma("gpsimd", R2[m * 128:(m + 1) * 128, thc],
                                          rb, waits=[a1], sem=("s_resA", "s_resB")[par])
                    sq = (sqA, sqB)[par]
                    d2 = R.op("vector", lambda E, rb=rb, sq=sq: E.tensor_tensor(
                        sq, rb, rb, op=ALU.mult), waits=[a1], inc="s_dve")

                    def tail(m=m, rb=rb, sq=sq, a1=a1, d2=d2, par=par):
                        stt = st["first"]
                        st["first"] = False
                        w_extra = R.fence(("s_dve",)) if stt else []
                        for nt in range(2):
                            R.op("tensor", lambda E, rb=rb, nt=nt, stt=stt, m=m: E.matmul(
                                ps_sx[:, ts2(nt)], onesCb, rb[:, ts2(nt)],
                                start=stt, stop=(m == 31)),
                                waits=[a1] + w_extra)
                            t = R.op("tensor", lambda E, sq=sq, nt=nt, stt=stt, m=m: E.matmul(
                                ps_sq[:, ts2(nt)], onesCb, sq[:, ts2(nt)],
                                start=stt, stop=(m == 31)), waits=[d2],
                                inc="s_pe")
                            if nt == 1:
                                st["tail_tick"][par] = t
                                if m == 31:
                                    st["final"] = t
                    return [t1], tail

                gemm(32, 32,
                     mk_stripe(WD, 32, static_waits=lambda: R.fence(("s_wd",))),
                     lambda k, nt: ABF[:, k * TH + nt * 512: k * TH + nt * 512 + 512],
                     ps_g, epi, rhs_wait_of=lambda k: corr[k], stripes=str3)
                return st["final"]

            def out_loop(th, thc):
                tickO = [None, None]
                wfree = [None, None]
                r2_fence = R.fence(("s_resA", "s_resB", "s_dve"))
                for b in range(8):
                    par = b % 2
                    ox = ox4[:, par * 4 * TH:(par + 1) * 4 * TH]
                    w0 = [tickO[par]] + (list(r2_fence) if b < 2 else [])
                    t = R.dma("sync", ox.rearrange("p (c t) -> p c t", c=4),
                              R2[b * 512:(b + 1) * 512, thc]
                              .rearrange("(c p) t -> p c t", p=128),
                              waits=w0, sem=("s_ra", "s_rb")[par])
                    tickO[par] = t
                    for half in range(2):
                        ouw = ouf[:, half * 2 * TH:(half + 1) * 2 * TH]
                        d2 = None
                        for j in range(2):
                            oxj = ox[:, (half * 2 + j) * TH:(half * 2 + j + 1) * TH]
                            R.op("vector", lambda E, oxj=oxj: E.tensor_tensor(
                                oev, oxj, mu2_bc, op=ALU.subtract),
                                waits=[t, wfree[half]])
                            d2 = R.op("vector", lambda E, oxj=oxj, ouw=ouw, j=j:
                                      E.tensor_tensor(
                                          ouw[:, j * TH:(j + 1) * TH], oev, rs2_bc,
                                          op=ALU.mult), inc="s_dve")
                        wfree[half] = R.dma(
                            "sync",
                            outT[b * 512 + half * 256:(b * 512 + half * 256) + 256, thc]
                            .rearrange("(c p) t -> p c t", p=128),
                            ouw.rearrange("p (c t) -> p c t", c=2),
                            waits=[d2], sem=("s_oA", "s_oB")[half])

            def load_at(thc):
                at = []
                for q in range(4):
                    at.append(R.dma(
                        "sync",
                        ABF[:, q * 8 * TH:(q + 1) * 8 * TH].rearrange("p (k t) -> p k t", k=8),
                        ASC[q * 1024:(q + 1) * 1024, thc].rearrange("(k p) t -> p k t", p=128),
                        waits=R.fence(("s_pe", "s_dve", "s_act", "s_wA", "s_wB"))
                        if q == 0 else (), sem="s_abfs"))
                return at

            thc0 = slice(0, TH)
            thc1 = slice(TH, 2 * TH)
            at = load_at(thc0)
            prefetched = False
            for th in range(2):
                thc = (thc0, thc1)[th]
                fin = run_wo(th, thc, at, prefetched=prefetched)
                mu_t, rs_t = ln_rows_and_bcast(fin, mu1_bc, rs1_bc)
                corr = corr_pass(th, thc, mu_t)
                fin2 = run_ffn(th, thc, corr)
                mu2_t, rs2_t = ln_rows_and_bcast(fin2, mu2_bc, rs2_bc)
                if th == 0:
                    at = load_at(thc1)
                    wo_fetch_x(0, thc1)
                    wo_fetch_x(1, thc1)
                    prefetched = True
                out_loop(th, thc)
            R.flush()

    def finale():
        allsems = tuple(R.sems.keys())
        for name in debug_taps:
            R.dma("gpsimd", dbg[name], scratch[name], waits=R.fence(allsems),
                  sem="s_smg")
        R.op("gpsimd", lambda E: E.nop(), waits=R.fence(allsems))
        R.flush()

    phases = [("p0", phase0),
              ("p1t0", lambda: phase1(0)), ("p1t1", lambda: phase1(1)),
              ("p2", phase2),
              ("p3", phase3)]
    for name, fn in phases:
        fn()
        if name == "p1t1":
            trig_stack.close()
        if name == "p2":
            p12_stack.close()
        if last_phase is not None and name == last_phase:
            break
    finale()
    return nc


def _host_prep(inputs):
    x = np.asarray(inputs["L0_attn_hidden_states"])
    w_qa = np.ascontiguousarray(np.asarray(inputs["L0_attn_w_qa"]))
    g_qa = np.ascontiguousarray(np.asarray(inputs["L0_attn_g_qa"]))
    w_qb = np.asarray(inputs["L0_attn_w_qb"])
    w_kva = np.ascontiguousarray(np.asarray(inputs["L0_attn_w_kva"]))
    g_kva = np.ascontiguousarray(np.asarray(inputs["L0_attn_g_kva"]))
    w_kvb = np.asarray(inputs["L0_attn_w_kvb"])
    w_o = np.ascontiguousarray(np.asarray(inputs["L0_attn_w_o"]))
    qw = np.asarray(inputs["L0_ffn_qweight"])
    scales = np.asarray(inputs["L0_ffn_scales"]).reshape(H, 16)
    zeros = np.asarray(inputs["L0_ffn_zeros"]).reshape(H, 16)
    bias = np.ascontiguousarray(np.asarray(inputs["L0_ffn_bias"]))

    perm_q = np.empty(NH * QK, np.int64)
    for h in range(NH):
        perm_q[h * ND:(h + 1) * ND] = h * QK + np.arange(ND)
        perm_q[NH * ND + h * RD: NH * ND + (h + 1) * RD] = h * QK + ND + np.arange(RD)
    w_qb_r = np.ascontiguousarray(w_qb[:, perm_q])

    perm_k = np.empty(NH * (ND + VD), np.int64)
    for h in range(NH):
        perm_k[h * ND:(h + 1) * ND] = h * (ND + VD) + np.arange(ND)
        perm_k[NH * ND + h * VD: NH * ND + (h + 1) * VD] = h * (ND + VD) + ND + np.arange(VD)
    w_kvb_r = np.ascontiguousarray(w_kvb[:, perm_k])

    assert int(qw.min()) >= 0 and int(qw.max()) <= 255
    qwT = np.ascontiguousarray(qw.T).astype(np.uint8)
    sbcv = np.ascontiguousarray(scales.T).astype(np.float32)
    zbcv = np.ascontiguousarray(zeros.T).astype(np.float32)

    R64 = np.zeros((64, 64), np.float32)
    for j in range(32):
        R64[j, 32 + j] = -1.0
        R64[32 + j, j] = 1.0
    rotk = np.ascontiguousarray(R64.T)
    R128 = np.zeros((128, 128), np.float32)
    R128[0:64, 0:64] = R64
    R128[64:128, 64:128] = R64
    rotq = np.ascontiguousarray(R128.T)

    shared = dict(w_qa=w_qa, g_qa=g_qa, w_qb=w_qb_r, w_kva=w_kva, g_kva=g_kva,
                  w_kvb=w_kvb_r, w_o=w_o, qwT=qwT, sbc=sbcv, zbc=zbcv, bias=bias,
                  rotq=rotq, rotk=rotk)
    in_maps = []
    for b in range(x.shape[0]):
        m = dict(shared)
        m["xT"] = np.ascontiguousarray(x[b].T)
        in_maps.append(m)
    return in_maps


def kernel(**inputs):
    in_maps = _host_prep(inputs)
    nc = build_nc(tuple(DEBUG_TAPS), DEBUG_LAST_PHASE)
    res = run_bass_kernel_spmd(nc, in_maps, core_ids=list(range(len(in_maps))))
    out = np.stack([np.asarray(r["outT"]).T for r in res.results], axis=0)
    return out.astype(np.float32)


# version 08
# speedup vs baseline: 1.2301x; 1.2301x over previous
import math
from contextlib import ExitStack

import numpy as np

import concourse.bass as bass
import concourse.mybir as mybir
from concourse.bass_utils import run_bass_kernel_spmd

F32 = mybir.dt.float32
BF16 = mybir.dt.bfloat16
I32 = mybir.dt.int32
AF = mybir.ActivationFunctionType
ALU = mybir.AluOpType
AX = mybir.AxisListType

S = 2048
TH = 1024
H = 4096
NH = 32
QR = 1536
KVR = 512
RD = 64
ND = 128
VD = 128
QK = ND + RD
SCALE = QK ** -0.5
NKC = H // 128
EPS_RMS = 1e-6
EPS_LN = 1e-5
TWO_PI = 2.0 * math.pi

ENGINES = ("sync", "gpsimd", "tensor", "vector", "scalar")

DEBUG_TAPS = []
DEBUG_LAST_PHASE = None
DEBUG_SUB = None


def ts2(nt):
    return slice(nt * 512, (nt + 1) * 512)


class Rec:

    def __init__(self, nc, stack):
        self.nc = nc
        self.ops = {e: [] for e in ENGINES}
        self.sems = {}
        self.cnt = {}
        self.wait_cache = {e: {} for e in ENGINES}
        for s in ("s_sms", "s_smg", "s_abf", "s_abfs", "s_strA", "s_strB", "s_strC", "s_xa", "s_xb",
                  "s_h0", "s_h1", "s_wA", "s_wB", "s_resA", "s_resB",
                  "s_dq", "s_wd", "s_oA", "s_oB", "s_oC", "s_oD",
                  "s_ca", "s_cb", "s_cc", "s_cd", "s_ra", "s_rb", "s_rc", "s_rd",
                  "s_pe", "s_dve", "s_act", "s_gp"):
            self.sems[s] = stack.enter_context(nc.semaphore(s))
            self.cnt[s] = 0

    def op(self, eng, fn, waits=(), inc=None, inc_by=1):
        w = tuple((s, v) for t in waits if t is not None for (s, v) in (t,) if v > 0)
        self.ops[eng].append((w, fn, inc, inc_by))
        if inc is not None:
            self.cnt[inc] += inc_by
            return (inc, self.cnt[inc])
        return None

    def dma(self, eng, out, in_, waits=(), sem=None):
        assert sem is not None, "explicit dma semaphore family required"
        return self.op(eng, lambda E, o=out, i=in_: E.dma_start(out=o, in_=i),
                       waits, inc=sem, inc_by=16)

    def fence(self, sems=("s_pe", "s_dve", "s_act")):
        return [(s, self.cnt[s]) for s in sems]

    def flush(self):
        nc = self.nc
        todo = {e: self.ops[e] for e in ENGINES if self.ops[e]}
        if not todo:
            return
        with nc.Block() as blk:
            for engname, oplist in todo.items():
                def make_cb(oplist=oplist, engname=engname):
                    def cb(E):
                        cache = self.wait_cache[engname]
                        for waits, fn, inc, n in oplist:
                            for sname, val in waits:
                                if val > cache.get(sname, 0):
                                    E.wait_ge(self.sems[sname], val)
                                    cache[sname] = val
                            inst = fn(E)
                            if inc is not None:
                                inst.then_inc(self.sems[inc], n)
                    return cb
                getattr(blk, engname)(make_cb())
        self.ops = {e: [] for e in ENGINES}


def build_nc(debug_taps=(), last_phase=None):
    nc = bass.Bass("TRN2", target_bir_lowering=False, detect_race_conditions=False)

    xT = nc.dram_tensor("xT", [H, S], F32, kind="ExternalInput").ap()
    w_qa = nc.dram_tensor("w_qa", [H, QR], F32, kind="ExternalInput").ap()
    g_qa = nc.dram_tensor("g_qa", [QR], F32, kind="ExternalInput").ap()
    w_qb = nc.dram_tensor("w_qb", [QR, NH * QK], F32, kind="ExternalInput").ap()
    w_kva = nc.dram_tensor("w_kva", [H, KVR + RD], F32, kind="ExternalInput").ap()
    g_kva = nc.dram_tensor("g_kva", [KVR], F32, kind="ExternalInput").ap()
    w_kvb = nc.dram_tensor("w_kvb", [KVR, NH * (ND + VD)], F32, kind="ExternalInput").ap()
    w_o = nc.dram_tensor("w_o", [NH * VD, H], F32, kind="ExternalInput").ap()
    qwT = nc.dram_tensor("qwT", [H, H], mybir.dt.uint8, kind="ExternalInput").ap()
    sbc = nc.dram_tensor("sbc", [16, H], F32, kind="ExternalInput").ap()
    zbc = nc.dram_tensor("zbc", [16, H], F32, kind="ExternalInput").ap()
    bias_d = nc.dram_tensor("bias", [H], F32, kind="ExternalInput").ap()
    rotq_d = nc.dram_tensor("rotq", [128, 128], F32, kind="ExternalInput").ap()
    rotk_d = nc.dram_tensor("rotk", [64, 64], F32, kind="ExternalInput").ap()
    outT = nc.dram_tensor("outT", [H, S], F32, kind="ExternalOutput").ap()

    QN = nc.dram_tensor("QN_sc", [NH * ND, S], BF16).ap()
    QRS = nc.dram_tensor("QR_sc", [NH * RD, S], BF16).ap()
    KN = nc.dram_tensor("KN_sc", [NH * ND, S], BF16).ap()
    VSC = nc.dram_tensor("V_sc", [S, NH * VD], BF16).ap()
    ASC = nc.dram_tensor("A_sc", [NH * VD, S], BF16).ap()
    R1 = nc.dram_tensor("R1_sc", [H, S], F32).ap()
    WD = nc.dram_tensor("WD_sc", [H, H], BF16).ap()
    R2 = nc.dram_tensor("R2_sc", [H, S], BF16).ap()
    BOUNCE = nc.dram_tensor("BN_sc", [8192], F32).ap()
    scratch = {"QN_sc": QN, "QR_sc": QRS, "KN_sc": KN, "V_sc": VSC, "A_sc": ASC,
               "R1_sc": R1, "WD_sc": WD, "R2_sc": R2}

    dbg = {}
    for name in debug_taps:
        src = scratch[name]
        dbg[name] = nc.dram_tensor("dbg_" + name, list(src.shape), src.dtype,
                                   kind="ExternalOutput").ap()

    stack = ExitStack()
    R = Rec(nc, stack)

    def sb(name, shape, dt):
        return stack.enter_context(nc.sbuf_tensor(name, shape, dt)).ap()

    ABF = sb("ABF", [128, 32 * TH], BF16)
    strA = sb("strA", [128, 32 * 128], BF16)
    strB = sb("strB", [128, 32 * 128], BF16)
    evA = sb("evA", [128, TH], F32)
    evB = sb("evB", [128, TH], F32)
    evC = sb("evC", [128, TH], F32)
    sqA = sb("sqA", [128, TH], BF16)
    sqB = sb("sqB", [128, TH], BF16)
    stgA = sb("stgA", [128, TH], BF16)
    stgB = sb("stgB", [128, TH], BF16)
    xfA = sb("xfA", [128, TH], F32)
    xfB = sb("xfB", [128, TH], F32)
    rowsA = sb("rowsA", [1, 2 * TH], F32)
    ones1f = sb("ones1f", [1, 128], F32)
    onesCb = sb("onesCb", [128, 1], BF16)
    ones128b = sb("ones128b", [128, 128], BF16)
    gqa_c = sb("gqa_c", [128, 12], F32)
    gkva_c = sb("gkva_c", [128, 4], F32)
    bias_c = sb("bias_c", [128, 32], F32)
    rotqb = sb("rotqb", [128, 128], BF16)
    rotkb = sb("rotkb", [64, 64], BF16)
    rkcols = sb("rkcols", [128, 16], F32)
    nmhat = sb("nmhat", [128, 32], F32)
    mrA = sb("mrA", [1, 32], F32)
    mrB = sb("mrB", [1, 32], F32)
    mrC = sb("mrC", [1, 32], F32)
    constT = sb("constT", [128, 8], F32)
    _consts = {}

    def creg(val):
        if val not in _consts:
            idx = len(_consts)
            ap = constT[:, idx:idx + 1]
            _consts[val] = ap
            nc.const_aps.register(F32, val, ap)
        return _consts[val]

    for v in (math.pi / 2.0, EPS_RMS, EPS_LN):
        creg(v)

    p12_stack = ExitStack()
    def sb12(name, shape, dt):
        return p12_stack.enter_context(nc.sbuf_tensor(name, shape, dt)).ap()
    krop = sb12("krop", [128, S], BF16)
    selN = sb12("selN", [128, 32 * 32], BF16)
    selR = sb12("selR", [128, 16 * 32], BF16)
    qn2max = sb12("qn2max", [32, 1], F32)
    kn2max = sb12("kn2max", [32, 1], F32)
    kr2max = sb12("kr2max", [1, 1], F32)
    n2tmp = sb12("n2tmp", [32, 1], F32)

    trig_stack = ExitStack()
    cosd = trig_stack.enter_context(nc.sbuf_tensor("cosd", [128, S], F32)).ap()
    sind = trig_stack.enter_context(nc.sbuf_tensor("sind", [128, S], F32)).ap()

    ROW = {"r0": rowsA[:, 0:TH], "r1": rowsA[:, TH:2 * TH],
           "t0": xfA[0:1, :], "t1": xfB[0:1, :]}

    def gemm(n_m, n_k, stripe_dma, rhs_of, ps_pair, epilogue,
             rhs_wait_of=None, mwid_of=None, extra_first_waits=(), stripes=None):
        if stripes is None:
            stripes = (strA, strB)
        nstr = len(stripes)
        stripe_tick = [None] * nstr
        ep_free = [(), ()]
        pe_tick = [None] * n_m
        pe_tails = [None] * n_m
        init_w = [("s_pe", R.cnt["s_pe"])]
        for m in range(min(nstr, n_m)):
            stripe_tick[m % nstr] = stripe_dma(m, stripes[m % nstr], m % nstr,
                                               extra_waits=init_w)
        for m in range(n_m):
            par = m % 2
            spar = m % nstr
            ps = ps_pair[par]
            buf = stripes[spar]
            mwid = 128 if mwid_of is None else mwid_of(m)
            if m >= 1 and pe_tails[m - 1] is not None:
                pe_tails[m - 1]()
            first = True
            last_t = None
            for k in range(n_k):
                for nt in range(2):
                    w = []
                    if first:
                        w = [stripe_tick[spar]] + list(ep_free[par]) + list(extra_first_waits)
                        first = False
                    if nt == 0 and rhs_wait_of is not None:
                        w.append(rhs_wait_of(k))
                    last_t = R.op("tensor", lambda E, ps=ps, buf=buf, k=k, nt=nt,
                                  mwid=mwid, n_k=n_k: E.matmul(
                                      ps[0:mwid, ts2(nt)],
                                      buf[:, k * mwid:(k + 1) * mwid],
                                      rhs_of(k, nt),
                                      start=(k == 0), stop=(k == n_k - 1)),
                                  waits=w,
                                  inc=("s_pe" if (k == n_k - 1 and nt == 1) else None))
            pe_tick[m] = last_t
            if m + nstr < n_m:
                stripe_tick[spar] = stripe_dma(m + nstr, stripes[spar], spar,
                                               extra_waits=[pe_tick[m]])
            free_waits, tail = epilogue(m, ps, pe_tick[m])
            ep_free[par] = free_waits
            pe_tails[m] = tail
        if pe_tails[n_m - 1] is not None:
            pe_tails[n_m - 1]()
        return pe_tick

    STR_SEMS = ("s_strA", "s_strB", "s_strC")

    def mk_stripe(wap, n_k, eng="gpsimd", mwid=128, kpar=128, static_waits=None):
        def f(m, buf, slot, extra_waits=()):
            w = list(extra_waits) + (static_waits() if static_waits else [])
            return R.dma(eng,
                         buf[:, 0:n_k * mwid].rearrange("p (k m) -> p k m", k=n_k),
                         wap[:, m * mwid:(m + 1) * mwid]
                         .rearrange("(k p) m -> p k m", p=kpar),
                         waits=w, sem=STR_SEMS[slot])
        return f

    def phase0():
        with (nc.sbuf_tensor("trig_u", [64, S], F32) as h1,
              nc.sbuf_tensor("trig_i", [64, S], I32) as h2,
              nc.sbuf_tensor("trig_y", [64, S], F32) as h3,
              nc.sbuf_tensor("trig_t", [64, S], F32) as h3b,
              nc.sbuf_tensor("ifrow", [1, 64], F32) as h4,
              nc.sbuf_tensor("itrow", [1, S], F32) as h5,
              nc.psum_tensor("ps_ang", [64, S], F32) as h6):
            trig_u, trig_i, trig_y = h1.ap(), h2.ap(), h3.ap()
            trig_t = h3b.ap()
            ifrow, itrow, ps_ang = h4.ap(), h5.ap(), h6.ap()

            for val, ap in _consts.items():
                R.op("vector", lambda E, ap=ap, val=val: E.memset(ap, val))
            R.op("vector", lambda E: E.memset(ones1f, 1.0))
            R.op("vector", lambda E: E.memset(onesCb, 1.0))
            R.op("vector", lambda E: E.memset(ones128b, 1.0))
            R.op("vector", lambda E: E.memset(qn2max, 0.0))
            R.op("vector", lambda E: E.memset(kn2max, 0.0))
            R.op("vector", lambda E: E.memset(kr2max, 0.0))
            R.op("vector", lambda E: E.memset(krop, 0.0))
            R.op("vector", lambda E: E.memset(selN, 0.0))
            R.op("vector", lambda E: E.memset(selR, 0.0))
            for h in range(32):
                R.op("vector", lambda E, h=h: E.memset(
                    selN[:, h * 32 + h:h * 32 + h + 1], 1.0))
            for j in range(16):
                R.op("vector", lambda E, j=j: E.memset(
                    selR[0:64, j * 32 + 2 * j:j * 32 + 2 * j + 1], 1.0))
                R.op("vector", lambda E, j=j: E.memset(
                    selR[64:128, j * 32 + 2 * j + 1:j * 32 + 2 * j + 2], 1.0))

            for c in range(12):
                R.dma("sync", gqa_c[:, c:c + 1],
                      g_qa[c * 128:(c + 1) * 128].rearrange("(p o) -> p o", o=1), sem="s_sms")
            for c in range(4):
                R.dma("sync", gkva_c[:, c:c + 1],
                      g_kva[c * 128:(c + 1) * 128].rearrange("(p o) -> p o", o=1), sem="s_sms")
            for c in range(32):
                R.dma("sync", bias_c[:, c:c + 1],
                      bias_d[c * 128:(c + 1) * 128].rearrange("(p o) -> p o", o=1), sem="s_sms")
            R.dma("gpsimd", rotqb, rotq_d, sem="s_smg")
            R.dma("gpsimd", rotkb, rotk_d, sem="s_smg")

            ti1 = R.op("gpsimd", lambda E: E.iota(
                trig_y[0:1, 0:32], [[1, 32]], base=0, channel_multiplier=0,
                allow_small_or_imprecise_dtypes=True), inc="s_gp")
            ti2 = R.op("gpsimd", lambda E: E.iota(
                itrow, [[1, S]], base=0, channel_multiplier=0,
                allow_small_or_imprecise_dtypes=True), inc="s_gp")
            a1 = R.op("scalar", lambda E: E.activation(
                ifrow[:, 0:32], trig_y[0:1, 0:32], AF.Exp,
                scale=-math.log(10000.0) / 32.0), waits=[ti1], inc="s_act")
            d0 = R.op("vector", lambda E: E.tensor_copy(ifrow[:, 32:64], ifrow[:, 0:32]),
                      waits=[a1], inc="s_dve")
            p_ang = None
            for n in range(4):
                p_ang = R.op("tensor", lambda E, n=n: E.matmul(
                    ps_ang[:, ts2(n)], ifrow[0:1, :], itrow[0:1, ts2(n)],
                    start=True, stop=True),
                    waits=[d0, ti2], inc=("s_pe" if n == 3 else None))
            def trig_table(out_ap, off, shift, w0):
                R.op("vector", lambda E: E.tensor_scalar(
                    trig_u, ps_ang, 1.0 / TWO_PI, off, op0=ALU.mult,
                    op1=(ALU.add if off else ALU.bypass)), waits=[w0])
                R.op("vector", lambda E: E.tensor_copy(trig_i, trig_u))
                R.op("vector", lambda E: E.tensor_copy(trig_u, trig_i))
                R.op("vector", lambda E: E.scalar_tensor_tensor(
                    trig_y, trig_u, TWO_PI, ps_ang, op0=ALU.mult, op1=ALU.subtract))
                if shift:
                    R.op("vector", lambda E: E.tensor_scalar(
                        trig_y, trig_y, shift, None, op0=ALU.subtract))
                R.op("vector", lambda E: E.tensor_scalar(
                    trig_u, trig_y, -math.pi, None, op0=ALU.is_lt))
                R.op("vector", lambda E: E.tensor_scalar(
                    trig_t, trig_y, math.pi, None, op0=ALU.is_gt))
                R.op("vector", lambda E: E.tensor_tensor(
                    trig_u, trig_u, trig_t, op=ALU.subtract))
                dd = R.op("vector", lambda E: E.scalar_tensor_tensor(
                    trig_t, trig_u, TWO_PI, trig_y, op0=ALU.mult, op1=ALU.add),
                    inc="s_dve")
                return R.op("scalar", lambda E, out_ap=out_ap: E.activation(
                    out_ap, trig_t, AF.Sin, scale=-1.0),
                    waits=[dd], inc="s_act")

            a3 = trig_table(sind[0:64, :], None, 0.0, p_ang)
            a4 = trig_table(cosd[0:64, :], 0.25, math.pi / 2.0, a3)
            R.dma("gpsimd", sind[64:128, :], sind[0:64, :], waits=[a4], sem="s_smg")
            R.dma("gpsimd", cosd[64:128, :], cosd[0:64, :], waits=[a4], sem="s_smg")
            R.flush()

    PH = {}

    def load_xbf(th, w0):
        thc = slice(th * TH, (th + 1) * TH)
        xt = []
        for q in range(4):
            xt.append(R.dma(
                "gpsimd",
                ABF[:, q * 8 * TH:(q + 1) * 8 * TH].rearrange("p (k t) -> p k t", k=8),
                xT[q * 1024:(q + 1) * 1024, thc].rearrange("(k p) t -> p k t", p=128),
                waits=w0 if q == 0 else (), sem="s_abf"))
        return xt

    def phase1(th):
        thc = slice(th * TH, (th + 1) * TH)
        with ExitStack() as pst:
            qag = pst.enter_context(nc.sbuf_tensor(f"qag{th}", [128, 12 * TH], BF16)).ap()
            kcg = pst.enter_context(nc.sbuf_tensor(f"kcg{th}", [128, 4 * TH], BF16)).ap()
            rqbc = pst.enter_context(nc.sbuf_tensor(f"rqbc{th}", [128, TH], F32)).ap()
            rkbc = pst.enter_context(nc.sbuf_tensor(f"rkbc{th}", [128, TH], F32)).ap()
            krbf = pst.enter_context(nc.sbuf_tensor(f"krbf{th}", [64, TH], BF16)).ap()
            qrbuf = pst.enter_context(nc.sbuf_tensor(f"qrbuf{th}", [128, 2 * TH], BF16)).ap()
            strC = pst.enter_context(nc.sbuf_tensor(f"strC{th}", [128, 32 * 128], BF16)).ap()
            str3 = (strA, strB, strC)

            with (nc.psum_tensor(f"ps_g0_{th}", [128, TH], F32) as hg0,
                  nc.psum_tensor(f"ps_g1_{th}", [128, TH], F32) as hg1,
                  nc.psum_tensor(f"ps_qs{th}", [1, TH], F32) as hqs,
                  nc.psum_tensor(f"ps_ks{th}", [1, TH], F32) as hks):
                ps_g = (hg0.ap(), hg1.ap())
                ps_qs, ps_ks = hqs.ap(), hks.ap()

                if th == 0:
                    xt = load_xbf(0, R.fence())
                else:
                    xt = PH["xt1"]

                def xbf_of(k, nt):
                    return ABF[:, k * TH + nt * 512: k * TH + nt * 512 + 512]

                if DEBUG_SUB == "x":
                    R.flush()
                    return

                st_q = {"first": True}

                def epi_qa_mm(m, ps, pe_w):
                    return [pe_w], None

                def epi_qa_act(m, ps, pe_w):
                    sq = (sqA, sqB)[m % 2]
                    a = R.op("scalar", lambda E, ps=ps, sq=sq: E.activation(
                        sq, ps, AF.Square), waits=[pe_w], inc="s_act")
                    return [a], None

                def epi_qa_dve(m, ps, pe_w):
                    d = R.op("vector", lambda E, ps=ps, m=m: E.tensor_scalar(
                        qag[:, m * TH:(m + 1) * TH], ps, gqa_c[:, m:m + 1], None,
                        op0=ALU.mult), waits=[pe_w], inc="s_dve")
                    return [d], None

                def epi_qa(m, ps, pe_w):
                    sq = (sqA, sqB)[m % 2]
                    a = R.op("scalar", lambda E, ps=ps, sq=sq: E.activation(
                        sq, ps, AF.Square), waits=[pe_w], inc="s_act")
                    d = R.op("vector", lambda E, ps=ps, m=m: E.tensor_scalar(
                        qag[:, m * TH:(m + 1) * TH], ps, gqa_c[:, m:m + 1], None,
                        op0=ALU.mult), waits=[pe_w, a], inc="s_dve")

                    def tail(m=m, sq=sq, a=a):
                        st = st_q["first"]
                        st_q["first"] = False
                        for nt in range(2):
                            R.op("tensor", lambda E, sq=sq, nt=nt, st=st, m=m: E.matmul(
                                ps_qs[:, ts2(nt)], onesCb, sq[:, ts2(nt)],
                                start=st, stop=(m == 11)),
                                waits=[a],
                                inc=("s_pe" if (m == 11 and nt == 1) else None))
                    return [a, d], tail

                epi_sel = {"qamm": epi_qa_mm, "qaact": epi_qa_act,
                           "qadve": epi_qa_dve}.get(DEBUG_SUB, epi_qa)
                gemm(12, NKC, mk_stripe(w_qa, NKC), xbf_of, ps_g, epi_sel,
                     rhs_wait_of=lambda k: xt[k // 8],
                     extra_first_waits=R.fence(("s_sms", "s_smg")), stripes=str3)
                if DEBUG_SUB in ("qa", "qamm", "qaact", "qadve"):
                    R.flush()
                    return

                def kva_stripe(m, buf, slot, extra_waits=()):
                    sem = STR_SEMS[slot]
                    if m < 4:
                        return R.dma("gpsimd",
                                     buf[:, 0:NKC * 128].rearrange("p (k m) -> p k m", k=NKC),
                                     w_kva[:, m * 128:(m + 1) * 128]
                                     .rearrange("(k p) m -> p k m", p=128),
                                     waits=list(extra_waits), sem=sem)
                    return R.dma("gpsimd",
                                 buf[:, 0:NKC * 64].rearrange("p (k m) -> p k m", k=NKC),
                                 w_kva[:, 512:576].rearrange("(k p) m -> p k m", p=128),
                                 waits=list(extra_waits), sem=sem)

                st_k = {"first": True}
                krope_tick = [None]

                def epi_kva(m, ps, pe_w):
                    if m < 4:
                        sq = (sqA, sqB)[m % 2]
                        a = R.op("scalar", lambda E, ps=ps, sq=sq: E.activation(
                            sq, ps, AF.Square), waits=[pe_w], inc="s_act")
                        d = R.op("vector", lambda E, ps=ps, m=m: E.tensor_scalar(
                            kcg[:, m * TH:(m + 1) * TH], ps, gkva_c[:, m:m + 1], None,
                            op0=ALU.mult), waits=[pe_w, a], inc="s_dve")

                        def tail(m=m, sq=sq, a=a):
                            st = st_k["first"]
                            st_k["first"] = False
                            for nt in range(2):
                                R.op("tensor", lambda E, sq=sq, nt=nt, st=st, m=m: E.matmul(
                                    ps_ks[:, ts2(nt)], onesCb, sq[:, ts2(nt)],
                                    start=st, stop=(m == 3)),
                                    waits=[a],
                                    inc=("s_pe" if (m == 3 and nt == 1) else None))
                        return [a, d], tail

                    a = R.op("scalar", lambda E, ps=ps: E.activation(
                        krbf, ps[0:64, :], AF.Copy), waits=[pe_w], inc="s_act")
                    other = ps_g[(m + 1) % 2]
                    pr = None
                    for nt in range(2):
                        pr = R.op("tensor", lambda E, other=other, nt=nt: E.matmul(
                            other[0:64, ts2(nt)], rotkb, krbf[:, ts2(nt)],
                            start=True, stop=True),
                            waits=[a] + R.fence(("s_act", "s_dve")),
                            inc=("s_pe" if nt == 1 else None))
                    R.op("vector", lambda E: E.tensor_tensor(
                        evA[0:64, :], krbf, cosd[0:64, thc], op=ALU.mult), waits=[a])
                    R.op("vector", lambda E, other=other: E.tensor_tensor(
                        evB[0:64, :], other[0:64, :], sind[0:64, thc], op=ALU.mult),
                        waits=[pr])
                    d3 = R.op("vector", lambda E: E.tensor_tensor(
                        krop[0:64, thc], evA[0:64, :], evB[0:64, :], op=ALU.add),
                        inc="s_dve")
                    krope_tick[0] = d3
                    return [d3, pr], None

                kva_pe = gemm(5, NKC, kva_stripe, xbf_of, ps_g, epi_kva,
                              rhs_wait_of=lambda k: xt[k // 8],
                              mwid_of=lambda m: 128 if m < 4 else 64, stripes=str3)
                if th == 0:
                    PH["xt1"] = load_xbf(1, [kva_pe[4]])
                if DEBUG_SUB == "kva":
                    R.flush()
                    return

                def recip_chain(ps_row, nfeat, out_row, w0):
                    R.op("scalar", lambda E, ps_row=ps_row, nfeat=nfeat: E.activation(
                        ROW["t0"], ps_row, AF.Ln, scale=1.0 / nfeat),
                        waits=[w0], inc="s_act")
                    R.op("scalar", lambda E: E.activation(
                        ROW["t1"], ROW["t0"], AF.Exp, scale=0.5), inc="s_act")
                    R.op("scalar", lambda E: E.activation(
                        ROW["t0"], ROW["t1"], AF.Ln, bias=EPS_RMS), inc="s_act")
                    return R.op("scalar", lambda E, o=out_row: E.activation(
                        o, ROW["t0"], AF.Exp, scale=-1.0), inc="s_act")

                aq = recip_chain(ps_qs, QR, ROW["r0"], ("s_pe", R.cnt["s_pe"]))
                ak = recip_chain(ps_ks, KVR, ROW["r1"], aq)
                for (row, psb, out_bc, aw) in ((ROW["r0"], ps_g[0], rqbc, aq),
                                               (ROW["r1"], ps_g[1], rkbc, ak)):
                    lastb = None
                    for nt in range(2):
                        lastb = R.op("tensor", lambda E, psb=psb, row=row, nt=nt: E.matmul(
                            psb[:, ts2(nt)], ones1f, row[:, ts2(nt)],
                            start=True, stop=True),
                            waits=[aw] + R.fence(("s_dve", "s_act")),
                            inc=("s_pe" if nt == 1 else None))
                    R.op("vector", lambda E, o=out_bc, psb=psb: E.tensor_copy(o, psb),
                         waits=[lastb], inc="s_dve")

                trow = R.dma("gpsimd",
                             BOUNCE[th * TH:(th + 1) * TH].rearrange("(o t) -> o t", o=1),
                             ROW["r1"], waits=[ak], sem="s_smg")
                rk_tick = None
                for tc in range(8):
                    rk_tick = R.dma("gpsimd", rkcols[:, th * 8 + tc:th * 8 + tc + 1],
                                    BOUNCE[th * TH + tc * 128: th * TH + (tc + 1) * 128]
                                    .rearrange("(p o) -> p o", o=1),
                                    waits=[trow], sem="s_smg")
                phase1.rk_tick = rk_tick

                dsq = R.op("vector", lambda E: E.tensor_tensor(
                    sqA[0:64, :], krop[0:64, thc], krop[0:64, thc], op=ALU.mult),
                    waits=[krope_tick[0]], inc="s_dve")
                lastm = None
                for nt in range(2):
                    lastm = R.op("tensor", lambda E, nt=nt: E.matmul(
                        ps_ks[:, ts2(nt)], onesCb[0:64, :], sqA[0:64, ts2(nt)],
                        start=True, stop=True),
                        waits=[dsq, ak], inc=("s_pe" if nt == 1 else None))
                R.op("vector", lambda E: E.tensor_reduce(
                    ROW["t0"][:, 0:1], ps_ks, axis=AX.X, op=ALU.max), waits=[lastm])
                R.op("vector", lambda E: E.tensor_tensor(
                    kr2max, kr2max, ROW["t0"][:, 0:1], op=ALU.max), inc="s_dve")
                R.flush()
            if DEBUG_SUB == "1a":
                return

            with (nc.psum_tensor(f"ps_g0b{th}", [128, TH], F32) as hg0,
                  nc.psum_tensor(f"ps_g1b{th}", [128, TH], F32) as hg1,
                  nc.psum_tensor(f"ps_R{th}", [128, TH], F32) as hR,
                  nc.psum_tensor(f"ps_n2{th}", [32, TH], F32) as hn2):
                ps_g = (hg0.ap(), hg1.ap())
                ps_R, ps_n2 = hR.ap(), hn2.ap()

                stg_tick = {0: None, 1: None}
                rope_state = {"prev_d2": None, "prev_pr": None}
                n2s = {"first": True, "final": None}

                def epi_qb(m, ps, pe_w):
                    par = m % 2
                    stg = (stgA, stgB)[par]
                    sq = (sqA, sqB)[par]
                    if m < 32:
                        d1 = R.op("vector", lambda E, ps=ps, stg=stg: E.tensor_tensor(
                            stg, ps, rqbc, op=ALU.mult),
                            waits=[pe_w, stg_tick[par]], inc="s_dve")
                        stg_tick[par] = R.dma("sync", QN[m * 128:(m + 1) * 128, thc],
                                              stg, waits=[d1], sem=("s_wA", "s_wB")[par])
                        d2 = R.op("vector", lambda E, stg=stg, sq=sq: E.tensor_tensor(
                            sq, stg, stg, op=ALU.mult), inc="s_dve")

                        def tail(m=m, sq=sq, d2=d2):
                            st = n2s["first"]
                            n2s["first"] = False
                            for nt in range(2):
                                t = R.op("tensor", lambda E, sq=sq, nt=nt, st=st, m=m: E.matmul(
                                    ps_n2[:, ts2(nt)], selN[:, m * 32:(m + 1) * 32],
                                    sq[:, ts2(nt)], start=st, stop=(m == 47)),
                                    waits=[d2],
                                    inc=("s_pe" if (m == 47 and nt == 1) else None))
                                if m == 47 and nt == 1:
                                    n2s["final"] = t
                        return [d1, d2], tail

                    j = m - 32
                    qrb = qrbuf[:, par * TH:(par + 1) * TH]
                    d0 = R.op("vector", lambda E, ps=ps, qrb=qrb: E.tensor_tensor(
                        qrb, ps, rqbc, op=ALU.mult),
                        waits=[pe_w, rope_state["prev_pr"]], inc="s_dve")
                    dsq = R.op("vector", lambda E, qrb=qrb, sq=sq: E.tensor_tensor(
                        sq, qrb, qrb, op=ALU.mult), inc="s_dve")

                    def tail(m=m, j=j, par=par, qrb=qrb, stg=stg, sq=sq, d0=d0, dsq=dsq):
                        pr = None
                        for nt in range(2):
                            pr = R.op("tensor", lambda E, qrb=qrb, nt=nt: E.matmul(
                                ps_R[:, ts2(nt)], rotqb, qrb[:, ts2(nt)],
                                start=True, stop=True),
                                waits=[d0, rope_state["prev_d2"]],
                                inc=("s_pe" if nt == 1 else None))
                        rope_state["prev_pr"] = pr
                        st = n2s["first"]
                        n2s["first"] = False
                        for nt in range(2):
                            t = R.op("tensor", lambda E, sq=sq, nt=nt, st=st, j=j, m=m: E.matmul(
                                ps_n2[:, ts2(nt)], selR[:, j * 32:(j + 1) * 32],
                                sq[:, ts2(nt)], start=st, stop=(m == 47)),
                                waits=[dsq],
                                inc=("s_pe" if (m == 47 and nt == 1) else None))
                            if m == 47 and nt == 1:
                                n2s["final"] = t
                        dd1 = R.op("vector", lambda E, qrb=qrb: E.tensor_tensor(
                            evA, qrb, cosd[:, thc], op=ALU.mult), waits=[dsq])
                        dd2 = R.op("vector", lambda E: E.tensor_tensor(
                            evB, ps_R, sind[:, thc], op=ALU.mult), waits=[pr])
                        rope_state["prev_d2"] = dd2
                        d3 = R.op("vector", lambda E, stg=stg: E.tensor_tensor(
                            stg, evA, evB, op=ALU.add),
                            waits=[stg_tick[par]], inc="s_dve")
                        stg_tick[par] = R.dma("sync", QRS[j * 64 * 2:(j + 1) * 64 * 2, thc],
                                              stg, waits=[d3], sem=("s_wA", "s_wB")[par])
                    return [d0], tail

                gemm(48, 12, mk_stripe(w_qb, 12),
                     lambda k, nt: qag[:, k * TH + nt * 512: k * TH + nt * 512 + 512],
                     ps_g, epi_qb, stripes=str3)

                R.op("vector", lambda E: E.tensor_reduce(
                    n2tmp, ps_n2, axis=AX.X, op=ALU.max), waits=[n2s["final"]])
                R.op("vector", lambda E: E.tensor_tensor(
                    qn2max, qn2max, n2tmp, op=ALU.max), inc="s_dve")
                R.flush()
            if DEBUG_SUB == "1b":
                return

            with (nc.psum_tensor(f"ps_g0c{th}", [128, TH], F32) as hg0,
                  nc.psum_tensor(f"ps_g1c{th}", [128, TH], F32) as hg1,
                  nc.psum_tensor(f"ps_n2c{th}", [32, TH], F32) as hn2):
                ps_g = (hg0.ap(), hg1.ap())
                ps_n2 = hn2.ap()
                stg_tick = {0: None, 1: None}
                n2s = {"first": True, "final": None}

                def epi_kvb(m, ps, pe_w):
                    par = m % 2
                    stg = (stgA, stgB)[par]
                    sq = (sqA, sqB)[par]
                    d1 = R.op("vector", lambda E, ps=ps, stg=stg: E.tensor_tensor(
                        stg, ps, rkbc, op=ALU.mult),
                        waits=[pe_w, stg_tick[par]], inc="s_dve")
                    stg_tick[par] = R.dma("sync", KN[m * 128:(m + 1) * 128, thc],
                                          stg, waits=[d1], sem=("s_wA", "s_wB")[par])
                    d2 = R.op("vector", lambda E, stg=stg, sq=sq: E.tensor_tensor(
                        sq, stg, stg, op=ALU.mult), inc="s_dve")

                    def tail(m=m, sq=sq, d2=d2):
                        st = n2s["first"]
                        n2s["first"] = False
                        for nt in range(2):
                            t = R.op("tensor", lambda E, sq=sq, nt=nt, st=st, m=m: E.matmul(
                                ps_n2[:, ts2(nt)], selN[:, m * 32:(m + 1) * 32],
                                sq[:, ts2(nt)], start=st, stop=(m == 31)),
                                waits=[d2],
                                inc=("s_pe" if (m == 31 and nt == 1) else None))
                            if m == 31 and nt == 1:
                                n2s["final"] = t
                    return [d1, d2], tail

                gemm(32, 4, mk_stripe(w_kvb, 4),
                     lambda k, nt: kcg[:, k * TH + nt * 512: k * TH + nt * 512 + 512],
                     ps_g, epi_kvb, stripes=str3)

                R.op("vector", lambda E: E.tensor_reduce(
                    n2tmp, ps_n2, axis=AX.X, op=ALU.max), waits=[n2s["final"]])
                R.op("vector", lambda E: E.tensor_tensor(
                    kn2max, kn2max, n2tmp, op=ALU.max), inc="s_dve")

                with nc.sbuf_tensor(f"wvb{th}", [128, 2 * 4 * 512], BF16) as hwv:
                    wvb = hwv.ap()
                    wv_tick = [None, None]
                    pe_done = [None, None]
                    dve_done = {0: None, 1: None}

                    def load_wv(vq, extra=()):
                        par = vq % 2
                        wv_tick[par] = R.dma(
                            "gpsimd",
                            wvb[:, par * 2048:(par + 1) * 2048]
                            .rearrange("p (k m) -> p k m", k=4),
                            w_kvb[:, 4096 + vq * 512: 4096 + (vq + 1) * 512]
                            .rearrange("(k p) m -> p k m", p=128),
                            waits=list(extra), sem=("s_strA", "s_strB")[par])

                    init_v = [("s_pe", R.cnt["s_pe"])]
                    load_wv(0, extra=init_v)
                    load_wv(1, extra=init_v)
                    for vq in range(8):
                        par = vq % 2
                        for tc in range(8):
                            pp = ps_g[tc % 2]
                            last = None
                            for kv in range(4):
                                w = []
                                if kv == 0:
                                    w = [wv_tick[par], dve_done[tc % 2]]
                                    if vq == 0 and tc < 2:
                                        w += R.fence(("s_dve", "s_act"))
                                last = R.op("tensor", lambda E, pp=pp, kv=kv, tc=tc,
                                            par=par: E.matmul(
                                                pp[:, 0:512],
                                                kcg[:, kv * TH + tc * 128: kv * TH + (tc + 1) * 128],
                                                wvb[:, par * 2048 + kv * 512: par * 2048 + (kv + 1) * 512],
                                                start=(kv == 0), stop=(kv == 3)),
                                            waits=w, inc=("s_pe" if kv == 3 else None))
                            pe_done[par] = last
                            stg = (stgA, stgB)[tc % 2]
                            dv = R.op("vector", lambda E, pp=pp, stg=stg, tc=tc: E.tensor_scalar(
                                stg[:, 0:512], pp[:, 0:512],
                                rkcols[:, th * 8 + tc:th * 8 + tc + 1], None, op0=ALU.mult),
                                waits=[last, stg_tick[tc % 2], phase1.rk_tick], inc="s_dve")
                            dve_done[tc % 2] = dv
                            stg_tick[tc % 2] = R.dma(
                                "sync",
                                VSC[(th * 8 + tc) * 128:(th * 8 + tc + 1) * 128,
                                    vq * 512:(vq + 1) * 512],
                                stg[:, 0:512], waits=[dv], sem=("s_wA", "s_wB")[tc % 2])
                        if vq + 2 < 8:
                            load_wv(vq + 2, extra=[pe_done[par]])
                R.flush()

    def phase2():
        with ExitStack() as pst:
            def psb(name, shape, dt):
                return pst.enter_context(nc.sbuf_tensor(name, shape, dt)).ap()
            qnb = psb("qnb", [128, 2 * S], BF16)
            qrb2 = psb("qrb2", [128, 2 * S], BF16)
            knb = psb("knb", [128, 2 * S], BF16)
            vsb = psb("vsb", [128, 2 * 16 * 128], BF16)
            ptb = psb("ptb", [128, 2 * TH], BF16)
            recl = psb("recl", [128, TH], F32)
            lnl = psb("lnl", [128, TH], F32)
            aostg = psb("aostg", [128, 2 * TH], BF16)
            qfb = psb("qfb", [128, TH], F32)
            R.op("vector", lambda E: E.memset(qrb2, 0.0))
            sbcb = psb("sbcb", [128, TH], F32)
            zbcb = psb("zbcb", [128, TH], F32)
            dqt = psb("dqt", [128, TH], F32)
            wdst = psb("wdst", [128, 2 * TH], BF16)

            with nc.psum_tensor("ps_m", [128, 32], F32) as hm:
                ps_m = hm.ap()
                t1 = R.dma("gpsimd", BOUNCE[4096:4128].rearrange("(p o) -> p o", o=1),
                           qn2max, waits=R.fence(("s_dve", "s_pe")), sem="s_smg")
                t2 = R.dma("gpsimd", BOUNCE[4128:4160].rearrange("(p o) -> p o", o=1),
                           kn2max, sem="s_smg")
                t3 = R.dma("gpsimd", mrA, BOUNCE[4096:4128].rearrange("(o t) -> o t", o=1),
                           waits=[t2], sem="s_smg")
                t4 = R.dma("gpsimd", mrB, BOUNCE[4128:4160].rearrange("(o t) -> o t", o=1),
                           waits=[t2], sem="s_smg")
                d1 = R.op("vector", lambda E: E.tensor_scalar(
                    mrC, mrB, kr2max[0:1, 0:1], None, op0=ALU.add),
                    waits=[t3, t4], inc="s_dve")
                R.op("scalar", lambda E: E.activation(mrB, mrC, AF.Sqrt),
                     waits=[d1], inc="s_act")
                a2 = R.op("scalar", lambda E: E.activation(mrC, mrA, AF.Sqrt),
                          inc="s_act")
                R.op("vector", lambda E: E.tensor_tensor(
                    mrA, mrB, mrC, op=ALU.mult), waits=[a2])
                R.op("vector", lambda E: E.tensor_scalar(
                    mrB, mrA, SCALE, 0.1, op0=ALU.mult, op1=ALU.add))
                d4 = R.op("vector", lambda E: E.tensor_scalar(
                    mrC, mrB, -1.0, None, op0=ALU.mult), inc="s_dve")
                pm = R.op("tensor", lambda E: E.matmul(
                    ps_m, ones1f, mrC, start=True, stop=True),
                    waits=[d4], inc="s_pe")
                nm_tick = R.op("vector", lambda E: E.tensor_copy(nmhat, ps_m),
                               waits=[pm], inc="s_dve")
                R.flush()

            with (nc.psum_tensor("ps_s0", [128, TH], F32) as hs0,
                  nc.psum_tensor("ps_s1", [128, TH], F32) as hs1,
                  nc.psum_tensor("ps_o", [128, TH], F32) as ho,
                  nc.psum_tensor("ps_l", [128, TH], F32) as hl):
                ps_s = (hs0.ap(), hs1.ap())
                ps_o, ps_l = ho.ap(), hl.ap()

                ld_tick = [None, None]
                head_pe_end = [None, None]
                ao_tick = {0: None, 1: None}
                ps_s_free = {0: nm_tick, 1: nm_tick}
                prev = {"ao_dve": None, "ln": None}
                dq = {"s": None, "z": None, "sub": None,
                      "mul": {0: None, 1: None}, "out": {0: None, 1: None}}

                def load_head(h):
                    par = h % 2
                    sem = ("s_h0", "s_h1")[par]
                    w = [head_pe_end[par]]
                    if h < 2:
                        w = w + R.fence(("s_wA", "s_wB"))
                    R.dma("sync", qnb[:, par * S:(par + 1) * S],
                          QN[h * 128:(h + 1) * 128, :], waits=w, sem=sem)
                    R.dma("sync", qrb2[0:64, par * S:(par + 1) * S],
                          QRS[h * 64:(h + 1) * 64, :], waits=w, sem=sem)
                    R.dma("sync", knb[:, par * S:(par + 1) * S],
                          KN[h * 128:(h + 1) * 128, :], waits=w, sem=sem)
                    ld_tick[par] = R.dma(
                        "sync",
                        vsb[:, par * 2048:(par + 1) * 2048]
                        .rearrange("p (c v) -> p c v", c=16),
                        VSC[:, h * 128:(h + 1) * 128]
                        .rearrange("(c p) v -> p c v", p=128), waits=w, sem=sem)

                def dequant_stripe(c):
                    gi = c // 2
                    for qv in range(4):
                        tq = R.dma("gpsimd", qfb,
                                   qwT[c * 128:(c + 1) * 128, qv * TH:(qv + 1) * TH],
                                   waits=[dq["sub"]], sem="s_dq")
                        dq["s"] = R.dma("gpsimd", sbcb,
                                        sbc[gi:gi + 1, qv * TH:(qv + 1) * TH]
                                        .partition_broadcast(128),
                                        waits=[dq["mul"][0], dq["mul"][1]], sem="s_dq")
                        dq["z"] = R.dma("gpsimd", zbcb,
                                        zbc[gi:gi + 1, qv * TH:(qv + 1) * TH]
                                        .partition_broadcast(128),
                                        waits=[dq["sub"]], sem="s_dq")
                        dsub = R.op("gpsimd", lambda E: E.tensor_tensor(
                            dqt, qfb, zbcb, op=ALU.subtract),
                            waits=[tq, dq["z"], dq["s"]], inc="s_gp")
                        dq["sub"] = dsub
                        par = qv % 2
                        dmul = R.op("gpsimd", lambda E, par=par: E.tensor_tensor(
                            wdst[:, par * TH:(par + 1) * TH], dqt, sbcb, op=ALU.mult),
                            waits=[dq["out"][par]], inc="s_gp")
                        dq["mul"][par] = dmul
                        dq["out"][par] = R.dma(
                            "gpsimd", WD[c * 128:(c + 1) * 128, qv * TH:(qv + 1) * TH],
                            wdst[:, par * TH:(par + 1) * TH], waits=[dmul], sem="s_wd")

                def emit_pv(h, qh, c, vs, exp_tick):
                    cp = c % 2
                    last = None
                    for nt in range(2):
                        w = []
                        if nt == 0:
                            w = [exp_tick[c]]
                            if c == 0:
                                w += [prev["ao_dve"], prev["ln"]]
                        R.op("tensor", lambda E, c=c, nt=nt, cp=cp, vs=vs: E.matmul(
                            ps_o[:, ts2(nt)], vs[:, c * 128:(c + 1) * 128],
                            ptb[:, cp * TH + nt * 512: cp * TH + nt * 512 + 512],
                            start=(c == 0), stop=(c == 15)), waits=w)
                        last = R.op("tensor", lambda E, c=c, nt=nt, cp=cp: E.matmul(
                            ps_l[:, ts2(nt)], ones128b,
                            ptb[:, cp * TH + nt * 512: cp * TH + nt * 512 + 512],
                            start=(c == 0), stop=(c == 15)),
                            inc=("s_pe" if (c == 15 and nt == 1) else None))
                    return last

                load_head(0)
                load_head(1)
                for h in range(NH):
                    par = h % 2
                    dequant_stripe(h)
                    qn = qnb[:, par * S:(par + 1) * S]
                    qr = qrb2[:, par * S:(par + 1) * S]
                    kn = knb[:, par * S:(par + 1) * S]
                    vs = vsb[:, par * 2048:(par + 1) * 2048]
                    last_pe = None
                    for qh in range(2):
                        exp_tick = [None] * 16
                        pvl_last = None
                        for c in range(16):
                            cp = c % 2
                            sc_last = None
                            for nt in range(2):
                                w = []
                                if nt == 0:
                                    w = [ld_tick[par], ps_s_free[cp]]
                                colsl = slice(qh * TH + nt * 512, qh * TH + nt * 512 + 512)
                                R.op("tensor", lambda E, cp=cp, c=c, kn=kn, qn=qn,
                                     colsl=colsl, nt=nt: E.matmul(
                                         ps_s[cp][:, ts2(nt)],
                                         kn[:, c * 128:(c + 1) * 128],
                                         qn[:, colsl], start=True, stop=False),
                                     waits=w)
                                sc_last = R.op("tensor", lambda E, cp=cp, c=c, qr=qr,
                                               colsl=colsl, nt=nt: E.matmul(
                                                   ps_s[cp][:, ts2(nt)],
                                                   krop[:, c * 128:(c + 1) * 128],
                                                   qr[:, colsl], start=False, stop=True),
                                               inc=("s_pe" if nt == 1 else None))
                            if c >= 1:
                                pvl_last = emit_pv(h, qh, c - 1, vs, exp_tick)
                            exp_tick[c] = R.op("scalar", lambda E, cp=cp, h=h: E.activation(
                                ptb[:, cp * TH:(cp + 1) * TH], ps_s[cp], AF.Exp,
                                bias=nmhat[:, h:h + 1], scale=SCALE),
                                waits=[sc_last], inc="s_act")
                            ps_s_free[cp] = exp_tick[c]
                        pvl_last = emit_pv(h, qh, 15, vs, exp_tick)
                        a1 = R.op("scalar", lambda E: E.activation(
                            lnl, ps_l, AF.Ln), waits=[pvl_last], inc="s_act")
                        a2 = R.op("scalar", lambda E: E.activation(
                            recl, lnl, AF.Exp, scale=-1.0), inc="s_act")
                        dv = R.op("vector", lambda E, qh=qh: E.tensor_tensor(
                            aostg[:, qh * TH:(qh + 1) * TH], ps_o, recl, op=ALU.mult),
                            waits=[a2, ao_tick[qh]], inc="s_dve")
                        ao_tick[qh] = R.dma(
                            "sync", ASC[h * 128:(h + 1) * 128, qh * TH:(qh + 1) * TH],
                            aostg[:, qh * TH:(qh + 1) * TH], waits=[dv],
                            sem=("s_wA", "s_wB")[qh])
                        prev["ao_dve"] = dv
                        prev["ln"] = a1
                        last_pe = pvl_last
                    head_pe_end[par] = last_pe
                    if h + 2 < NH:
                        load_head(h + 2)
                R.flush()

    def phase3():
        with ExitStack() as pst:
            def psb(name, shape, dt):
                return pst.enter_context(nc.sbuf_tensor(name, shape, dt)).ap()
            hg0 = pst.enter_context(nc.psum_tensor("ps_g0d", [128, TH], F32))
            hg1 = pst.enter_context(nc.psum_tensor("ps_g1d", [128, TH], F32))
            hsx = pst.enter_context(nc.psum_tensor("ps_sx", [1, TH], F32))
            hsq = pst.enter_context(nc.psum_tensor("ps_sq", [1, TH], F32))
            ps_g = (hg0.ap(), hg1.ap())
            ps_sx, ps_sq = hsx.ap(), hsq.ap()
            strC3 = psb("strC3", [128, 32 * 128], BF16)
            str3 = (strA, strB, strC3)
            cx4 = psb("cx4", [128, 2 * 4 * TH], F32)
            ox4 = psb("ox4", [128, 2 * 4 * TH], BF16)
            ouf = psb("ouf", [128, 2 * 2 * TH], F32)
            oev = evC
            mu1_bc = psb("mu1bc", [128, TH], F32)
            rs1_bc = psb("rs1bc", [128, TH], F32)
            mu2_bc = psb("mu2bc", [128, TH], F32)
            rs2_bc = psb("rs2bc", [128, TH], F32)

            xf_tick = [None, None]
            xf_free = [None, None]
            res_tick = [None, None]
            stg_free = [None, None]

            def ln_rows_and_bcast(final_tick, mu_bc, rstd_bc):
                R.op("vector", lambda E: E.tensor_scalar(
                    ROW["r0"], ps_sx, 1.0 / H, None, op0=ALU.mult), waits=[final_tick])
                R.op("vector", lambda E: E.tensor_scalar(
                    ROW["t1"], ps_sq, 1.0 / H, None, op0=ALU.mult))
                R.op("vector", lambda E: E.tensor_tensor(
                    ROW["t0"], ROW["r0"], ROW["r0"], op=ALU.mult))
                dv = R.op("vector", lambda E: E.tensor_tensor(
                    ROW["t1"], ROW["t1"], ROW["t0"], op=ALU.subtract), inc="s_dve")
                R.op("scalar", lambda E: E.activation(
                    ROW["t0"], ROW["t1"], AF.Ln, bias=EPS_LN), waits=[dv], inc="s_act")
                ar = R.op("scalar", lambda E: E.activation(
                    ROW["r1"], ROW["t0"], AF.Exp, scale=-0.5), inc="s_act")
                out = {}
                for (row, psb_, bc, aw) in ((ROW["r0"], ps_g[0], mu_bc, dv),
                                            (ROW["r1"], ps_g[1], rstd_bc, ar)):
                    lastb = None
                    for nt in range(2):
                        lastb = R.op("tensor", lambda E, psb_=psb_, row=row, nt=nt: E.matmul(
                            psb_[:, ts2(nt)], ones1f, row[:, ts2(nt)],
                            start=True, stop=True),
                            waits=[aw] + R.fence(("s_dve", "s_act")),
                            inc=("s_pe" if nt == 1 else None))
                    out[id(bc)] = R.op("vector", lambda E, bc=bc, psb_=psb_: E.tensor_copy(bc, psb_),
                                       waits=[lastb], inc="s_dve")
                return out[id(mu_bc)], out[id(rstd_bc)]

            def wo_fetch_x(m, thc):
                par = m % 2
                w = [xf_free[par]]
                xf_tick[par] = R.dma("sync", (xfA, xfB)[par],
                                     xT[m * 128:(m + 1) * 128, thc], waits=w,
                                     sem=("s_xa", "s_xb")[par])

            def run_wo(th, thc, at, prefetched=False):
                st = {"first": True, "final": None,
                      "tail_tick": [("s_pe", R.cnt["s_pe"])] * 2}

                def fetch_x(m):
                    wo_fetch_x(m, thc)

                if not prefetched:
                    fetch_x(0)
                    fetch_x(1)

                def epi(m, ps, pe_w):
                    par = m % 2
                    rr = (evA, evB)[par]
                    xf = (xfA, xfB)[par]
                    d1 = R.op("vector", lambda E, ps=ps, rr=rr, xf=xf: E.tensor_tensor(
                        rr, ps, xf, op=ALU.add),
                        waits=[pe_w, xf_tick[par], res_tick[par]], inc="s_dve")
                    xf_free[par] = d1
                    if m + 2 < 32:
                        fetch_x(m + 2)
                    res_tick[par] = R.dma("gpsimd", R1[m * 128:(m + 1) * 128, thc],
                                          rr, waits=[d1], sem=("s_resA", "s_resB")[par])
                    stg = (stgA, stgB)[par]
                    a1 = R.op("scalar", lambda E, rr=rr, stg=stg: E.activation(
                        stg, rr, AF.Copy),
                        waits=[d1, st["tail_tick"][par]], inc="s_act")
                    sq = (sqA, sqB)[par]
                    d2 = R.op("vector", lambda E, stg=stg, sq=sq: E.tensor_tensor(
                        sq, stg, stg, op=ALU.mult), waits=[a1], inc="s_dve")

                    def tail(m=m, stg=stg, sq=sq, a1=a1, d2=d2, par=par):
                        stt = st["first"]
                        st["first"] = False
                        w_extra = R.fence(("s_dve",)) if stt else []
                        for nt in range(2):
                            R.op("tensor", lambda E, stg=stg, nt=nt, stt=stt, m=m: E.matmul(
                                ps_sx[:, ts2(nt)], onesCb, stg[:, ts2(nt)],
                                start=stt, stop=(m == 31)),
                                waits=[a1] + w_extra)
                            t = R.op("tensor", lambda E, sq=sq, nt=nt, stt=stt, m=m: E.matmul(
                                ps_sq[:, ts2(nt)], onesCb, sq[:, ts2(nt)],
                                start=stt, stop=(m == 31)), waits=[d2],
                                inc="s_pe")
                            if nt == 1:
                                st["tail_tick"][par] = t
                                if m == 31:
                                    st["final"] = t
                    return [d1], tail

                gemm(32, 32, mk_stripe(w_o, 32),
                     lambda k, nt: ABF[:, k * TH + nt * 512: k * TH + nt * 512 + 512],
                     ps_g, epi, rhs_wait_of=lambda k: at[k // 8], stripes=str3)
                return st["final"]

            def corr_pass(th, thc, mu_tick):
                ticks = [None] * 32
                dv = [None, None]
                r1f = R.fence(("s_resA", "s_resB", "s_pe"))
                for b in range(8):
                    par = b % 2
                    cx = cx4[:, par * 4 * TH:(par + 1) * 4 * TH]
                    w = [dv[par]] + (list(r1f) if b < 2 else [])
                    ld = R.dma("sync", cx.rearrange("p (c t) -> p c t", c=4),
                               R1[b * 512:(b + 1) * 512, thc]
                               .rearrange("(c p) t -> p c t", p=128),
                               waits=w, sem=("s_ca", "s_cb")[par])
                    for j in range(4):
                        k = 4 * b + j
                        dv[par] = R.op("vector", lambda E, cx=cx, j=j, k=k: E.tensor_tensor(
                            ABF[:, k * TH:(k + 1) * TH],
                            cx[:, j * TH:(j + 1) * TH], mu1_bc, op=ALU.subtract),
                            waits=[ld, mu_tick], inc="s_dve")
                        ticks[k] = dv[par]
                return ticks

            def run_ffn(th, thc, corr):
                st = {"first": True, "final": None,
                      "tail_tick": [("s_pe", R.cnt["s_pe"])] * 2}

                def epi(m, ps, pe_w):
                    par = m % 2
                    rr = (evA, evB)[par]
                    t1 = R.op("vector", lambda E, ps=ps, m=m: E.tensor_tensor(
                        evC, ps, ABF[:, m * TH:(m + 1) * TH], op=ALU.add),
                        waits=[pe_w], inc="s_dve")
                    R.op("vector", lambda E: E.tensor_tensor(
                        evC, evC, rs1_bc, op=ALU.mult))
                    d1 = R.op("vector", lambda E, rr=rr, m=m: E.tensor_scalar(
                        rr, evC, bias_c[:, m:m + 1], None, op0=ALU.add), inc="s_dve")
                    rb = (stgA, stgB)[par]
                    a1 = R.op("scalar", lambda E, rr=rr, rb=rb: E.activation(
                        rb, rr, AF.Copy),
                        waits=[d1, st["tail_tick"][par], res_tick[par]], inc="s_act")
                    res_tick[par] = R.dma("gpsimd", R2[m * 128:(m + 1) * 128, thc],
                                          rb, waits=[a1], sem=("s_resA", "s_resB")[par])
                    sq = (sqA, sqB)[par]
                    d2 = R.op("vector", lambda E, rb=rb, sq=sq: E.tensor_tensor(
                        sq, rb, rb, op=ALU.mult), waits=[a1], inc="s_dve")

                    def tail(m=m, rb=rb, sq=sq, a1=a1, d2=d2, par=par):
                        stt = st["first"]
                        st["first"] = False
                        w_extra = R.fence(("s_dve",)) if stt else []
                        for nt in range(2):
                            R.op("tensor", lambda E, rb=rb, nt=nt, stt=stt, m=m: E.matmul(
                                ps_sx[:, ts2(nt)], onesCb, rb[:, ts2(nt)],
                                start=stt, stop=(m == 31)),
                                waits=[a1] + w_extra)
                            t = R.op("tensor", lambda E, sq=sq, nt=nt, stt=stt, m=m: E.matmul(
                                ps_sq[:, ts2(nt)], onesCb, sq[:, ts2(nt)],
                                start=stt, stop=(m == 31)), waits=[d2],
                                inc="s_pe")
                            if nt == 1:
                                st["tail_tick"][par] = t
                                if m == 31:
                                    st["final"] = t
                    return [t1], tail

                gemm(32, 32,
                     mk_stripe(WD, 32, static_waits=lambda: R.fence(("s_wd",))),
                     lambda k, nt: ABF[:, k * TH + nt * 512: k * TH + nt * 512 + 512],
                     ps_g, epi, rhs_wait_of=lambda k: corr[k], stripes=str3)
                return st["final"]

            def out_loop(th, thc):
                tickO = [None, None]
                wfree = [None, None]
                r2_fence = R.fence(("s_resA", "s_resB", "s_dve"))
                for b in range(8):
                    par = b % 2
                    ox = ox4[:, par * 4 * TH:(par + 1) * 4 * TH]
                    w0 = [tickO[par]] + (list(r2_fence) if b < 2 else [])
                    t = R.dma("sync", ox.rearrange("p (c t) -> p c t", c=4),
                              R2[b * 512:(b + 1) * 512, thc]
                              .rearrange("(c p) t -> p c t", p=128),
                              waits=w0, sem=("s_ra", "s_rb")[par])
                    tickO[par] = t
                    for half in range(2):
                        ouw = ouf[:, half * 2 * TH:(half + 1) * 2 * TH]
                        d2 = None
                        for j in range(2):
                            oxj = ox[:, (half * 2 + j) * TH:(half * 2 + j + 1) * TH]
                            R.op("vector", lambda E, oxj=oxj: E.tensor_tensor(
                                oev, oxj, mu2_bc, op=ALU.subtract),
                                waits=[t, wfree[half]])
                            d2 = R.op("vector", lambda E, oxj=oxj, ouw=ouw, j=j:
                                      E.tensor_tensor(
                                          ouw[:, j * TH:(j + 1) * TH], oev, rs2_bc,
                                          op=ALU.mult), inc="s_dve")
                        wfree[half] = R.dma(
                            "sync",
                            outT[b * 512 + half * 256:(b * 512 + half * 256) + 256, thc]
                            .rearrange("(c p) t -> p c t", p=128),
                            ouw.rearrange("p (c t) -> p c t", c=2),
                            waits=[d2], sem=("s_oA", "s_oB")[half])

            def load_at(thc):
                at = []
                for q in range(4):
                    at.append(R.dma(
                        "sync",
                        ABF[:, q * 8 * TH:(q + 1) * 8 * TH].rearrange("p (k t) -> p k t", k=8),
                        ASC[q * 1024:(q + 1) * 1024, thc].rearrange("(k p) t -> p k t", p=128),
                        waits=R.fence(("s_pe", "s_dve", "s_act", "s_wA", "s_wB"))
                        if q == 0 else (), sem="s_abfs"))
                return at

            thc0 = slice(0, TH)
            thc1 = slice(TH, 2 * TH)
            at = load_at(thc0)
            prefetched = False
            for th in range(2):
                thc = (thc0, thc1)[th]
                fin = run_wo(th, thc, at, prefetched=prefetched)
                mu_t, rs_t = ln_rows_and_bcast(fin, mu1_bc, rs1_bc)
                corr = corr_pass(th, thc, mu_t)
                fin2 = run_ffn(th, thc, corr)
                mu2_t, rs2_t = ln_rows_and_bcast(fin2, mu2_bc, rs2_bc)
                if th == 0:
                    at = load_at(thc1)
                    wo_fetch_x(0, thc1)
                    wo_fetch_x(1, thc1)
                    prefetched = True
                out_loop(th, thc)
            R.flush()

    def finale():
        allsems = tuple(R.sems.keys())
        for name in debug_taps:
            R.dma("gpsimd", dbg[name], scratch[name], waits=R.fence(allsems),
                  sem="s_smg")
        R.op("gpsimd", lambda E: E.nop(), waits=R.fence(allsems))
        R.flush()

    phases = [("p0", phase0),
              ("p1t0", lambda: phase1(0)), ("p1t1", lambda: phase1(1)),
              ("p2", phase2),
              ("p3", phase3)]
    for name, fn in phases:
        fn()
        if name == "p1t1":
            trig_stack.close()
        if name == "p2":
            p12_stack.close()
        if last_phase is not None and name == last_phase:
            break
    finale()
    return nc


def _host_prep(inputs):
    x = np.asarray(inputs["L0_attn_hidden_states"])
    w_qa = np.ascontiguousarray(np.asarray(inputs["L0_attn_w_qa"]))
    g_qa = np.ascontiguousarray(np.asarray(inputs["L0_attn_g_qa"]))
    w_qb = np.asarray(inputs["L0_attn_w_qb"])
    w_kva = np.ascontiguousarray(np.asarray(inputs["L0_attn_w_kva"]))
    g_kva = np.ascontiguousarray(np.asarray(inputs["L0_attn_g_kva"]))
    w_kvb = np.asarray(inputs["L0_attn_w_kvb"])
    w_o = np.ascontiguousarray(np.asarray(inputs["L0_attn_w_o"]))
    qw = np.asarray(inputs["L0_ffn_qweight"])
    scales = np.asarray(inputs["L0_ffn_scales"]).reshape(H, 16)
    zeros = np.asarray(inputs["L0_ffn_zeros"]).reshape(H, 16)
    bias = np.ascontiguousarray(np.asarray(inputs["L0_ffn_bias"]))

    perm_q = np.empty(NH * QK, np.int64)
    for h in range(NH):
        perm_q[h * ND:(h + 1) * ND] = h * QK + np.arange(ND)
        perm_q[NH * ND + h * RD: NH * ND + (h + 1) * RD] = h * QK + ND + np.arange(RD)
    w_qb_r = np.ascontiguousarray(w_qb[:, perm_q])

    perm_k = np.empty(NH * (ND + VD), np.int64)
    for h in range(NH):
        perm_k[h * ND:(h + 1) * ND] = h * (ND + VD) + np.arange(ND)
        perm_k[NH * ND + h * VD: NH * ND + (h + 1) * VD] = h * (ND + VD) + ND + np.arange(VD)
    w_kvb_r = np.ascontiguousarray(w_kvb[:, perm_k])

    assert int(qw.min()) >= 0 and int(qw.max()) <= 255
    qwT = np.ascontiguousarray(qw.T).astype(np.uint8)
    sbcv = np.ascontiguousarray(scales.T).astype(np.float32)
    zbcv = np.ascontiguousarray(zeros.T).astype(np.float32)

    R64 = np.zeros((64, 64), np.float32)
    for j in range(32):
        R64[j, 32 + j] = -1.0
        R64[32 + j, j] = 1.0
    rotk = np.ascontiguousarray(R64.T)
    R128 = np.zeros((128, 128), np.float32)
    R128[0:64, 0:64] = R64
    R128[64:128, 64:128] = R64
    rotq = np.ascontiguousarray(R128.T)

    shared = dict(w_qa=w_qa, g_qa=g_qa, w_qb=w_qb_r, w_kva=w_kva, g_kva=g_kva,
                  w_kvb=w_kvb_r, w_o=w_o, qwT=qwT, sbc=sbcv, zbc=zbcv, bias=bias,
                  rotq=rotq, rotk=rotk)
    in_maps = []
    for b in range(x.shape[0]):
        m = dict(shared)
        m["xT"] = np.ascontiguousarray(x[b].T)
        in_maps.append(m)
    return in_maps


def kernel(**inputs):
    in_maps = _host_prep(inputs)
    nc = build_nc(tuple(DEBUG_TAPS), DEBUG_LAST_PHASE)
    res = run_bass_kernel_spmd(nc, in_maps, core_ids=list(range(len(in_maps))))
    out = np.stack([np.asarray(r["outT"]).T for r in res.results], axis=0)
    return out.astype(np.float32)


# version 09
# speedup vs baseline: 1.3213x; 1.0741x over previous
import math
from contextlib import ExitStack

import numpy as np

import concourse.bass as bass
import concourse.mybir as mybir
from concourse.bass_utils import run_bass_kernel_spmd

F32 = mybir.dt.float32
BF16 = mybir.dt.bfloat16
I32 = mybir.dt.int32
AF = mybir.ActivationFunctionType
ALU = mybir.AluOpType
AX = mybir.AxisListType

S = 2048
TH = 1024
H = 4096
NH = 32
QR = 1536
KVR = 512
RD = 64
ND = 128
VD = 128
QK = ND + RD
SCALE = QK ** -0.5
NKC = H // 128
EPS_RMS = 1e-6
EPS_LN = 1e-5
TWO_PI = 2.0 * math.pi

ENGINES = ("sync", "gpsimd", "tensor", "vector", "scalar")

DEBUG_TAPS = []
DEBUG_LAST_PHASE = None
DEBUG_SUB = None


def ts2(nt):
    return slice(nt * 512, (nt + 1) * 512)


class Rec:

    def __init__(self, nc, stack):
        self.nc = nc
        self.ops = {e: [] for e in ENGINES}
        self.sems = {}
        self.cnt = {}
        self.wait_cache = {e: {} for e in ENGINES}
        for s in ("s_sms", "s_smg", "s_abf", "s_abfs", "s_strA", "s_strB", "s_strC", "s_xa", "s_xb",
                  "s_h0", "s_h1", "s_wA", "s_wB", "s_resA", "s_resB",
                  "s_dq", "s_wd", "s_oA", "s_oB", "s_oC", "s_oD",
                  "s_ca", "s_cb", "s_cc", "s_cd", "s_ra", "s_rb", "s_rc", "s_rd",
                  "s_pe", "s_dve", "s_act", "s_gp"):
            self.sems[s] = stack.enter_context(nc.semaphore(s))
            self.cnt[s] = 0

    def op(self, eng, fn, waits=(), inc=None, inc_by=1):
        w = tuple((s, v) for t in waits if t is not None for (s, v) in (t,) if v > 0)
        self.ops[eng].append((w, fn, inc, inc_by))
        if inc is not None:
            self.cnt[inc] += inc_by
            return (inc, self.cnt[inc])
        return None

    def dma(self, eng, out, in_, waits=(), sem=None):
        assert sem is not None, "explicit dma semaphore family required"
        return self.op(eng, lambda E, o=out, i=in_: E.dma_start(out=o, in_=i),
                       waits, inc=sem, inc_by=16)

    def fence(self, sems=("s_pe", "s_dve", "s_act")):
        return [(s, self.cnt[s]) for s in sems]

    def flush(self):
        nc = self.nc
        todo = {e: self.ops[e] for e in ENGINES if self.ops[e]}
        if not todo:
            return
        with nc.Block() as blk:
            for engname, oplist in todo.items():
                def make_cb(oplist=oplist, engname=engname):
                    def cb(E):
                        cache = self.wait_cache[engname]
                        for waits, fn, inc, n in oplist:
                            for sname, val in waits:
                                if val > cache.get(sname, 0):
                                    E.wait_ge(self.sems[sname], val)
                                    cache[sname] = val
                            inst = fn(E)
                            if inc is not None:
                                inst.then_inc(self.sems[inc], n)
                    return cb
                getattr(blk, engname)(make_cb())
        self.ops = {e: [] for e in ENGINES}


def build_nc(debug_taps=(), last_phase=None):
    nc = bass.Bass("TRN2", target_bir_lowering=False, detect_race_conditions=False)

    xT = nc.dram_tensor("xT", [H, S], F32, kind="ExternalInput").ap()
    w_qa = nc.dram_tensor("w_qa", [H, QR], F32, kind="ExternalInput").ap()
    g_qa = nc.dram_tensor("g_qa", [QR], F32, kind="ExternalInput").ap()
    w_qb = nc.dram_tensor("w_qb", [QR, NH * QK], F32, kind="ExternalInput").ap()
    w_kva = nc.dram_tensor("w_kva", [H, KVR + RD], F32, kind="ExternalInput").ap()
    g_kva = nc.dram_tensor("g_kva", [KVR], F32, kind="ExternalInput").ap()
    w_kvb = nc.dram_tensor("w_kvb", [KVR, NH * (ND + VD)], F32, kind="ExternalInput").ap()
    w_o = nc.dram_tensor("w_o", [NH * VD, H], F32, kind="ExternalInput").ap()
    qwT = nc.dram_tensor("qwT", [H, H], mybir.dt.uint8, kind="ExternalInput").ap()
    sbc = nc.dram_tensor("sbc", [16, H], F32, kind="ExternalInput").ap()
    zbc = nc.dram_tensor("zbc", [16, H], F32, kind="ExternalInput").ap()
    bias_d = nc.dram_tensor("bias", [H], F32, kind="ExternalInput").ap()
    rotq_d = nc.dram_tensor("rotq", [128, 128], F32, kind="ExternalInput").ap()
    rotk_d = nc.dram_tensor("rotk", [64, 64], F32, kind="ExternalInput").ap()
    outT = nc.dram_tensor("outT", [H, S], F32, kind="ExternalOutput").ap()

    QN = nc.dram_tensor("QN_sc", [NH * ND, S], BF16).ap()
    QRS = nc.dram_tensor("QR_sc", [NH * RD, S], BF16).ap()
    KN = nc.dram_tensor("KN_sc", [NH * ND, S], BF16).ap()
    VSC = nc.dram_tensor("V_sc", [S, NH * VD], BF16).ap()
    ASC = nc.dram_tensor("A_sc", [NH * VD, S], BF16).ap()
    R1 = nc.dram_tensor("R1_sc", [H, S], F32).ap()
    WD = nc.dram_tensor("WD_sc", [H, H], BF16).ap()
    R2 = nc.dram_tensor("R2_sc", [H, S], BF16).ap()
    BOUNCE = nc.dram_tensor("BN_sc", [8192], F32).ap()
    scratch = {"QN_sc": QN, "QR_sc": QRS, "KN_sc": KN, "V_sc": VSC, "A_sc": ASC,
               "R1_sc": R1, "WD_sc": WD, "R2_sc": R2}

    dbg = {}
    for name in debug_taps:
        src = scratch[name]
        dbg[name] = nc.dram_tensor("dbg_" + name, list(src.shape), src.dtype,
                                   kind="ExternalOutput").ap()

    stack = ExitStack()
    R = Rec(nc, stack)

    def sb(name, shape, dt):
        return stack.enter_context(nc.sbuf_tensor(name, shape, dt)).ap()

    ABF = sb("ABF", [128, 32 * TH], BF16)
    strA = sb("strA", [128, 32 * 128], BF16)
    strB = sb("strB", [128, 32 * 128], BF16)
    evA = sb("evA", [128, TH], F32)
    evB = sb("evB", [128, TH], F32)
    evC = sb("evC", [128, TH], F32)
    sqA = sb("sqA", [128, TH], BF16)
    sqB = sb("sqB", [128, TH], BF16)
    stgA = sb("stgA", [128, TH], BF16)
    stgB = sb("stgB", [128, TH], BF16)
    xfA = sb("xfA", [128, TH], F32)
    xfB = sb("xfB", [128, TH], F32)
    rowsA = sb("rowsA", [1, 2 * TH], F32)
    ones1f = sb("ones1f", [1, 128], F32)
    onesCb = sb("onesCb", [128, 1], BF16)
    ones128b = sb("ones128b", [128, 128], BF16)
    gqa_c = sb("gqa_c", [128, 12], F32)
    gkva_c = sb("gkva_c", [128, 4], F32)
    bias_c = sb("bias_c", [128, 32], F32)
    rotqb = sb("rotqb", [128, 128], BF16)
    rotkb = sb("rotkb", [64, 64], BF16)
    rkcols = sb("rkcols", [128, 16], F32)
    nmhat = sb("nmhat", [128, 32], F32)
    mrA = sb("mrA", [1, 32], F32)
    mrB = sb("mrB", [1, 32], F32)
    mrC = sb("mrC", [1, 32], F32)
    constT = sb("constT", [128, 8], F32)
    _consts = {}

    def creg(val):
        if val not in _consts:
            idx = len(_consts)
            ap = constT[:, idx:idx + 1]
            _consts[val] = ap
            nc.const_aps.register(F32, val, ap)
        return _consts[val]

    for v in (math.pi / 2.0, EPS_RMS, EPS_LN):
        creg(v)

    p12_stack = ExitStack()
    def sb12(name, shape, dt):
        return p12_stack.enter_context(nc.sbuf_tensor(name, shape, dt)).ap()
    krop = sb12("krop", [128, S], BF16)
    selN = sb12("selN", [128, 32 * 32], BF16)
    selR = sb12("selR", [128, 16 * 32], BF16)
    qn2max = sb12("qn2max", [32, 1], F32)
    kn2max = sb12("kn2max", [32, 1], F32)
    kr2max = sb12("kr2max", [1, 1], F32)
    n2tmp = sb12("n2tmp", [32, 1], F32)

    trig_stack = ExitStack()
    cosd = trig_stack.enter_context(nc.sbuf_tensor("cosd", [128, S], F32)).ap()
    sind = trig_stack.enter_context(nc.sbuf_tensor("sind", [128, S], F32)).ap()

    ROW = {"r0": rowsA[:, 0:TH], "r1": rowsA[:, TH:2 * TH],
           "t0": xfA[0:1, :], "t1": xfB[0:1, :]}

    def gemm(n_m, n_k, stripe_dma, rhs_of, ps_pair, epilogue,
             rhs_wait_of=None, mwid_of=None, extra_first_waits=(), stripes=None,
             tail_delay=2):
        if stripes is None:
            stripes = (strA, strB)
        nstr = len(stripes)
        stripe_tick = [None] * nstr
        ep_free = [(), ()]
        pe_tick = [None] * n_m
        pe_tails = [None] * n_m
        init_w = [("s_pe", R.cnt["s_pe"])]
        for m in range(min(nstr, n_m)):
            stripe_tick[m % nstr] = stripe_dma(m, stripes[m % nstr], m % nstr,
                                               extra_waits=init_w)
        for m in range(n_m):
            par = m % 2
            spar = m % nstr
            ps = ps_pair[par]
            buf = stripes[spar]
            mwid = 128 if mwid_of is None else mwid_of(m)
            if m >= tail_delay and pe_tails[m - tail_delay] is not None:
                pe_tails[m - tail_delay]()
            first = True
            last_t = None
            for k in range(n_k):
                for nt in range(2):
                    w = []
                    if first:
                        w = [stripe_tick[spar]] + list(ep_free[par]) + list(extra_first_waits)
                        first = False
                    if nt == 0 and rhs_wait_of is not None:
                        w.append(rhs_wait_of(k))
                    last_t = R.op("tensor", lambda E, ps=ps, buf=buf, k=k, nt=nt,
                                  mwid=mwid, n_k=n_k: E.matmul(
                                      ps[0:mwid, ts2(nt)],
                                      buf[:, k * mwid:(k + 1) * mwid],
                                      rhs_of(k, nt),
                                      start=(k == 0), stop=(k == n_k - 1)),
                                  waits=w,
                                  inc=("s_pe" if (k == n_k - 1 and nt == 1) else None))
            pe_tick[m] = last_t
            if m + nstr < n_m:
                stripe_tick[spar] = stripe_dma(m + nstr, stripes[spar], spar,
                                               extra_waits=[pe_tick[m]])
            free_waits, tail = epilogue(m, ps, pe_tick[m])
            ep_free[par] = free_waits
            pe_tails[m] = tail
        for mm in range(max(0, n_m - tail_delay), n_m):
            if pe_tails[mm] is not None:
                pe_tails[mm]()
        return pe_tick

    STR_SEMS = ("s_strA", "s_strB", "s_strC")

    def mk_stripe(wap, n_k, eng="gpsimd", mwid=128, kpar=128, static_waits=None):
        def f(m, buf, slot, extra_waits=()):
            w = list(extra_waits) + (static_waits() if static_waits else [])
            return R.dma(eng,
                         buf[:, 0:n_k * mwid].rearrange("p (k m) -> p k m", k=n_k),
                         wap[:, m * mwid:(m + 1) * mwid]
                         .rearrange("(k p) m -> p k m", p=kpar),
                         waits=w, sem=STR_SEMS[slot])
        return f

    def phase0():
        with (nc.sbuf_tensor("trig_u", [64, S], F32) as h1,
              nc.sbuf_tensor("trig_i", [64, S], I32) as h2,
              nc.sbuf_tensor("trig_y", [64, S], F32) as h3,
              nc.sbuf_tensor("trig_t", [64, S], F32) as h3b,
              nc.sbuf_tensor("ifrow", [1, 64], F32) as h4,
              nc.sbuf_tensor("itrow", [1, S], F32) as h5,
              nc.psum_tensor("ps_ang", [64, S], F32) as h6):
            trig_u, trig_i, trig_y = h1.ap(), h2.ap(), h3.ap()
            trig_t = h3b.ap()
            ifrow, itrow, ps_ang = h4.ap(), h5.ap(), h6.ap()

            for val, ap in _consts.items():
                R.op("vector", lambda E, ap=ap, val=val: E.memset(ap, val))
            R.op("vector", lambda E: E.memset(ones1f, 1.0))
            R.op("vector", lambda E: E.memset(onesCb, 1.0))
            R.op("vector", lambda E: E.memset(ones128b, 1.0))
            R.op("vector", lambda E: E.memset(qn2max, 0.0))
            R.op("vector", lambda E: E.memset(kn2max, 0.0))
            R.op("vector", lambda E: E.memset(kr2max, 0.0))
            R.op("vector", lambda E: E.memset(krop, 0.0))
            R.op("vector", lambda E: E.memset(selN, 0.0))
            R.op("vector", lambda E: E.memset(selR, 0.0))
            for h in range(32):
                R.op("vector", lambda E, h=h: E.memset(
                    selN[:, h * 32 + h:h * 32 + h + 1], 1.0))
            for j in range(16):
                R.op("vector", lambda E, j=j: E.memset(
                    selR[0:64, j * 32 + 2 * j:j * 32 + 2 * j + 1], 1.0))
                R.op("vector", lambda E, j=j: E.memset(
                    selR[64:128, j * 32 + 2 * j + 1:j * 32 + 2 * j + 2], 1.0))

            for c in range(12):
                R.dma("sync", gqa_c[:, c:c + 1],
                      g_qa[c * 128:(c + 1) * 128].rearrange("(p o) -> p o", o=1), sem="s_sms")
            for c in range(4):
                R.dma("sync", gkva_c[:, c:c + 1],
                      g_kva[c * 128:(c + 1) * 128].rearrange("(p o) -> p o", o=1), sem="s_sms")
            for c in range(32):
                R.dma("sync", bias_c[:, c:c + 1],
                      bias_d[c * 128:(c + 1) * 128].rearrange("(p o) -> p o", o=1), sem="s_sms")
            R.dma("gpsimd", rotqb, rotq_d, sem="s_smg")
            R.dma("gpsimd", rotkb, rotk_d, sem="s_smg")

            ti1 = R.op("gpsimd", lambda E: E.iota(
                trig_y[0:1, 0:32], [[1, 32]], base=0, channel_multiplier=0,
                allow_small_or_imprecise_dtypes=True), inc="s_gp")
            ti2 = R.op("gpsimd", lambda E: E.iota(
                itrow, [[1, S]], base=0, channel_multiplier=0,
                allow_small_or_imprecise_dtypes=True), inc="s_gp")
            a1 = R.op("scalar", lambda E: E.activation(
                ifrow[:, 0:32], trig_y[0:1, 0:32], AF.Exp,
                scale=-math.log(10000.0) / 32.0), waits=[ti1], inc="s_act")
            d0 = R.op("vector", lambda E: E.tensor_copy(ifrow[:, 32:64], ifrow[:, 0:32]),
                      waits=[a1], inc="s_dve")
            p_ang = None
            for n in range(4):
                p_ang = R.op("tensor", lambda E, n=n: E.matmul(
                    ps_ang[:, ts2(n)], ifrow[0:1, :], itrow[0:1, ts2(n)],
                    start=True, stop=True),
                    waits=[d0, ti2], inc=("s_pe" if n == 3 else None))
            def trig_table(out_ap, off, shift, w0):
                R.op("vector", lambda E: E.tensor_scalar(
                    trig_u, ps_ang, 1.0 / TWO_PI, off, op0=ALU.mult,
                    op1=(ALU.add if off else ALU.bypass)), waits=[w0])
                R.op("vector", lambda E: E.tensor_copy(trig_i, trig_u))
                R.op("vector", lambda E: E.tensor_copy(trig_u, trig_i))
                R.op("vector", lambda E: E.scalar_tensor_tensor(
                    trig_y, trig_u, TWO_PI, ps_ang, op0=ALU.mult, op1=ALU.subtract))
                if shift:
                    R.op("vector", lambda E: E.tensor_scalar(
                        trig_y, trig_y, shift, None, op0=ALU.subtract))
                R.op("vector", lambda E: E.tensor_scalar(
                    trig_u, trig_y, -math.pi, None, op0=ALU.is_lt))
                R.op("vector", lambda E: E.tensor_scalar(
                    trig_t, trig_y, math.pi, None, op0=ALU.is_gt))
                R.op("vector", lambda E: E.tensor_tensor(
                    trig_u, trig_u, trig_t, op=ALU.subtract))
                dd = R.op("vector", lambda E: E.scalar_tensor_tensor(
                    trig_t, trig_u, TWO_PI, trig_y, op0=ALU.mult, op1=ALU.add),
                    inc="s_dve")
                return R.op("scalar", lambda E, out_ap=out_ap: E.activation(
                    out_ap, trig_t, AF.Sin, scale=-1.0),
                    waits=[dd], inc="s_act")

            a3 = trig_table(sind[0:64, :], None, 0.0, p_ang)
            a4 = trig_table(cosd[0:64, :], 0.25, math.pi / 2.0, a3)
            R.dma("gpsimd", sind[64:128, :], sind[0:64, :], waits=[a4], sem="s_smg")
            R.dma("gpsimd", cosd[64:128, :], cosd[0:64, :], waits=[a4], sem="s_smg")
            R.flush()

    PH = {"stg_tick": {0: None, 1: None}}

    def load_xbf(th, w0):
        thc = slice(th * TH, (th + 1) * TH)
        xt = []
        for q in range(4):
            xt.append(R.dma(
                "gpsimd",
                ABF[:, q * 8 * TH:(q + 1) * 8 * TH].rearrange("p (k t) -> p k t", k=8),
                xT[q * 1024:(q + 1) * 1024, thc].rearrange("(k p) t -> p k t", p=128),
                waits=w0 if q == 0 else (), sem="s_abf"))
        return xt

    def phase1(th):
        thc = slice(th * TH, (th + 1) * TH)
        with ExitStack() as pst:
            qag = pst.enter_context(nc.sbuf_tensor(f"qag{th}", [128, 12 * TH], BF16)).ap()
            kcg = pst.enter_context(nc.sbuf_tensor(f"kcg{th}", [128, 4 * TH], BF16)).ap()
            rqbc = pst.enter_context(nc.sbuf_tensor(f"rqbc{th}", [128, TH], F32)).ap()
            rkbc = pst.enter_context(nc.sbuf_tensor(f"rkbc{th}", [128, TH], F32)).ap()
            krbf = pst.enter_context(nc.sbuf_tensor(f"krbf{th}", [64, TH], BF16)).ap()
            qrbuf = pst.enter_context(nc.sbuf_tensor(f"qrbuf{th}", [128, 2 * TH], BF16)).ap()
            strC = pst.enter_context(nc.sbuf_tensor(f"strC{th}", [128, 32 * 128], BF16)).ap()
            str3 = (strA, strB, strC)

            with (nc.psum_tensor(f"ps_g0_{th}", [128, TH], F32) as hg0,
                  nc.psum_tensor(f"ps_g1_{th}", [128, TH], F32) as hg1,
                  nc.psum_tensor(f"ps_qs{th}", [1, TH], F32) as hqs,
                  nc.psum_tensor(f"ps_ks{th}", [1, TH], F32) as hks):
                ps_g = (hg0.ap(), hg1.ap())
                ps_qs, ps_ks = hqs.ap(), hks.ap()

                if th == 0:
                    xt = load_xbf(0, R.fence())
                else:
                    xt = PH["xt1"]

                def xbf_of(k, nt):
                    return ABF[:, k * TH + nt * 512: k * TH + nt * 512 + 512]

                if DEBUG_SUB == "x":
                    R.flush()
                    return

                st_q = {"first": True}

                def epi_qa_mm(m, ps, pe_w):
                    return [pe_w], None

                def epi_qa_act(m, ps, pe_w):
                    sq = (sqA, sqB)[m % 2]
                    a = R.op("scalar", lambda E, ps=ps, sq=sq: E.activation(
                        sq, ps, AF.Square), waits=[pe_w], inc="s_act")
                    return [a], None

                def epi_qa_dve(m, ps, pe_w):
                    d = R.op("vector", lambda E, ps=ps, m=m: E.tensor_scalar(
                        qag[:, m * TH:(m + 1) * TH], ps, gqa_c[:, m:m + 1], None,
                        op0=ALU.mult), waits=[pe_w], inc="s_dve")
                    return [d], None

                def epi_qa(m, ps, pe_w):
                    sq = (sqA, sqB)[m % 2]
                    a = R.op("scalar", lambda E, ps=ps, sq=sq: E.activation(
                        sq, ps, AF.Square), waits=[pe_w], inc="s_act")
                    d = R.op("vector", lambda E, ps=ps, m=m: E.tensor_scalar(
                        qag[:, m * TH:(m + 1) * TH], ps, gqa_c[:, m:m + 1], None,
                        op0=ALU.mult), waits=[pe_w, a], inc="s_dve")

                    def tail(m=m, sq=sq, a=a):
                        st = st_q["first"]
                        st_q["first"] = False
                        for nt in range(2):
                            R.op("tensor", lambda E, sq=sq, nt=nt, st=st, m=m: E.matmul(
                                ps_qs[:, ts2(nt)], onesCb, sq[:, ts2(nt)],
                                start=st, stop=(m == 11)),
                                waits=[a],
                                inc=("s_pe" if (m == 11 and nt == 1) else None))
                    return [a, d], tail

                epi_sel = {"qamm": epi_qa_mm, "qaact": epi_qa_act,
                           "qadve": epi_qa_dve}.get(DEBUG_SUB, epi_qa)
                gemm(12, NKC, mk_stripe(w_qa, NKC), xbf_of, ps_g, epi_sel,
                     rhs_wait_of=lambda k: xt[k // 8],
                     extra_first_waits=R.fence(("s_sms", "s_smg")), stripes=str3)
                if DEBUG_SUB in ("qa", "qamm", "qaact", "qadve"):
                    R.flush()
                    return

                def kva_stripe(m, buf, slot, extra_waits=()):
                    sem = STR_SEMS[slot]
                    if m < 4:
                        return R.dma("gpsimd",
                                     buf[:, 0:NKC * 128].rearrange("p (k m) -> p k m", k=NKC),
                                     w_kva[:, m * 128:(m + 1) * 128]
                                     .rearrange("(k p) m -> p k m", p=128),
                                     waits=list(extra_waits), sem=sem)
                    return R.dma("gpsimd",
                                 buf[:, 0:NKC * 64].rearrange("p (k m) -> p k m", k=NKC),
                                 w_kva[:, 512:576].rearrange("(k p) m -> p k m", p=128),
                                 waits=list(extra_waits), sem=sem)

                st_k = {"first": True}
                krope_tick = [None]

                def epi_kva(m, ps, pe_w):
                    if m < 4:
                        sq = (sqA, sqB)[m % 2]
                        a = R.op("scalar", lambda E, ps=ps, sq=sq: E.activation(
                            sq, ps, AF.Square), waits=[pe_w], inc="s_act")
                        d = R.op("vector", lambda E, ps=ps, m=m: E.tensor_scalar(
                            kcg[:, m * TH:(m + 1) * TH], ps, gkva_c[:, m:m + 1], None,
                            op0=ALU.mult), waits=[pe_w, a], inc="s_dve")

                        def tail(m=m, sq=sq, a=a):
                            st = st_k["first"]
                            st_k["first"] = False
                            for nt in range(2):
                                R.op("tensor", lambda E, sq=sq, nt=nt, st=st, m=m: E.matmul(
                                    ps_ks[:, ts2(nt)], onesCb, sq[:, ts2(nt)],
                                    start=st, stop=(m == 3)),
                                    waits=[a],
                                    inc=("s_pe" if (m == 3 and nt == 1) else None))
                        return [a, d], tail

                    a = R.op("scalar", lambda E, ps=ps: E.activation(
                        krbf, ps[0:64, :], AF.Copy), waits=[pe_w], inc="s_act")
                    other = ps_g[(m + 1) % 2]
                    pr = None
                    for nt in range(2):
                        pr = R.op("tensor", lambda E, other=other, nt=nt: E.matmul(
                            other[0:64, ts2(nt)], rotkb, krbf[:, ts2(nt)],
                            start=True, stop=True),
                            waits=[a] + R.fence(("s_act", "s_dve")),
                            inc=("s_pe" if nt == 1 else None))
                    R.op("vector", lambda E: E.tensor_tensor(
                        evA[0:64, :], krbf, cosd[0:64, thc], op=ALU.mult), waits=[a])
                    R.op("vector", lambda E, other=other: E.tensor_tensor(
                        evB[0:64, :], other[0:64, :], sind[0:64, thc], op=ALU.mult),
                        waits=[pr])
                    d3 = R.op("vector", lambda E: E.tensor_tensor(
                        krop[0:64, thc], evA[0:64, :], evB[0:64, :], op=ALU.add),
                        inc="s_dve")
                    krope_tick[0] = d3
                    return [d3, pr], None

                kva_pe = gemm(5, NKC, kva_stripe, xbf_of, ps_g, epi_kva,
                              rhs_wait_of=lambda k: xt[k // 8],
                              mwid_of=lambda m: 128 if m < 4 else 64, stripes=str3)
                if th == 0:
                    PH["xt1"] = load_xbf(1, [kva_pe[4]])
                if DEBUG_SUB == "kva":
                    R.flush()
                    return

                def recip_chain(ps_row, nfeat, out_row, w0):
                    R.op("scalar", lambda E, ps_row=ps_row, nfeat=nfeat: E.activation(
                        ROW["t0"], ps_row, AF.Ln, scale=1.0 / nfeat),
                        waits=[w0], inc="s_act")
                    R.op("scalar", lambda E: E.activation(
                        ROW["t1"], ROW["t0"], AF.Exp, scale=0.5), inc="s_act")
                    R.op("scalar", lambda E: E.activation(
                        ROW["t0"], ROW["t1"], AF.Ln, bias=EPS_RMS), inc="s_act")
                    return R.op("scalar", lambda E, o=out_row: E.activation(
                        o, ROW["t0"], AF.Exp, scale=-1.0), inc="s_act")

                aq = recip_chain(ps_qs, QR, ROW["r0"], ("s_pe", R.cnt["s_pe"]))
                ak = recip_chain(ps_ks, KVR, ROW["r1"], aq)
                for (row, psb, out_bc, aw) in ((ROW["r0"], ps_g[0], rqbc, aq),
                                               (ROW["r1"], ps_g[1], rkbc, ak)):
                    lastb = None
                    for nt in range(2):
                        lastb = R.op("tensor", lambda E, psb=psb, row=row, nt=nt: E.matmul(
                            psb[:, ts2(nt)], ones1f, row[:, ts2(nt)],
                            start=True, stop=True),
                            waits=[aw] + R.fence(("s_dve", "s_act")),
                            inc=("s_pe" if nt == 1 else None))
                    R.op("vector", lambda E, o=out_bc, psb=psb: E.tensor_copy(o, psb),
                         waits=[lastb], inc="s_dve")

                trow = R.dma("gpsimd",
                             BOUNCE[th * TH:(th + 1) * TH].rearrange("(o t) -> o t", o=1),
                             ROW["r1"], waits=[ak], sem="s_smg")
                rk_tick = None
                for tc in range(8):
                    rk_tick = R.dma("gpsimd", rkcols[:, th * 8 + tc:th * 8 + tc + 1],
                                    BOUNCE[th * TH + tc * 128: th * TH + (tc + 1) * 128]
                                    .rearrange("(p o) -> p o", o=1),
                                    waits=[trow], sem="s_smg")
                phase1.rk_tick = rk_tick

                dsq = R.op("vector", lambda E: E.tensor_tensor(
                    sqA[0:64, :], krop[0:64, thc], krop[0:64, thc], op=ALU.mult),
                    waits=[krope_tick[0]], inc="s_dve")
                lastm = None
                for nt in range(2):
                    lastm = R.op("tensor", lambda E, nt=nt: E.matmul(
                        ps_ks[:, ts2(nt)], onesCb[0:64, :], sqA[0:64, ts2(nt)],
                        start=True, stop=True),
                        waits=[dsq, ak], inc=("s_pe" if nt == 1 else None))
                R.op("vector", lambda E: E.tensor_reduce(
                    ROW["t0"][:, 0:1], ps_ks, axis=AX.X, op=ALU.max), waits=[lastm])
                R.op("vector", lambda E: E.tensor_tensor(
                    kr2max, kr2max, ROW["t0"][:, 0:1], op=ALU.max), inc="s_dve")
                R.flush()
            if DEBUG_SUB == "1a":
                return

            with (nc.psum_tensor(f"ps_g0b{th}", [128, TH], F32) as hg0,
                  nc.psum_tensor(f"ps_g1b{th}", [128, TH], F32) as hg1,
                  nc.psum_tensor(f"ps_R{th}", [128, TH], F32) as hR,
                  nc.psum_tensor(f"ps_n2{th}", [32, TH], F32) as hn2):
                ps_g = (hg0.ap(), hg1.ap())
                ps_R, ps_n2 = hR.ap(), hn2.ap()

                stg_tick = PH["stg_tick"]
                rope_state = {"prev_d2": None, "prev_pr": None}
                n2s = {"first": True, "final": None}

                def epi_qb(m, ps, pe_w):
                    par = m % 2
                    stg = (stgA, stgB)[par]
                    sq = (sqA, sqB)[par]
                    if m < 32:
                        d1 = R.op("vector", lambda E, ps=ps, stg=stg: E.tensor_tensor(
                            stg, ps, rqbc, op=ALU.mult),
                            waits=[pe_w, stg_tick[par]], inc="s_dve")
                        stg_tick[par] = R.dma("sync", QN[m * 128:(m + 1) * 128, thc],
                                              stg, waits=[d1], sem=("s_wA", "s_wB")[par])
                        d2 = R.op("vector", lambda E, stg=stg, sq=sq: E.tensor_tensor(
                            sq, stg, stg, op=ALU.mult), inc="s_dve")

                        def tail(m=m, sq=sq, d2=d2):
                            st = n2s["first"]
                            n2s["first"] = False
                            for nt in range(2):
                                t = R.op("tensor", lambda E, sq=sq, nt=nt, st=st, m=m: E.matmul(
                                    ps_n2[:, ts2(nt)], selN[:, m * 32:(m + 1) * 32],
                                    sq[:, ts2(nt)], start=st, stop=(m == 47)),
                                    waits=[d2],
                                    inc=("s_pe" if (m == 47 and nt == 1) else None))
                                if m == 47 and nt == 1:
                                    n2s["final"] = t
                        return [d1, d2], tail

                    j = m - 32
                    qrb = qrbuf[:, par * TH:(par + 1) * TH]
                    d0 = R.op("vector", lambda E, ps=ps, qrb=qrb: E.tensor_tensor(
                        qrb, ps, rqbc, op=ALU.mult),
                        waits=[pe_w, rope_state["prev_pr"]], inc="s_dve")
                    dsq = R.op("vector", lambda E, qrb=qrb, sq=sq: E.tensor_tensor(
                        sq, qrb, qrb, op=ALU.mult), inc="s_dve")

                    def tail(m=m, j=j, par=par, qrb=qrb, stg=stg, sq=sq, d0=d0, dsq=dsq):
                        pr = None
                        for nt in range(2):
                            pr = R.op("tensor", lambda E, qrb=qrb, nt=nt: E.matmul(
                                ps_R[:, ts2(nt)], rotqb, qrb[:, ts2(nt)],
                                start=True, stop=True),
                                waits=[d0, rope_state["prev_d2"]],
                                inc=("s_pe" if nt == 1 else None))
                        rope_state["prev_pr"] = pr
                        st = n2s["first"]
                        n2s["first"] = False
                        for nt in range(2):
                            t = R.op("tensor", lambda E, sq=sq, nt=nt, st=st, j=j, m=m: E.matmul(
                                ps_n2[:, ts2(nt)], selR[:, j * 32:(j + 1) * 32],
                                sq[:, ts2(nt)], start=st, stop=(m == 47)),
                                waits=[dsq],
                                inc=("s_pe" if (m == 47 and nt == 1) else None))
                            if m == 47 and nt == 1:
                                n2s["final"] = t
                        dd1 = R.op("vector", lambda E, qrb=qrb: E.tensor_tensor(
                            evA, qrb, cosd[:, thc], op=ALU.mult), waits=[dsq])
                        dd2 = R.op("vector", lambda E: E.tensor_tensor(
                            evB, ps_R, sind[:, thc], op=ALU.mult), waits=[pr])
                        rope_state["prev_d2"] = dd2
                        d3 = R.op("vector", lambda E, stg=stg: E.tensor_tensor(
                            stg, evA, evB, op=ALU.add),
                            waits=[stg_tick[par]], inc="s_dve")
                        stg_tick[par] = R.dma("sync", QRS[j * 64 * 2:(j + 1) * 64 * 2, thc],
                                              stg, waits=[d3], sem=("s_wA", "s_wB")[par])
                    return [d0], tail

                gemm(48, 12, mk_stripe(w_qb, 12),
                     lambda k, nt: qag[:, k * TH + nt * 512: k * TH + nt * 512 + 512],
                     ps_g, epi_qb, stripes=str3, tail_delay=1)

                R.op("vector", lambda E: E.tensor_reduce(
                    n2tmp, ps_n2, axis=AX.X, op=ALU.max), waits=[n2s["final"]])
                R.op("vector", lambda E: E.tensor_tensor(
                    qn2max, qn2max, n2tmp, op=ALU.max), inc="s_dve")
                R.flush()
            if DEBUG_SUB == "1b":
                return

            with (nc.psum_tensor(f"ps_g0c{th}", [128, TH], F32) as hg0,
                  nc.psum_tensor(f"ps_g1c{th}", [128, TH], F32) as hg1,
                  nc.psum_tensor(f"ps_n2c{th}", [32, TH], F32) as hn2):
                ps_g = (hg0.ap(), hg1.ap())
                ps_n2 = hn2.ap()
                stg_tick = PH["stg_tick"]
                n2s = {"first": True, "final": None}

                def epi_kvb(m, ps, pe_w):
                    par = m % 2
                    stg = (stgA, stgB)[par]
                    sq = (sqA, sqB)[par]
                    d1 = R.op("vector", lambda E, ps=ps, stg=stg: E.tensor_tensor(
                        stg, ps, rkbc, op=ALU.mult),
                        waits=[pe_w, stg_tick[par]], inc="s_dve")
                    stg_tick[par] = R.dma("sync", KN[m * 128:(m + 1) * 128, thc],
                                          stg, waits=[d1], sem=("s_wA", "s_wB")[par])
                    d2 = R.op("vector", lambda E, stg=stg, sq=sq: E.tensor_tensor(
                        sq, stg, stg, op=ALU.mult), inc="s_dve")

                    def tail(m=m, sq=sq, d2=d2):
                        st = n2s["first"]
                        n2s["first"] = False
                        for nt in range(2):
                            t = R.op("tensor", lambda E, sq=sq, nt=nt, st=st, m=m: E.matmul(
                                ps_n2[:, ts2(nt)], selN[:, m * 32:(m + 1) * 32],
                                sq[:, ts2(nt)], start=st, stop=(m == 31)),
                                waits=[d2],
                                inc=("s_pe" if (m == 31 and nt == 1) else None))
                            if m == 31 and nt == 1:
                                n2s["final"] = t
                    return [d1, d2], tail

                gemm(32, 4, mk_stripe(w_kvb, 4),
                     lambda k, nt: kcg[:, k * TH + nt * 512: k * TH + nt * 512 + 512],
                     ps_g, epi_kvb, stripes=str3)

                R.op("vector", lambda E: E.tensor_reduce(
                    n2tmp, ps_n2, axis=AX.X, op=ALU.max), waits=[n2s["final"]])
                R.op("vector", lambda E: E.tensor_tensor(
                    kn2max, kn2max, n2tmp, op=ALU.max), inc="s_dve")

                with nc.sbuf_tensor(f"wvb{th}", [128, 2 * 4 * 512], BF16) as hwv:
                    wvb = hwv.ap()
                    wv_tick = [None, None]
                    pe_done = [None, None]
                    dve_done = {0: None, 1: None}

                    def load_wv(vq, extra=()):
                        par = vq % 2
                        wv_tick[par] = R.dma(
                            "gpsimd",
                            wvb[:, par * 2048:(par + 1) * 2048]
                            .rearrange("p (k m) -> p k m", k=4),
                            w_kvb[:, 4096 + vq * 512: 4096 + (vq + 1) * 512]
                            .rearrange("(k p) m -> p k m", p=128),
                            waits=list(extra), sem=("s_strA", "s_strB")[par])

                    init_v = [("s_pe", R.cnt["s_pe"])]
                    load_wv(0, extra=init_v)
                    load_wv(1, extra=init_v)
                    for vq in range(8):
                        par = vq % 2
                        for tc in range(8):
                            pp = ps_g[tc % 2]
                            last = None
                            for kv in range(4):
                                w = []
                                if kv == 0:
                                    w = [wv_tick[par], dve_done[tc % 2]]
                                    if vq == 0 and tc < 2:
                                        w += R.fence(("s_dve", "s_act"))
                                last = R.op("tensor", lambda E, pp=pp, kv=kv, tc=tc,
                                            par=par: E.matmul(
                                                pp[:, 0:512],
                                                kcg[:, kv * TH + tc * 128: kv * TH + (tc + 1) * 128],
                                                wvb[:, par * 2048 + kv * 512: par * 2048 + (kv + 1) * 512],
                                                start=(kv == 0), stop=(kv == 3)),
                                            waits=w, inc=("s_pe" if kv == 3 else None))
                            pe_done[par] = last
                            stg = (stgA, stgB)[tc % 2]
                            dv = R.op("vector", lambda E, pp=pp, stg=stg, tc=tc: E.tensor_scalar(
                                stg[:, 0:512], pp[:, 0:512],
                                rkcols[:, th * 8 + tc:th * 8 + tc + 1], None, op0=ALU.mult),
                                waits=[last, stg_tick[tc % 2], phase1.rk_tick], inc="s_dve")
                            dve_done[tc % 2] = dv
                            stg_tick[tc % 2] = R.dma(
                                "sync",
                                VSC[(th * 8 + tc) * 128:(th * 8 + tc + 1) * 128,
                                    vq * 512:(vq + 1) * 512],
                                stg[:, 0:512], waits=[dv], sem=("s_wA", "s_wB")[tc % 2])
                        if vq + 2 < 8:
                            load_wv(vq + 2, extra=[pe_done[par]])
                R.flush()

    def phase2():
        with ExitStack() as pst:
            def psb(name, shape, dt):
                return pst.enter_context(nc.sbuf_tensor(name, shape, dt)).ap()
            qnb = psb("qnb", [128, 2 * S], BF16)
            qrb2 = psb("qrb2", [128, 2 * S], BF16)
            knb = psb("knb", [128, 2 * S], BF16)
            vsb = psb("vsb", [128, 2 * 16 * 128], BF16)
            ptb = psb("ptb", [128, 2 * TH], BF16)
            recl = psb("recl", [128, TH], F32)
            lnl = psb("lnl", [128, TH], F32)
            aostg = psb("aostg", [128, 2 * TH], BF16)
            qfb = psb("qfb", [128, TH], F32)
            R.op("vector", lambda E: E.memset(qrb2, 0.0))
            sbcb = psb("sbcb", [128, TH], F32)
            zbcb = psb("zbcb", [128, TH], F32)
            dqt = psb("dqt", [128, TH], F32)
            wdst = psb("wdst", [128, 2 * TH], BF16)

            with nc.psum_tensor("ps_m", [128, 32], F32) as hm:
                ps_m = hm.ap()
                t1 = R.dma("gpsimd", BOUNCE[4096:4128].rearrange("(p o) -> p o", o=1),
                           qn2max, waits=R.fence(("s_dve", "s_pe")), sem="s_smg")
                t2 = R.dma("gpsimd", BOUNCE[4128:4160].rearrange("(p o) -> p o", o=1),
                           kn2max, sem="s_smg")
                t3 = R.dma("gpsimd", mrA, BOUNCE[4096:4128].rearrange("(o t) -> o t", o=1),
                           waits=[t2], sem="s_smg")
                t4 = R.dma("gpsimd", mrB, BOUNCE[4128:4160].rearrange("(o t) -> o t", o=1),
                           waits=[t2], sem="s_smg")
                d1 = R.op("vector", lambda E: E.tensor_scalar(
                    mrC, mrB, kr2max[0:1, 0:1], None, op0=ALU.add),
                    waits=[t3, t4], inc="s_dve")
                R.op("scalar", lambda E: E.activation(mrB, mrC, AF.Sqrt),
                     waits=[d1], inc="s_act")
                a2 = R.op("scalar", lambda E: E.activation(mrC, mrA, AF.Sqrt),
                          inc="s_act")
                R.op("vector", lambda E: E.tensor_tensor(
                    mrA, mrB, mrC, op=ALU.mult), waits=[a2])
                R.op("vector", lambda E: E.tensor_scalar(
                    mrB, mrA, SCALE, 0.1, op0=ALU.mult, op1=ALU.add))
                d4 = R.op("vector", lambda E: E.tensor_scalar(
                    mrC, mrB, -1.0, None, op0=ALU.mult), inc="s_dve")
                pm = R.op("tensor", lambda E: E.matmul(
                    ps_m, ones1f, mrC, start=True, stop=True),
                    waits=[d4], inc="s_pe")
                nm_tick = R.op("vector", lambda E: E.tensor_copy(nmhat, ps_m),
                               waits=[pm], inc="s_dve")
                R.flush()

            with (nc.psum_tensor("ps_s0", [128, TH], F32) as hs0,
                  nc.psum_tensor("ps_s1", [128, TH], F32) as hs1,
                  nc.psum_tensor("ps_o", [128, TH], F32) as ho,
                  nc.psum_tensor("ps_l", [128, TH], F32) as hl):
                ps_s = (hs0.ap(), hs1.ap())
                ps_o, ps_l = ho.ap(), hl.ap()

                ld_tick = [None, None]
                head_pe_end = [None, None]
                ao_tick = {0: None, 1: None}
                ps_s_free = {0: nm_tick, 1: nm_tick}
                prev = {"ao_dve": None, "ln": None}
                dq = {"s": None, "z": None, "sub": None,
                      "mul": {0: None, 1: None}, "out": {0: None, 1: None}}

                def load_head(h):
                    par = h % 2
                    sem = ("s_h0", "s_h1")[par]
                    w = [head_pe_end[par]]
                    if h < 2:
                        w = w + R.fence(("s_wA", "s_wB"))
                    R.dma("sync", qnb[:, par * S:(par + 1) * S],
                          QN[h * 128:(h + 1) * 128, :], waits=w, sem=sem)
                    R.dma("sync", qrb2[0:64, par * S:(par + 1) * S],
                          QRS[h * 64:(h + 1) * 64, :], waits=w, sem=sem)
                    R.dma("sync", knb[:, par * S:(par + 1) * S],
                          KN[h * 128:(h + 1) * 128, :], waits=w, sem=sem)
                    ld_tick[par] = R.dma(
                        "sync",
                        vsb[:, par * 2048:(par + 1) * 2048]
                        .rearrange("p (c v) -> p c v", c=16),
                        VSC[:, h * 128:(h + 1) * 128]
                        .rearrange("(c p) v -> p c v", p=128), waits=w, sem=sem)

                def dequant_stripe(c):
                    gi = c // 2
                    for qv in range(4):
                        tq = R.dma("gpsimd", qfb,
                                   qwT[c * 128:(c + 1) * 128, qv * TH:(qv + 1) * TH],
                                   waits=[dq["sub"]], sem="s_dq")
                        dq["s"] = R.dma("gpsimd", sbcb,
                                        sbc[gi:gi + 1, qv * TH:(qv + 1) * TH]
                                        .partition_broadcast(128),
                                        waits=[dq["mul"][0], dq["mul"][1]], sem="s_dq")
                        dq["z"] = R.dma("gpsimd", zbcb,
                                        zbc[gi:gi + 1, qv * TH:(qv + 1) * TH]
                                        .partition_broadcast(128),
                                        waits=[dq["sub"]], sem="s_dq")
                        dsub = R.op("gpsimd", lambda E: E.tensor_tensor(
                            dqt, qfb, zbcb, op=ALU.subtract),
                            waits=[tq, dq["z"], dq["s"]], inc="s_gp")
                        dq["sub"] = dsub
                        par = qv % 2
                        dmul = R.op("gpsimd", lambda E, par=par: E.tensor_tensor(
                            wdst[:, par * TH:(par + 1) * TH], dqt, sbcb, op=ALU.mult),
                            waits=[dq["out"][par]], inc="s_gp")
                        dq["mul"][par] = dmul
                        dq["out"][par] = R.dma(
                            "gpsimd", WD[c * 128:(c + 1) * 128, qv * TH:(qv + 1) * TH],
                            wdst[:, par * TH:(par + 1) * TH], waits=[dmul], sem="s_wd")

                def emit_pv(h, qh, c, vs, exp_tick):
                    cp = c % 2
                    last = None
                    for nt in range(2):
                        w = []
                        if nt == 0:
                            w = [exp_tick[c]]
                            if c == 0:
                                w += [prev["ao_dve"], prev["ln"]]
                        R.op("tensor", lambda E, c=c, nt=nt, cp=cp, vs=vs: E.matmul(
                            ps_o[:, ts2(nt)], vs[:, c * 128:(c + 1) * 128],
                            ptb[:, cp * TH + nt * 512: cp * TH + nt * 512 + 512],
                            start=(c == 0), stop=(c == 15)), waits=w)
                        last = R.op("tensor", lambda E, c=c, nt=nt, cp=cp: E.matmul(
                            ps_l[:, ts2(nt)], ones128b,
                            ptb[:, cp * TH + nt * 512: cp * TH + nt * 512 + 512],
                            start=(c == 0), stop=(c == 15)),
                            inc=("s_pe" if (c == 15 and nt == 1) else None))
                    return last

                load_head(0)
                load_head(1)
                for h in range(NH):
                    par = h % 2
                    dequant_stripe(h)
                    qn = qnb[:, par * S:(par + 1) * S]
                    qr = qrb2[:, par * S:(par + 1) * S]
                    kn = knb[:, par * S:(par + 1) * S]
                    vs = vsb[:, par * 2048:(par + 1) * 2048]
                    last_pe = None
                    for qh in range(2):
                        exp_tick = [None] * 16
                        pvl_last = None
                        for c in range(16):
                            cp = c % 2
                            sc_last = None
                            for nt in range(2):
                                w = []
                                if nt == 0:
                                    w = [ld_tick[par], ps_s_free[cp]]
                                colsl = slice(qh * TH + nt * 512, qh * TH + nt * 512 + 512)
                                R.op("tensor", lambda E, cp=cp, c=c, kn=kn, qn=qn,
                                     colsl=colsl, nt=nt: E.matmul(
                                         ps_s[cp][:, ts2(nt)],
                                         kn[:, c * 128:(c + 1) * 128],
                                         qn[:, colsl], start=True, stop=False),
                                     waits=w)
                                sc_last = R.op("tensor", lambda E, cp=cp, c=c, qr=qr,
                                               colsl=colsl, nt=nt: E.matmul(
                                                   ps_s[cp][:, ts2(nt)],
                                                   krop[:, c * 128:(c + 1) * 128],
                                                   qr[:, colsl], start=False, stop=True),
                                               inc=("s_pe" if nt == 1 else None))
                            if c >= 1:
                                pvl_last = emit_pv(h, qh, c - 1, vs, exp_tick)
                            exp_tick[c] = R.op("scalar", lambda E, cp=cp, h=h: E.activation(
                                ptb[:, cp * TH:(cp + 1) * TH], ps_s[cp], AF.Exp,
                                bias=nmhat[:, h:h + 1], scale=SCALE),
                                waits=[sc_last], inc="s_act")
                            ps_s_free[cp] = exp_tick[c]
                        pvl_last = emit_pv(h, qh, 15, vs, exp_tick)
                        a1 = R.op("scalar", lambda E: E.activation(
                            lnl, ps_l, AF.Ln), waits=[pvl_last], inc="s_act")
                        a2 = R.op("scalar", lambda E: E.activation(
                            recl, lnl, AF.Exp, scale=-1.0), inc="s_act")
                        dv = R.op("vector", lambda E, qh=qh: E.tensor_tensor(
                            aostg[:, qh * TH:(qh + 1) * TH], ps_o, recl, op=ALU.mult),
                            waits=[a2, ao_tick[qh]], inc="s_dve")
                        ao_tick[qh] = R.dma(
                            "sync", ASC[h * 128:(h + 1) * 128, qh * TH:(qh + 1) * TH],
                            aostg[:, qh * TH:(qh + 1) * TH], waits=[dv],
                            sem=("s_wA", "s_wB")[qh])
                        prev["ao_dve"] = dv
                        prev["ln"] = a1
                        last_pe = pvl_last
                    head_pe_end[par] = last_pe
                    if h + 2 < NH:
                        load_head(h + 2)
                R.flush()

    def phase3():
        with ExitStack() as pst:
            def psb(name, shape, dt):
                return pst.enter_context(nc.sbuf_tensor(name, shape, dt)).ap()
            hg0 = pst.enter_context(nc.psum_tensor("ps_g0d", [128, TH], F32))
            hg1 = pst.enter_context(nc.psum_tensor("ps_g1d", [128, TH], F32))
            hsx = pst.enter_context(nc.psum_tensor("ps_sx", [1, TH], F32))
            hsq = pst.enter_context(nc.psum_tensor("ps_sq", [1, TH], F32))
            ps_g = (hg0.ap(), hg1.ap())
            ps_sx, ps_sq = hsx.ap(), hsq.ap()
            strC3 = psb("strC3", [128, 32 * 128], BF16)
            str3 = (strA, strB, strC3)
            cx4 = psb("cx4", [128, 2 * 4 * TH], F32)
            ox4 = psb("ox4", [128, 2 * 4 * TH], BF16)
            ouf = psb("ouf", [128, 2 * 2 * TH], F32)
            oev = evC
            mu1_bc = psb("mu1bc", [128, TH], F32)
            rs1_bc = psb("rs1bc", [128, TH], F32)
            mu2_bc = psb("mu2bc", [128, TH], F32)
            rs2_bc = psb("rs2bc", [128, TH], F32)

            xf_tick = [None, None]
            xf_free = [None, None]
            res_tick = [None, None]
            stg_free = [None, None]

            def ln_rows_and_bcast(final_tick, mu_bc, rstd_bc):
                R.op("vector", lambda E: E.tensor_scalar(
                    ROW["r0"], ps_sx, 1.0 / H, None, op0=ALU.mult), waits=[final_tick])
                R.op("vector", lambda E: E.tensor_scalar(
                    ROW["t1"], ps_sq, 1.0 / H, None, op0=ALU.mult))
                R.op("vector", lambda E: E.tensor_tensor(
                    ROW["t0"], ROW["r0"], ROW["r0"], op=ALU.mult))
                dv = R.op("vector", lambda E: E.tensor_tensor(
                    ROW["t1"], ROW["t1"], ROW["t0"], op=ALU.subtract), inc="s_dve")
                R.op("scalar", lambda E: E.activation(
                    ROW["t0"], ROW["t1"], AF.Ln, bias=EPS_LN), waits=[dv], inc="s_act")
                ar = R.op("scalar", lambda E: E.activation(
                    ROW["r1"], ROW["t0"], AF.Exp, scale=-0.5), inc="s_act")
                out = {}
                for (row, psb_, bc, aw) in ((ROW["r0"], ps_g[0], mu_bc, dv),
                                            (ROW["r1"], ps_g[1], rstd_bc, ar)):
                    lastb = None
                    for nt in range(2):
                        lastb = R.op("tensor", lambda E, psb_=psb_, row=row, nt=nt: E.matmul(
                            psb_[:, ts2(nt)], ones1f, row[:, ts2(nt)],
                            start=True, stop=True),
                            waits=[aw] + R.fence(("s_dve", "s_act")),
                            inc=("s_pe" if nt == 1 else None))
                    out[id(bc)] = R.op("vector", lambda E, bc=bc, psb_=psb_: E.tensor_copy(bc, psb_),
                                       waits=[lastb], inc="s_dve")
                return out[id(mu_bc)], out[id(rstd_bc)]

            def wo_fetch_x(m, thc):
                par = m % 2
                w = [xf_free[par]]
                xf_tick[par] = R.dma("sync", (xfA, xfB)[par],
                                     xT[m * 128:(m + 1) * 128, thc], waits=w,
                                     sem=("s_xa", "s_xb")[par])

            def run_wo(th, thc, at, prefetched=False):
                st = {"first": True, "final": None,
                      "tail_tick": [("s_pe", R.cnt["s_pe"])] * 2}

                def fetch_x(m):
                    wo_fetch_x(m, thc)

                if not prefetched:
                    fetch_x(0)
                    fetch_x(1)

                def epi(m, ps, pe_w):
                    par = m % 2
                    rr = (evA, evB)[par]
                    xf = (xfA, xfB)[par]
                    d1 = R.op("vector", lambda E, ps=ps, rr=rr, xf=xf: E.tensor_tensor(
                        rr, ps, xf, op=ALU.add),
                        waits=[pe_w, xf_tick[par], res_tick[par]], inc="s_dve")
                    xf_free[par] = d1
                    if m + 2 < 32:
                        fetch_x(m + 2)
                    res_tick[par] = R.dma("gpsimd", R1[m * 128:(m + 1) * 128, thc],
                                          rr, waits=[d1], sem=("s_resA", "s_resB")[par])
                    stg = (stgA, stgB)[par]
                    a1 = R.op("scalar", lambda E, rr=rr, stg=stg: E.activation(
                        stg, rr, AF.Copy),
                        waits=[d1, st["tail_tick"][par]], inc="s_act")
                    sq = (sqA, sqB)[par]
                    d2 = R.op("vector", lambda E, stg=stg, sq=sq: E.tensor_tensor(
                        sq, stg, stg, op=ALU.mult), waits=[a1], inc="s_dve")

                    def tail(m=m, stg=stg, sq=sq, a1=a1, d2=d2, par=par):
                        stt = st["first"]
                        st["first"] = False
                        w_extra = R.fence(("s_dve",)) if stt else []
                        for nt in range(2):
                            R.op("tensor", lambda E, stg=stg, nt=nt, stt=stt, m=m: E.matmul(
                                ps_sx[:, ts2(nt)], onesCb, stg[:, ts2(nt)],
                                start=stt, stop=(m == 31)),
                                waits=[a1] + w_extra)
                            t = R.op("tensor", lambda E, sq=sq, nt=nt, stt=stt, m=m: E.matmul(
                                ps_sq[:, ts2(nt)], onesCb, sq[:, ts2(nt)],
                                start=stt, stop=(m == 31)), waits=[d2],
                                inc="s_pe")
                            if nt == 1:
                                st["tail_tick"][par] = t
                                if m == 31:
                                    st["final"] = t
                    return [d1], tail

                gemm(32, 32, mk_stripe(w_o, 32),
                     lambda k, nt: ABF[:, k * TH + nt * 512: k * TH + nt * 512 + 512],
                     ps_g, epi, rhs_wait_of=lambda k: at[k // 8], stripes=str3)
                return st["final"]

            def corr_pass(th, thc, mu_tick):
                ticks = [None] * 32
                dv = [None, None]
                r1f = R.fence(("s_resA", "s_resB", "s_pe"))
                for b in range(8):
                    par = b % 2
                    cx = cx4[:, par * 4 * TH:(par + 1) * 4 * TH]
                    w = [dv[par]] + (list(r1f) if b < 2 else [])
                    ld = R.dma("sync", cx.rearrange("p (c t) -> p c t", c=4),
                               R1[b * 512:(b + 1) * 512, thc]
                               .rearrange("(c p) t -> p c t", p=128),
                               waits=w, sem=("s_ca", "s_cb")[par])
                    for j in range(4):
                        k = 4 * b + j
                        dv[par] = R.op("vector", lambda E, cx=cx, j=j, k=k: E.tensor_tensor(
                            ABF[:, k * TH:(k + 1) * TH],
                            cx[:, j * TH:(j + 1) * TH], mu1_bc, op=ALU.subtract),
                            waits=[ld, mu_tick], inc="s_dve")
                        ticks[k] = dv[par]
                return ticks

            def run_ffn(th, thc, corr):
                st = {"first": True, "final": None,
                      "tail_tick": [("s_pe", R.cnt["s_pe"])] * 2}

                def epi(m, ps, pe_w):
                    par = m % 2
                    rr = (evA, evB)[par]
                    t1 = R.op("vector", lambda E, ps=ps, m=m: E.tensor_tensor(
                        evC, ps, ABF[:, m * TH:(m + 1) * TH], op=ALU.add),
                        waits=[pe_w], inc="s_dve")
                    R.op("vector", lambda E: E.tensor_tensor(
                        evC, evC, rs1_bc, op=ALU.mult))
                    d1 = R.op("vector", lambda E, rr=rr, m=m: E.tensor_scalar(
                        rr, evC, bias_c[:, m:m + 1], None, op0=ALU.add), inc="s_dve")
                    rb = (stgA, stgB)[par]
                    a1 = R.op("scalar", lambda E, rr=rr, rb=rb: E.activation(
                        rb, rr, AF.Copy),
                        waits=[d1, st["tail_tick"][par], res_tick[par]], inc="s_act")
                    res_tick[par] = R.dma("gpsimd", R2[m * 128:(m + 1) * 128, thc],
                                          rb, waits=[a1], sem=("s_resA", "s_resB")[par])
                    sq = (sqA, sqB)[par]
                    d2 = R.op("vector", lambda E, rb=rb, sq=sq: E.tensor_tensor(
                        sq, rb, rb, op=ALU.mult), waits=[a1], inc="s_dve")

                    def tail(m=m, rb=rb, sq=sq, a1=a1, d2=d2, par=par):
                        stt = st["first"]
                        st["first"] = False
                        w_extra = R.fence(("s_dve",)) if stt else []
                        for nt in range(2):
                            R.op("tensor", lambda E, rb=rb, nt=nt, stt=stt, m=m: E.matmul(
                                ps_sx[:, ts2(nt)], onesCb, rb[:, ts2(nt)],
                                start=stt, stop=(m == 31)),
                                waits=[a1] + w_extra)
                            t = R.op("tensor", lambda E, sq=sq, nt=nt, stt=stt, m=m: E.matmul(
                                ps_sq[:, ts2(nt)], onesCb, sq[:, ts2(nt)],
                                start=stt, stop=(m == 31)), waits=[d2],
                                inc="s_pe")
                            if nt == 1:
                                st["tail_tick"][par] = t
                                if m == 31:
                                    st["final"] = t
                    return [t1], tail

                gemm(32, 32,
                     mk_stripe(WD, 32, static_waits=lambda: R.fence(("s_wd",))),
                     lambda k, nt: ABF[:, k * TH + nt * 512: k * TH + nt * 512 + 512],
                     ps_g, epi, rhs_wait_of=lambda k: corr[k], stripes=str3)
                return st["final"]

            def out_loop(th, thc):
                tickO = [None, None]
                wfree = [None, None]
                r2_fence = R.fence(("s_resA", "s_resB", "s_dve"))
                for b in range(8):
                    par = b % 2
                    ox = ox4[:, par * 4 * TH:(par + 1) * 4 * TH]
                    w0 = [tickO[par]] + (list(r2_fence) if b < 2 else [])
                    t = R.dma("sync", ox.rearrange("p (c t) -> p c t", c=4),
                              R2[b * 512:(b + 1) * 512, thc]
                              .rearrange("(c p) t -> p c t", p=128),
                              waits=w0, sem=("s_ra", "s_rb")[par])
                    tickO[par] = t
                    for half in range(2):
                        ouw = ouf[:, half * 2 * TH:(half + 1) * 2 * TH]
                        d2 = None
                        for j in range(2):
                            oxj = ox[:, (half * 2 + j) * TH:(half * 2 + j + 1) * TH]
                            R.op("vector", lambda E, oxj=oxj: E.tensor_tensor(
                                oev, oxj, mu2_bc, op=ALU.subtract),
                                waits=[t, wfree[half]])
                            d2 = R.op("vector", lambda E, oxj=oxj, ouw=ouw, j=j:
                                      E.tensor_tensor(
                                          ouw[:, j * TH:(j + 1) * TH], oev, rs2_bc,
                                          op=ALU.mult), inc="s_dve")
                        wfree[half] = R.dma(
                            "sync",
                            outT[b * 512 + half * 256:(b * 512 + half * 256) + 256, thc]
                            .rearrange("(c p) t -> p c t", p=128),
                            ouw.rearrange("p (c t) -> p c t", c=2),
                            waits=[d2], sem=("s_oA", "s_oB")[half])

            def load_at(thc):
                at = []
                for q in range(4):
                    at.append(R.dma(
                        "sync",
                        ABF[:, q * 8 * TH:(q + 1) * 8 * TH].rearrange("p (k t) -> p k t", k=8),
                        ASC[q * 1024:(q + 1) * 1024, thc].rearrange("(k p) t -> p k t", p=128),
                        waits=R.fence(("s_pe", "s_dve", "s_act", "s_wA", "s_wB"))
                        if q == 0 else (), sem="s_abfs"))
                return at

            thc0 = slice(0, TH)
            thc1 = slice(TH, 2 * TH)
            at = load_at(thc0)
            prefetched = False
            for th in range(2):
                thc = (thc0, thc1)[th]
                fin = run_wo(th, thc, at, prefetched=prefetched)
                mu_t, rs_t = ln_rows_and_bcast(fin, mu1_bc, rs1_bc)
                corr = corr_pass(th, thc, mu_t)
                fin2 = run_ffn(th, thc, corr)
                mu2_t, rs2_t = ln_rows_and_bcast(fin2, mu2_bc, rs2_bc)
                if th == 0:
                    at = load_at(thc1)
                    wo_fetch_x(0, thc1)
                    wo_fetch_x(1, thc1)
                    prefetched = True
                out_loop(th, thc)
            R.flush()

    def finale():
        allsems = tuple(R.sems.keys())
        for name in debug_taps:
            R.dma("gpsimd", dbg[name], scratch[name], waits=R.fence(allsems),
                  sem="s_smg")
        R.op("gpsimd", lambda E: E.nop(), waits=R.fence(allsems))
        R.flush()

    phases = [("p0", phase0),
              ("p1t0", lambda: phase1(0)), ("p1t1", lambda: phase1(1)),
              ("p2", phase2),
              ("p3", phase3)]
    for name, fn in phases:
        fn()
        if name == "p1t1":
            trig_stack.close()
        if name == "p2":
            p12_stack.close()
        if last_phase is not None and name == last_phase:
            break
    finale()
    return nc


def _host_prep(inputs):
    x = np.asarray(inputs["L0_attn_hidden_states"])
    w_qa = np.ascontiguousarray(np.asarray(inputs["L0_attn_w_qa"]))
    g_qa = np.ascontiguousarray(np.asarray(inputs["L0_attn_g_qa"]))
    w_qb = np.asarray(inputs["L0_attn_w_qb"])
    w_kva = np.ascontiguousarray(np.asarray(inputs["L0_attn_w_kva"]))
    g_kva = np.ascontiguousarray(np.asarray(inputs["L0_attn_g_kva"]))
    w_kvb = np.asarray(inputs["L0_attn_w_kvb"])
    w_o = np.ascontiguousarray(np.asarray(inputs["L0_attn_w_o"]))
    qw = np.asarray(inputs["L0_ffn_qweight"])
    scales = np.asarray(inputs["L0_ffn_scales"]).reshape(H, 16)
    zeros = np.asarray(inputs["L0_ffn_zeros"]).reshape(H, 16)
    bias = np.ascontiguousarray(np.asarray(inputs["L0_ffn_bias"]))

    perm_q = np.empty(NH * QK, np.int64)
    for h in range(NH):
        perm_q[h * ND:(h + 1) * ND] = h * QK + np.arange(ND)
        perm_q[NH * ND + h * RD: NH * ND + (h + 1) * RD] = h * QK + ND + np.arange(RD)
    w_qb_r = np.ascontiguousarray(w_qb[:, perm_q])

    perm_k = np.empty(NH * (ND + VD), np.int64)
    for h in range(NH):
        perm_k[h * ND:(h + 1) * ND] = h * (ND + VD) + np.arange(ND)
        perm_k[NH * ND + h * VD: NH * ND + (h + 1) * VD] = h * (ND + VD) + ND + np.arange(VD)
    w_kvb_r = np.ascontiguousarray(w_kvb[:, perm_k])

    assert int(qw.min()) >= 0 and int(qw.max()) <= 255
    qwT = np.ascontiguousarray(qw.T).astype(np.uint8)
    sbcv = np.ascontiguousarray(scales.T).astype(np.float32)
    zbcv = np.ascontiguousarray(zeros.T).astype(np.float32)

    R64 = np.zeros((64, 64), np.float32)
    for j in range(32):
        R64[j, 32 + j] = -1.0
        R64[32 + j, j] = 1.0
    rotk = np.ascontiguousarray(R64.T)
    R128 = np.zeros((128, 128), np.float32)
    R128[0:64, 0:64] = R64
    R128[64:128, 64:128] = R64
    rotq = np.ascontiguousarray(R128.T)

    shared = dict(w_qa=w_qa, g_qa=g_qa, w_qb=w_qb_r, w_kva=w_kva, g_kva=g_kva,
                  w_kvb=w_kvb_r, w_o=w_o, qwT=qwT, sbc=sbcv, zbc=zbcv, bias=bias,
                  rotq=rotq, rotk=rotk)
    in_maps = []
    for b in range(x.shape[0]):
        m = dict(shared)
        m["xT"] = np.ascontiguousarray(x[b].T)
        in_maps.append(m)
    return in_maps


def kernel(**inputs):
    in_maps = _host_prep(inputs)
    nc = build_nc(tuple(DEBUG_TAPS), DEBUG_LAST_PHASE)
    res = run_bass_kernel_spmd(nc, in_maps, core_ids=list(range(len(in_maps))))
    out = np.stack([np.asarray(r["outT"]).T for r in res.results], axis=0)
    return out.astype(np.float32)


# version 10 (final)
# speedup vs baseline: 1.4964x; 1.1326x over previous
import math
from contextlib import ExitStack

import numpy as np

import concourse.bass as bass
import concourse.mybir as mybir
from concourse.bass_utils import run_bass_kernel_spmd

F32 = mybir.dt.float32
BF16 = mybir.dt.bfloat16
I32 = mybir.dt.int32
AF = mybir.ActivationFunctionType
ALU = mybir.AluOpType
AX = mybir.AxisListType

S = 2048
TH = 1024
H = 4096
NH = 32
QR = 1536
KVR = 512
RD = 64
ND = 128
VD = 128
QK = ND + RD
SCALE = QK ** -0.5
NKC = H // 128
EPS_RMS = 1e-6
EPS_LN = 1e-5
TWO_PI = 2.0 * math.pi

ENGINES = ("sync", "gpsimd", "tensor", "vector", "scalar")

DEBUG_TAPS = []
DEBUG_LAST_PHASE = None
DEBUG_SUB = None


def ts2(nt):
    return slice(nt * 512, (nt + 1) * 512)


class Rec:

    def __init__(self, nc, stack):
        self.nc = nc
        self.ops = {e: [] for e in ENGINES}
        self.sems = {}
        self.cnt = {}
        self.wait_cache = {e: {} for e in ENGINES}
        for s in ("s_sms", "s_smg", "s_abf", "s_abfs", "s_strA", "s_strB", "s_strC", "s_xa", "s_xb",
                  "s_h0", "s_h1", "s_wA", "s_wB", "s_resA", "s_resB",
                  "s_dq", "s_wd", "s_oA", "s_oB", "s_oC", "s_oD",
                  "s_ca", "s_cb", "s_cc", "s_cd", "s_ra", "s_rb", "s_rc", "s_rd",
                  "s_pe", "s_dve", "s_act", "s_gp"):
            self.sems[s] = stack.enter_context(nc.semaphore(s))
            self.cnt[s] = 0

    def op(self, eng, fn, waits=(), inc=None, inc_by=1):
        w = tuple((s, v) for t in waits if t is not None for (s, v) in (t,) if v > 0)
        self.ops[eng].append((w, fn, inc, inc_by))
        if inc is not None:
            self.cnt[inc] += inc_by
            return (inc, self.cnt[inc])
        return None

    def dma(self, eng, out, in_, waits=(), sem=None):
        assert sem is not None, "explicit dma semaphore family required"
        return self.op(eng, lambda E, o=out, i=in_: E.dma_start(out=o, in_=i),
                       waits, inc=sem, inc_by=16)

    def fence(self, sems=("s_pe", "s_dve", "s_act")):
        return [(s, self.cnt[s]) for s in sems]

    def flush(self):
        nc = self.nc
        todo = {e: self.ops[e] for e in ENGINES if self.ops[e]}
        if not todo:
            return
        with nc.Block() as blk:
            for engname, oplist in todo.items():
                def make_cb(oplist=oplist, engname=engname):
                    def cb(E):
                        cache = self.wait_cache[engname]
                        for waits, fn, inc, n in oplist:
                            for sname, val in waits:
                                if val > cache.get(sname, 0):
                                    E.wait_ge(self.sems[sname], val)
                                    cache[sname] = val
                            inst = fn(E)
                            if inc is not None:
                                inst.then_inc(self.sems[inc], n)
                    return cb
                getattr(blk, engname)(make_cb())
        self.ops = {e: [] for e in ENGINES}


def build_nc(debug_taps=(), last_phase=None):
    nc = bass.Bass("TRN2", target_bir_lowering=False, detect_race_conditions=False)

    xT = nc.dram_tensor("xT", [H, S], F32, kind="ExternalInput").ap()
    w_qa = nc.dram_tensor("w_qa", [H, QR], F32, kind="ExternalInput").ap()
    g_qa = nc.dram_tensor("g_qa", [QR], F32, kind="ExternalInput").ap()
    w_qb = nc.dram_tensor("w_qb", [QR, NH * QK], F32, kind="ExternalInput").ap()
    w_kva = nc.dram_tensor("w_kva", [H, KVR + RD], F32, kind="ExternalInput").ap()
    g_kva = nc.dram_tensor("g_kva", [KVR], F32, kind="ExternalInput").ap()
    w_kvb = nc.dram_tensor("w_kvb", [KVR, NH * (ND + VD)], F32, kind="ExternalInput").ap()
    w_o = nc.dram_tensor("w_o", [NH * VD, H], F32, kind="ExternalInput").ap()
    qwT = nc.dram_tensor("qwT", [H, H], mybir.dt.uint8, kind="ExternalInput").ap()
    sbc = nc.dram_tensor("sbc", [16, H], F32, kind="ExternalInput").ap()
    zbc = nc.dram_tensor("zbc", [16, H], F32, kind="ExternalInput").ap()
    bias_d = nc.dram_tensor("bias", [H], F32, kind="ExternalInput").ap()
    rotq_d = nc.dram_tensor("rotq", [128, 128], F32, kind="ExternalInput").ap()
    rotk_d = nc.dram_tensor("rotk", [64, 64], F32, kind="ExternalInput").ap()
    outT = nc.dram_tensor("outT", [H, S], F32, kind="ExternalOutput").ap()

    QN = nc.dram_tensor("QN_sc", [NH * ND, S], BF16).ap()
    QRS = nc.dram_tensor("QR_sc", [NH * RD, S], BF16).ap()
    KN = nc.dram_tensor("KN_sc", [NH * ND, S], BF16).ap()
    VSC = nc.dram_tensor("V_sc", [S, NH * VD], BF16).ap()
    ASC = nc.dram_tensor("A_sc", [NH * VD, S], BF16).ap()
    R1 = nc.dram_tensor("R1_sc", [H, S], F32).ap()
    WD = nc.dram_tensor("WD_sc", [H, H], BF16).ap()
    R2 = nc.dram_tensor("R2_sc", [H, S], BF16).ap()
    BOUNCE = nc.dram_tensor("BN_sc", [8192], F32).ap()
    scratch = {"QN_sc": QN, "QR_sc": QRS, "KN_sc": KN, "V_sc": VSC, "A_sc": ASC,
               "R1_sc": R1, "WD_sc": WD, "R2_sc": R2}

    dbg = {}
    for name in debug_taps:
        src = scratch[name]
        dbg[name] = nc.dram_tensor("dbg_" + name, list(src.shape), src.dtype,
                                   kind="ExternalOutput").ap()

    stack = ExitStack()
    R = Rec(nc, stack)

    def sb(name, shape, dt):
        return stack.enter_context(nc.sbuf_tensor(name, shape, dt)).ap()

    ABF = sb("ABF", [128, 32 * TH], BF16)
    strA = sb("strA", [128, 32 * 128], BF16)
    strB = sb("strB", [128, 32 * 128], BF16)
    evA = sb("evA", [128, TH], F32)
    evB = sb("evB", [128, TH], F32)
    evC = sb("evC", [128, TH], F32)
    sqA = sb("sqA", [128, TH], BF16)
    sqB = sb("sqB", [128, TH], BF16)
    stgA = sb("stgA", [128, TH], BF16)
    stgB = sb("stgB", [128, TH], BF16)
    xfA = sb("xfA", [128, TH], F32)
    xfB = sb("xfB", [128, TH], F32)
    rowsA = sb("rowsA", [1, 2 * TH], F32)
    ones1f = sb("ones1f", [1, 128], F32)
    onesCb = sb("onesCb", [128, 1], BF16)
    ones128b = sb("ones128b", [128, 128], BF16)
    gqa_c = sb("gqa_c", [128, 12], F32)
    gkva_c = sb("gkva_c", [128, 4], F32)
    bias_c = sb("bias_c", [128, 32], F32)
    rotqb = sb("rotqb", [128, 128], BF16)
    rotkb = sb("rotkb", [64, 64], BF16)
    rkcols = sb("rkcols", [128, 16], F32)
    nmhat = sb("nmhat", [128, 32], F32)
    mrA = sb("mrA", [1, 32], F32)
    mrB = sb("mrB", [1, 32], F32)
    mrC = sb("mrC", [1, 32], F32)
    constT = sb("constT", [128, 8], F32)
    _consts = {}

    def creg(val):
        if val not in _consts:
            idx = len(_consts)
            ap = constT[:, idx:idx + 1]
            _consts[val] = ap
            nc.const_aps.register(F32, val, ap)
        return _consts[val]

    for v in (math.pi / 2.0, EPS_RMS, EPS_LN):
        creg(v)

    p12_stack = ExitStack()
    def sb12(name, shape, dt):
        return p12_stack.enter_context(nc.sbuf_tensor(name, shape, dt)).ap()
    krop = sb12("krop", [128, S], BF16)
    selN = sb12("selN", [128, 32 * 32], BF16)
    selR = sb12("selR", [128, 16 * 32], BF16)
    qn2max = sb12("qn2max", [32, 1], F32)
    kn2max = sb12("kn2max", [32, 1], F32)
    kr2max = sb12("kr2max", [1, 1], F32)
    n2tmp = sb12("n2tmp", [32, 1], F32)

    trig_stack = ExitStack()
    cosd = trig_stack.enter_context(nc.sbuf_tensor("cosd", [128, S], F32)).ap()
    sind = trig_stack.enter_context(nc.sbuf_tensor("sind", [128, S], F32)).ap()

    ROW = {"r0": rowsA[:, 0:TH], "r1": rowsA[:, TH:2 * TH],
           "t0": xfA[0:1, :], "t1": xfB[0:1, :]}

    def gemm(n_m, n_k, stripe_dma, rhs_of, ps_pair, epilogue,
             rhs_wait_of=None, mwid_of=None, extra_first_waits=(), stripes=None,
             tail_delay=2):
        if stripes is None:
            stripes = (strA, strB)
        nstr = len(stripes)
        stripe_tick = [None] * nstr
        ep_free = [(), ()]
        pe_tick = [None] * n_m
        pe_tails = [None] * n_m
        init_w = [("s_pe", R.cnt["s_pe"])]
        for m in range(min(nstr, n_m)):
            stripe_tick[m % nstr] = stripe_dma(m, stripes[m % nstr], m % nstr,
                                               extra_waits=init_w)
        for m in range(n_m):
            par = m % 2
            spar = m % nstr
            ps = ps_pair[par]
            buf = stripes[spar]
            mwid = 128 if mwid_of is None else mwid_of(m)
            if m >= tail_delay and pe_tails[m - tail_delay] is not None:
                pe_tails[m - tail_delay]()
            first = True
            last_t = None
            for k in range(n_k):
                for nt in range(2):
                    w = []
                    if first:
                        w = [stripe_tick[spar]] + list(ep_free[par]) + list(extra_first_waits)
                        first = False
                    if nt == 0 and rhs_wait_of is not None:
                        w.append(rhs_wait_of(k))
                    last_t = R.op("tensor", lambda E, ps=ps, buf=buf, k=k, nt=nt,
                                  mwid=mwid, n_k=n_k: E.matmul(
                                      ps[0:mwid, ts2(nt)],
                                      buf[:, k * mwid:(k + 1) * mwid],
                                      rhs_of(k, nt),
                                      start=(k == 0), stop=(k == n_k - 1)),
                                  waits=w,
                                  inc=("s_pe" if (k == n_k - 1 and nt == 1) else None))
            pe_tick[m] = last_t
            if m + nstr < n_m:
                stripe_tick[spar] = stripe_dma(m + nstr, stripes[spar], spar,
                                               extra_waits=[pe_tick[m]])
            free_waits, tail = epilogue(m, ps, pe_tick[m])
            ep_free[par] = free_waits
            pe_tails[m] = tail
        for mm in range(max(0, n_m - tail_delay), n_m):
            if pe_tails[mm] is not None:
                pe_tails[mm]()
        return pe_tick

    STR_SEMS = ("s_strA", "s_strB", "s_strC")

    def mk_stripe(wap, n_k, eng="gpsimd", mwid=128, kpar=128, static_waits=None):
        def f(m, buf, slot, extra_waits=()):
            w = list(extra_waits) + (static_waits() if static_waits else [])
            return R.dma(eng,
                         buf[:, 0:n_k * mwid].rearrange("p (k m) -> p k m", k=n_k),
                         wap[:, m * mwid:(m + 1) * mwid]
                         .rearrange("(k p) m -> p k m", p=kpar),
                         waits=w, sem=STR_SEMS[slot])
        return f

    def phase0():
        with (nc.sbuf_tensor("trig_u", [64, S], F32) as h1,
              nc.sbuf_tensor("trig_i", [64, S], I32) as h2,
              nc.sbuf_tensor("trig_y", [64, S], F32) as h3,
              nc.sbuf_tensor("trig_t", [64, S], F32) as h3b,
              nc.sbuf_tensor("ifrow", [1, 64], F32) as h4,
              nc.sbuf_tensor("itrow", [1, S], F32) as h5,
              nc.psum_tensor("ps_ang", [64, S], F32) as h6):
            trig_u, trig_i, trig_y = h1.ap(), h2.ap(), h3.ap()
            trig_t = h3b.ap()
            ifrow, itrow, ps_ang = h4.ap(), h5.ap(), h6.ap()

            for val, ap in _consts.items():
                R.op("vector", lambda E, ap=ap, val=val: E.memset(ap, val))
            R.op("vector", lambda E: E.memset(ones1f, 1.0))
            R.op("vector", lambda E: E.memset(onesCb, 1.0))
            R.op("vector", lambda E: E.memset(ones128b, 1.0))
            R.op("vector", lambda E: E.memset(qn2max, 0.0))
            R.op("vector", lambda E: E.memset(kn2max, 0.0))
            R.op("vector", lambda E: E.memset(kr2max, 0.0))
            R.op("vector", lambda E: E.memset(krop, 0.0))
            R.op("vector", lambda E: E.memset(selN, 0.0))
            R.op("vector", lambda E: E.memset(selR, 0.0))
            for h in range(32):
                R.op("vector", lambda E, h=h: E.memset(
                    selN[:, h * 32 + h:h * 32 + h + 1], 1.0))
            for j in range(16):
                R.op("vector", lambda E, j=j: E.memset(
                    selR[0:64, j * 32 + 2 * j:j * 32 + 2 * j + 1], 1.0))
                R.op("vector", lambda E, j=j: E.memset(
                    selR[64:128, j * 32 + 2 * j + 1:j * 32 + 2 * j + 2], 1.0))

            for c in range(12):
                R.dma("sync", gqa_c[:, c:c + 1],
                      g_qa[c * 128:(c + 1) * 128].rearrange("(p o) -> p o", o=1), sem="s_sms")
            for c in range(4):
                R.dma("sync", gkva_c[:, c:c + 1],
                      g_kva[c * 128:(c + 1) * 128].rearrange("(p o) -> p o", o=1), sem="s_sms")
            for c in range(32):
                R.dma("sync", bias_c[:, c:c + 1],
                      bias_d[c * 128:(c + 1) * 128].rearrange("(p o) -> p o", o=1), sem="s_sms")
            R.dma("gpsimd", rotqb, rotq_d, sem="s_smg")
            R.dma("gpsimd", rotkb, rotk_d, sem="s_smg")

            ti1 = R.op("gpsimd", lambda E: E.iota(
                trig_y[0:1, 0:32], [[1, 32]], base=0, channel_multiplier=0,
                allow_small_or_imprecise_dtypes=True), inc="s_gp")
            ti2 = R.op("gpsimd", lambda E: E.iota(
                itrow, [[1, S]], base=0, channel_multiplier=0,
                allow_small_or_imprecise_dtypes=True), inc="s_gp")
            a1 = R.op("scalar", lambda E: E.activation(
                ifrow[:, 0:32], trig_y[0:1, 0:32], AF.Exp,
                scale=-math.log(10000.0) / 32.0), waits=[ti1], inc="s_act")
            d0 = R.op("vector", lambda E: E.tensor_copy(ifrow[:, 32:64], ifrow[:, 0:32]),
                      waits=[a1], inc="s_dve")
            p_ang = None
            for n in range(4):
                p_ang = R.op("tensor", lambda E, n=n: E.matmul(
                    ps_ang[:, ts2(n)], ifrow[0:1, :], itrow[0:1, ts2(n)],
                    start=True, stop=True),
                    waits=[d0, ti2], inc=("s_pe" if n == 3 else None))
            def trig_table(out_ap, off, shift, w0):
                R.op("vector", lambda E: E.tensor_scalar(
                    trig_u, ps_ang, 1.0 / TWO_PI, off, op0=ALU.mult,
                    op1=(ALU.add if off else ALU.bypass)), waits=[w0])
                R.op("vector", lambda E: E.tensor_copy(trig_i, trig_u))
                R.op("vector", lambda E: E.tensor_copy(trig_u, trig_i))
                R.op("vector", lambda E: E.scalar_tensor_tensor(
                    trig_y, trig_u, TWO_PI, ps_ang, op0=ALU.mult, op1=ALU.subtract))
                if shift:
                    R.op("vector", lambda E: E.tensor_scalar(
                        trig_y, trig_y, shift, None, op0=ALU.subtract))
                R.op("vector", lambda E: E.tensor_scalar(
                    trig_u, trig_y, -math.pi, None, op0=ALU.is_lt))
                R.op("vector", lambda E: E.tensor_scalar(
                    trig_t, trig_y, math.pi, None, op0=ALU.is_gt))
                R.op("vector", lambda E: E.tensor_tensor(
                    trig_u, trig_u, trig_t, op=ALU.subtract))
                dd = R.op("vector", lambda E: E.scalar_tensor_tensor(
                    trig_t, trig_u, TWO_PI, trig_y, op0=ALU.mult, op1=ALU.add),
                    inc="s_dve")
                return R.op("scalar", lambda E, out_ap=out_ap: E.activation(
                    out_ap, trig_t, AF.Sin, scale=-1.0),
                    waits=[dd], inc="s_act")

            a3 = trig_table(sind[0:64, :], None, 0.0, p_ang)
            a4 = trig_table(cosd[0:64, :], 0.25, math.pi / 2.0, a3)
            R.dma("gpsimd", sind[64:128, :], sind[0:64, :], waits=[a4], sem="s_smg")
            R.dma("gpsimd", cosd[64:128, :], cosd[0:64, :], waits=[a4], sem="s_smg")
            R.flush()

    PH = {"stg_tick": {0: None, 1: None}}

    def load_xbf(th, w0):
        thc = slice(th * TH, (th + 1) * TH)
        xt = []
        for q in range(4):
            xt.append(R.dma(
                "gpsimd",
                ABF[:, q * 8 * TH:(q + 1) * 8 * TH].rearrange("p (k t) -> p k t", k=8),
                xT[q * 1024:(q + 1) * 1024, thc].rearrange("(k p) t -> p k t", p=128),
                waits=w0 if q == 0 else (), sem="s_abf"))
        return xt

    def phase1(th):
        thc = slice(th * TH, (th + 1) * TH)
        with ExitStack() as pst:
            qag = pst.enter_context(nc.sbuf_tensor(f"qag{th}", [128, 12 * TH], BF16)).ap()
            kcg = pst.enter_context(nc.sbuf_tensor(f"kcg{th}", [128, 4 * TH], BF16)).ap()
            rqbc = pst.enter_context(nc.sbuf_tensor(f"rqbc{th}", [128, TH], F32)).ap()
            rkbc = pst.enter_context(nc.sbuf_tensor(f"rkbc{th}", [128, TH], F32)).ap()
            krbf = pst.enter_context(nc.sbuf_tensor(f"krbf{th}", [64, TH], BF16)).ap()
            qrbuf = pst.enter_context(nc.sbuf_tensor(f"qrbuf{th}", [128, 2 * TH], BF16)).ap()
            strC = pst.enter_context(nc.sbuf_tensor(f"strC{th}", [128, 32 * 128], BF16)).ap()
            str3 = (strA, strB, strC)

            with (nc.psum_tensor(f"ps_g0_{th}", [128, TH], F32) as hg0,
                  nc.psum_tensor(f"ps_g1_{th}", [128, TH], F32) as hg1,
                  nc.psum_tensor(f"ps_qs{th}", [1, TH], F32) as hqs,
                  nc.psum_tensor(f"ps_ks{th}", [1, TH], F32) as hks):
                ps_g = (hg0.ap(), hg1.ap())
                ps_qs, ps_ks = hqs.ap(), hks.ap()

                if th == 0:
                    xt = load_xbf(0, R.fence())
                else:
                    xt = PH["xt1"]

                def xbf_of(k, nt):
                    return ABF[:, k * TH + nt * 512: k * TH + nt * 512 + 512]

                if DEBUG_SUB == "x":
                    R.flush()
                    return

                st_q = {"first": True}
                guard = {0: None, 1: None}

                def epi_qa_mm(m, ps, pe_w):
                    return [pe_w], None

                def epi_qa_act(m, ps, pe_w):
                    sq = (sqA, sqB)[m % 2]
                    a = R.op("scalar", lambda E, ps=ps, sq=sq: E.activation(
                        sq, ps, AF.Square), waits=[pe_w], inc="s_act")
                    return [a], None

                def epi_qa_dve(m, ps, pe_w):
                    d = R.op("vector", lambda E, ps=ps, m=m: E.tensor_scalar(
                        qag[:, m * TH:(m + 1) * TH], ps, gqa_c[:, m:m + 1], None,
                        op0=ALU.mult), waits=[pe_w], inc="s_dve")
                    return [d], None

                def epi_qa(m, ps, pe_w):
                    sq = (sqA, sqB)[m % 2]
                    a = R.op("scalar", lambda E, ps=ps, sq=sq: E.activation(
                        sq, ps, AF.Square), waits=[pe_w, guard[m % 2]], inc="s_act")
                    d = R.op("vector", lambda E, ps=ps, m=m: E.tensor_scalar(
                        qag[:, m * TH:(m + 1) * TH], ps, gqa_c[:, m:m + 1], None,
                        op0=ALU.mult), waits=[pe_w, a], inc="s_dve")

                    def tail(m=m, sq=sq, a=a):
                        st = st_q["first"]
                        st_q["first"] = False
                        for nt in range(2):
                            t = R.op("tensor", lambda E, sq=sq, nt=nt, st=st, m=m: E.matmul(
                                ps_qs[:, ts2(nt)], onesCb, sq[:, ts2(nt)],
                                start=st, stop=(m == 11)),
                                waits=[a], inc="s_pe")
                            if nt == 1:
                                guard[m % 2] = t
                    return [a, d], tail

                epi_sel = {"qamm": epi_qa_mm, "qaact": epi_qa_act,
                           "qadve": epi_qa_dve}.get(DEBUG_SUB, epi_qa)
                gemm(12, NKC, mk_stripe(w_qa, NKC), xbf_of, ps_g, epi_sel,
                     rhs_wait_of=lambda k: xt[k // 8],
                     extra_first_waits=R.fence(("s_sms", "s_smg")), stripes=str3)
                if DEBUG_SUB in ("qa", "qamm", "qaact", "qadve"):
                    R.flush()
                    return

                def kva_stripe(m, buf, slot, extra_waits=()):
                    sem = STR_SEMS[slot]
                    if m < 4:
                        return R.dma("gpsimd",
                                     buf[:, 0:NKC * 128].rearrange("p (k m) -> p k m", k=NKC),
                                     w_kva[:, m * 128:(m + 1) * 128]
                                     .rearrange("(k p) m -> p k m", p=128),
                                     waits=list(extra_waits), sem=sem)
                    return R.dma("gpsimd",
                                 buf[:, 0:NKC * 64].rearrange("p (k m) -> p k m", k=NKC),
                                 w_kva[:, 512:576].rearrange("(k p) m -> p k m", p=128),
                                 waits=list(extra_waits), sem=sem)

                st_k = {"first": True}
                krope_tick = [None]

                def epi_kva(m, ps, pe_w):
                    if m < 4:
                        sq = (sqA, sqB)[m % 2]
                        a = R.op("scalar", lambda E, ps=ps, sq=sq: E.activation(
                            sq, ps, AF.Square), waits=[pe_w, guard[m % 2]], inc="s_act")
                        d = R.op("vector", lambda E, ps=ps, m=m: E.tensor_scalar(
                            kcg[:, m * TH:(m + 1) * TH], ps, gkva_c[:, m:m + 1], None,
                            op0=ALU.mult), waits=[pe_w, a], inc="s_dve")

                        def tail(m=m, sq=sq, a=a):
                            st = st_k["first"]
                            st_k["first"] = False
                            for nt in range(2):
                                t = R.op("tensor", lambda E, sq=sq, nt=nt, st=st, m=m: E.matmul(
                                    ps_ks[:, ts2(nt)], onesCb, sq[:, ts2(nt)],
                                    start=st, stop=(m == 3)),
                                    waits=[a], inc="s_pe")
                                if nt == 1:
                                    guard[m % 2] = t
                        return [a, d], tail

                    a = R.op("scalar", lambda E, ps=ps: E.activation(
                        krbf, ps[0:64, :], AF.Copy), waits=[pe_w], inc="s_act")
                    other = ps_g[(m + 1) % 2]
                    pr = None
                    for nt in range(2):
                        pr = R.op("tensor", lambda E, other=other, nt=nt: E.matmul(
                            other[0:64, ts2(nt)], rotkb, krbf[:, ts2(nt)],
                            start=True, stop=True),
                            waits=[a] + R.fence(("s_act", "s_dve")),
                            inc=("s_pe" if nt == 1 else None))
                    R.op("vector", lambda E: E.tensor_tensor(
                        evA[0:64, :], krbf, cosd[0:64, thc], op=ALU.mult), waits=[a])
                    R.op("vector", lambda E, other=other: E.tensor_tensor(
                        evB[0:64, :], other[0:64, :], sind[0:64, thc], op=ALU.mult),
                        waits=[pr])
                    d3 = R.op("vector", lambda E: E.tensor_tensor(
                        krop[0:64, thc], evA[0:64, :], evB[0:64, :], op=ALU.add),
                        inc="s_dve")
                    krope_tick[0] = d3
                    return [d3, pr], None

                kva_pe = gemm(5, NKC, kva_stripe, xbf_of, ps_g, epi_kva,
                              rhs_wait_of=lambda k: xt[k // 8],
                              mwid_of=lambda m: 128 if m < 4 else 64, stripes=str3)
                if th == 0:
                    PH["xt1"] = load_xbf(1, [kva_pe[4]])
                if DEBUG_SUB == "kva":
                    R.flush()
                    return

                def recip_chain(ps_row, nfeat, out_row, w0):
                    R.op("scalar", lambda E, ps_row=ps_row, nfeat=nfeat: E.activation(
                        ROW["t0"], ps_row, AF.Ln, scale=1.0 / nfeat),
                        waits=[w0], inc="s_act")
                    R.op("scalar", lambda E: E.activation(
                        ROW["t1"], ROW["t0"], AF.Exp, scale=0.5), inc="s_act")
                    R.op("scalar", lambda E: E.activation(
                        ROW["t0"], ROW["t1"], AF.Ln, bias=EPS_RMS), inc="s_act")
                    return R.op("scalar", lambda E, o=out_row: E.activation(
                        o, ROW["t0"], AF.Exp, scale=-1.0), inc="s_act")

                aq = recip_chain(ps_qs, QR, ROW["r0"], ("s_pe", R.cnt["s_pe"]))
                ak = recip_chain(ps_ks, KVR, ROW["r1"], aq)
                for (row, psb, out_bc, aw) in ((ROW["r0"], ps_g[0], rqbc, aq),
                                               (ROW["r1"], ps_g[1], rkbc, ak)):
                    lastb = None
                    for nt in range(2):
                        lastb = R.op("tensor", lambda E, psb=psb, row=row, nt=nt: E.matmul(
                            psb[:, ts2(nt)], ones1f, row[:, ts2(nt)],
                            start=True, stop=True),
                            waits=[aw] + R.fence(("s_dve", "s_act")),
                            inc=("s_pe" if nt == 1 else None))
                    R.op("vector", lambda E, o=out_bc, psb=psb: E.tensor_copy(o, psb),
                         waits=[lastb], inc="s_dve")

                trow = R.dma("gpsimd",
                             BOUNCE[th * TH:(th + 1) * TH].rearrange("(o t) -> o t", o=1),
                             ROW["r1"], waits=[ak], sem="s_smg")
                rk_tick = None
                for tc in range(8):
                    rk_tick = R.dma("gpsimd", rkcols[:, th * 8 + tc:th * 8 + tc + 1],
                                    BOUNCE[th * TH + tc * 128: th * TH + (tc + 1) * 128]
                                    .rearrange("(p o) -> p o", o=1),
                                    waits=[trow], sem="s_smg")
                phase1.rk_tick = rk_tick

                dsq = R.op("vector", lambda E: E.tensor_tensor(
                    sqA[0:64, :], krop[0:64, thc], krop[0:64, thc], op=ALU.mult),
                    waits=[krope_tick[0]], inc="s_dve")
                lastm = None
                for nt in range(2):
                    lastm = R.op("tensor", lambda E, nt=nt: E.matmul(
                        ps_ks[:, ts2(nt)], onesCb[0:64, :], sqA[0:64, ts2(nt)],
                        start=True, stop=True),
                        waits=[dsq, ak], inc=("s_pe" if nt == 1 else None))
                R.op("vector", lambda E: E.tensor_reduce(
                    ROW["t0"][:, 0:1], ps_ks, axis=AX.X, op=ALU.max), waits=[lastm])
                R.op("vector", lambda E: E.tensor_tensor(
                    kr2max, kr2max, ROW["t0"][:, 0:1], op=ALU.max), inc="s_dve")
                R.flush()
            if DEBUG_SUB == "1a":
                return

            with (nc.psum_tensor(f"ps_g0b{th}", [128, TH], F32) as hg0,
                  nc.psum_tensor(f"ps_g1b{th}", [128, TH], F32) as hg1,
                  nc.psum_tensor(f"ps_R{th}", [128, TH], F32) as hR,
                  nc.psum_tensor(f"ps_n2{th}", [32, TH], F32) as hn2):
                ps_g = (hg0.ap(), hg1.ap())
                ps_R, ps_n2 = hR.ap(), hn2.ap()

                stg_tick = PH["stg_tick"]
                rope_state = {"prev_d2": None, "prev_pr": None}
                n2s = {"first": True, "final": None}
                guard = {0: None, 1: None}

                def epi_qb(m, ps, pe_w):
                    par = m % 2
                    stg = (stgA, stgB)[par]
                    sq = (sqA, sqB)[par]
                    if m < 32:
                        d1 = R.op("vector", lambda E, ps=ps, stg=stg: E.tensor_tensor(
                            stg, ps, rqbc, op=ALU.mult),
                            waits=[pe_w, stg_tick[par]], inc="s_dve")
                        stg_tick[par] = R.dma("sync", QN[m * 128:(m + 1) * 128, thc],
                                              stg, waits=[d1], sem=("s_wA", "s_wB")[par])
                        d2 = R.op("vector", lambda E, stg=stg, sq=sq: E.tensor_tensor(
                            sq, stg, stg, op=ALU.mult), waits=[guard[par]], inc="s_dve")

                        def tail(m=m, sq=sq, d2=d2, par=par):
                            st = n2s["first"]
                            n2s["first"] = False
                            for nt in range(2):
                                t = R.op("tensor", lambda E, sq=sq, nt=nt, st=st, m=m: E.matmul(
                                    ps_n2[:, ts2(nt)], selN[:, m * 32:(m + 1) * 32],
                                    sq[:, ts2(nt)], start=st, stop=(m == 47)),
                                    waits=[d2], inc="s_pe")
                                if nt == 1:
                                    guard[par] = t
                                if m == 47 and nt == 1:
                                    n2s["final"] = t
                        return [d1, d2], tail

                    j = m - 32
                    qrb = qrbuf[:, par * TH:(par + 1) * TH]
                    d0 = R.op("vector", lambda E, ps=ps, qrb=qrb: E.tensor_tensor(
                        qrb, ps, rqbc, op=ALU.mult),
                        waits=[pe_w, rope_state["prev_pr"]], inc="s_dve")
                    dsq = R.op("vector", lambda E, qrb=qrb, sq=sq: E.tensor_tensor(
                        sq, qrb, qrb, op=ALU.mult), waits=[guard[par]], inc="s_dve")

                    def tail(m=m, j=j, par=par, qrb=qrb, stg=stg, sq=sq, d0=d0, dsq=dsq):
                        pr = None
                        for nt in range(2):
                            pr = R.op("tensor", lambda E, qrb=qrb, nt=nt: E.matmul(
                                ps_R[:, ts2(nt)], rotqb, qrb[:, ts2(nt)],
                                start=True, stop=True),
                                waits=[d0, rope_state["prev_d2"]],
                                inc=("s_pe" if nt == 1 else None))
                        rope_state["prev_pr"] = pr
                        st = n2s["first"]
                        n2s["first"] = False
                        for nt in range(2):
                            t = R.op("tensor", lambda E, sq=sq, nt=nt, st=st, j=j, m=m: E.matmul(
                                ps_n2[:, ts2(nt)], selR[:, j * 32:(j + 1) * 32],
                                sq[:, ts2(nt)], start=st, stop=(m == 47)),
                                waits=[dsq], inc="s_pe")
                            if nt == 1:
                                guard[par] = t
                            if m == 47 and nt == 1:
                                n2s["final"] = t
                        dd1 = R.op("vector", lambda E, qrb=qrb: E.tensor_tensor(
                            evA, qrb, cosd[:, thc], op=ALU.mult), waits=[dsq])
                        dd2 = R.op("vector", lambda E: E.tensor_tensor(
                            evB, ps_R, sind[:, thc], op=ALU.mult), waits=[pr],
                            inc="s_dve")
                        rope_state["prev_d2"] = dd2
                        d3 = R.op("vector", lambda E, stg=stg: E.tensor_tensor(
                            stg, evA, evB, op=ALU.add),
                            waits=[stg_tick[par]], inc="s_dve")
                        stg_tick[par] = R.dma("sync", QRS[j * 64 * 2:(j + 1) * 64 * 2, thc],
                                              stg, waits=[d3], sem=("s_wA", "s_wB")[par])
                    return [d0], tail

                gemm(48, 12, mk_stripe(w_qb, 12),
                     lambda k, nt: qag[:, k * TH + nt * 512: k * TH + nt * 512 + 512],
                     ps_g, epi_qb, stripes=str3)

                R.op("vector", lambda E: E.tensor_reduce(
                    n2tmp, ps_n2, axis=AX.X, op=ALU.max), waits=[n2s["final"]])
                R.op("vector", lambda E: E.tensor_tensor(
                    qn2max, qn2max, n2tmp, op=ALU.max), inc="s_dve")
                R.flush()
            if DEBUG_SUB == "1b":
                return

            with (nc.psum_tensor(f"ps_g0c{th}", [128, TH], F32) as hg0,
                  nc.psum_tensor(f"ps_g1c{th}", [128, TH], F32) as hg1,
                  nc.psum_tensor(f"ps_n2c{th}", [32, TH], F32) as hn2):
                ps_g = (hg0.ap(), hg1.ap())
                ps_n2 = hn2.ap()
                stg_tick = PH["stg_tick"]
                n2s = {"first": True, "final": None}
                guard = {0: None, 1: None}

                def epi_kvb(m, ps, pe_w):
                    par = m % 2
                    stg = (stgA, stgB)[par]
                    sq = (sqA, sqB)[par]
                    d1 = R.op("vector", lambda E, ps=ps, stg=stg: E.tensor_tensor(
                        stg, ps, rkbc, op=ALU.mult),
                        waits=[pe_w, stg_tick[par]], inc="s_dve")
                    stg_tick[par] = R.dma("sync", KN[m * 128:(m + 1) * 128, thc],
                                          stg, waits=[d1], sem=("s_wA", "s_wB")[par])
                    d2 = R.op("vector", lambda E, stg=stg, sq=sq: E.tensor_tensor(
                        sq, stg, stg, op=ALU.mult), waits=[guard[par]], inc="s_dve")

                    def tail(m=m, sq=sq, d2=d2, par=par):
                        st = n2s["first"]
                        n2s["first"] = False
                        for nt in range(2):
                            t = R.op("tensor", lambda E, sq=sq, nt=nt, st=st, m=m: E.matmul(
                                ps_n2[:, ts2(nt)], selN[:, m * 32:(m + 1) * 32],
                                sq[:, ts2(nt)], start=st, stop=(m == 31)),
                                waits=[d2], inc="s_pe")
                            if nt == 1:
                                guard[par] = t
                            if m == 31 and nt == 1:
                                n2s["final"] = t
                    return [d1, d2], tail

                gemm(32, 4, mk_stripe(w_kvb, 4),
                     lambda k, nt: kcg[:, k * TH + nt * 512: k * TH + nt * 512 + 512],
                     ps_g, epi_kvb, stripes=str3)

                R.op("vector", lambda E: E.tensor_reduce(
                    n2tmp, ps_n2, axis=AX.X, op=ALU.max), waits=[n2s["final"]])
                R.op("vector", lambda E: E.tensor_tensor(
                    kn2max, kn2max, n2tmp, op=ALU.max), inc="s_dve")

                with nc.sbuf_tensor(f"wvb{th}", [128, 2 * 4 * 512], BF16) as hwv:
                    wvb = hwv.ap()
                    wv_tick = [None, None]
                    pe_done = [None, None]
                    dve_done = {0: None, 1: None}

                    def load_wv(vq, extra=()):
                        par = vq % 2
                        wv_tick[par] = R.dma(
                            "gpsimd",
                            wvb[:, par * 2048:(par + 1) * 2048]
                            .rearrange("p (k m) -> p k m", k=4),
                            w_kvb[:, 4096 + vq * 512: 4096 + (vq + 1) * 512]
                            .rearrange("(k p) m -> p k m", p=128),
                            waits=list(extra), sem=("s_strA", "s_strB")[par])

                    init_v = [("s_pe", R.cnt["s_pe"])]
                    load_wv(0, extra=init_v)
                    load_wv(1, extra=init_v)
                    for vq in range(8):
                        par = vq % 2
                        for tc in range(8):
                            pp = ps_g[tc % 2]
                            last = None
                            for kv in range(4):
                                w = []
                                if kv == 0:
                                    w = [wv_tick[par], dve_done[tc % 2]]
                                    if vq == 0 and tc < 2:
                                        w += R.fence(("s_dve", "s_act"))
                                last = R.op("tensor", lambda E, pp=pp, kv=kv, tc=tc,
                                            par=par: E.matmul(
                                                pp[:, 0:512],
                                                kcg[:, kv * TH + tc * 128: kv * TH + (tc + 1) * 128],
                                                wvb[:, par * 2048 + kv * 512: par * 2048 + (kv + 1) * 512],
                                                start=(kv == 0), stop=(kv == 3)),
                                            waits=w, inc=("s_pe" if kv == 3 else None))
                            pe_done[par] = last
                            stg = (stgA, stgB)[tc % 2]
                            dv = R.op("vector", lambda E, pp=pp, stg=stg, tc=tc: E.tensor_scalar(
                                stg[:, 0:512], pp[:, 0:512],
                                rkcols[:, th * 8 + tc:th * 8 + tc + 1], None, op0=ALU.mult),
                                waits=[last, stg_tick[tc % 2], phase1.rk_tick], inc="s_dve")
                            dve_done[tc % 2] = dv
                            stg_tick[tc % 2] = R.dma(
                                "sync",
                                VSC[(th * 8 + tc) * 128:(th * 8 + tc + 1) * 128,
                                    vq * 512:(vq + 1) * 512],
                                stg[:, 0:512], waits=[dv], sem=("s_wA", "s_wB")[tc % 2])
                        if vq + 2 < 8:
                            load_wv(vq + 2, extra=[pe_done[par]])
                R.flush()

    def phase2():
        with ExitStack() as pst:
            def psb(name, shape, dt):
                return pst.enter_context(nc.sbuf_tensor(name, shape, dt)).ap()
            qnb = psb("qnb", [128, 2 * S], BF16)
            qrb2 = psb("qrb2", [128, 2 * S], BF16)
            knb = psb("knb", [128, 2 * S], BF16)
            vsb = psb("vsb", [128, 2 * 16 * 128], BF16)
            ptb = psb("ptb", [128, 2 * TH], BF16)
            recl = psb("recl", [128, TH], F32)
            lnl = psb("lnl", [128, TH], F32)
            aostg = psb("aostg", [128, 2 * TH], BF16)
            qfb = psb("qfb", [128, TH], F32)
            R.op("vector", lambda E: E.memset(qrb2, 0.0))
            sbcb = psb("sbcb", [128, TH], F32)
            zbcb = psb("zbcb", [128, TH], F32)
            dqt = psb("dqt", [128, TH], F32)
            wdst = psb("wdst", [128, 2 * TH], BF16)

            with nc.psum_tensor("ps_m", [128, 32], F32) as hm:
                ps_m = hm.ap()
                t1 = R.dma("gpsimd", BOUNCE[4096:4128].rearrange("(p o) -> p o", o=1),
                           qn2max, waits=R.fence(("s_dve", "s_pe")), sem="s_smg")
                t2 = R.dma("gpsimd", BOUNCE[4128:4160].rearrange("(p o) -> p o", o=1),
                           kn2max, sem="s_smg")
                t3 = R.dma("gpsimd", mrA, BOUNCE[4096:4128].rearrange("(o t) -> o t", o=1),
                           waits=[t2], sem="s_smg")
                t4 = R.dma("gpsimd", mrB, BOUNCE[4128:4160].rearrange("(o t) -> o t", o=1),
                           waits=[t2], sem="s_smg")
                d1 = R.op("vector", lambda E: E.tensor_scalar(
                    mrC, mrB, kr2max[0:1, 0:1], None, op0=ALU.add),
                    waits=[t3, t4], inc="s_dve")
                R.op("scalar", lambda E: E.activation(mrB, mrC, AF.Sqrt),
                     waits=[d1], inc="s_act")
                a2 = R.op("scalar", lambda E: E.activation(mrC, mrA, AF.Sqrt),
                          inc="s_act")
                R.op("vector", lambda E: E.tensor_tensor(
                    mrA, mrB, mrC, op=ALU.mult), waits=[a2])
                R.op("vector", lambda E: E.tensor_scalar(
                    mrB, mrA, SCALE, 0.1, op0=ALU.mult, op1=ALU.add))
                d4 = R.op("vector", lambda E: E.tensor_scalar(
                    mrC, mrB, -1.0, None, op0=ALU.mult), inc="s_dve")
                pm = R.op("tensor", lambda E: E.matmul(
                    ps_m, ones1f, mrC, start=True, stop=True),
                    waits=[d4], inc="s_pe")
                nm_tick = R.op("vector", lambda E: E.tensor_copy(nmhat, ps_m),
                               waits=[pm], inc="s_dve")
                R.flush()

            with (nc.psum_tensor("ps_s0", [128, TH], F32) as hs0,
                  nc.psum_tensor("ps_s1", [128, TH], F32) as hs1,
                  nc.psum_tensor("ps_o", [128, TH], F32) as ho,
                  nc.psum_tensor("ps_l", [128, TH], F32) as hl):
                ps_s = (hs0.ap(), hs1.ap())
                ps_o, ps_l = ho.ap(), hl.ap()

                ld_tick = [None, None]
                head_pe_end = [None, None]
                ao_tick = {0: None, 1: None}
                ps_s_free = {0: nm_tick, 1: nm_tick}
                prev = {"ao_dve": None, "ln": None}
                dq = {"s": None, "z": None, "sub": None,
                      "mul": {0: None, 1: None}, "out": {0: None, 1: None}}

                def load_head(h):
                    par = h % 2
                    sem = ("s_h0", "s_h1")[par]
                    w = [head_pe_end[par]]
                    if h < 2:
                        w = w + R.fence(("s_wA", "s_wB"))
                    R.dma("sync", qnb[:, par * S:(par + 1) * S],
                          QN[h * 128:(h + 1) * 128, :], waits=w, sem=sem)
                    R.dma("sync", qrb2[0:64, par * S:(par + 1) * S],
                          QRS[h * 64:(h + 1) * 64, :], waits=w, sem=sem)
                    R.dma("sync", knb[:, par * S:(par + 1) * S],
                          KN[h * 128:(h + 1) * 128, :], waits=w, sem=sem)
                    ld_tick[par] = R.dma(
                        "sync",
                        vsb[:, par * 2048:(par + 1) * 2048]
                        .rearrange("p (c v) -> p c v", c=16),
                        VSC[:, h * 128:(h + 1) * 128]
                        .rearrange("(c p) v -> p c v", p=128), waits=w, sem=sem)

                def dequant_stripe(c):
                    gi = c // 2
                    for qv in range(4):
                        tq = R.dma("gpsimd", qfb,
                                   qwT[c * 128:(c + 1) * 128, qv * TH:(qv + 1) * TH],
                                   waits=[dq["sub"]], sem="s_dq")
                        dq["s"] = R.dma("gpsimd", sbcb,
                                        sbc[gi:gi + 1, qv * TH:(qv + 1) * TH]
                                        .partition_broadcast(128),
                                        waits=[dq["mul"][0], dq["mul"][1]], sem="s_dq")
                        dq["z"] = R.dma("gpsimd", zbcb,
                                        zbc[gi:gi + 1, qv * TH:(qv + 1) * TH]
                                        .partition_broadcast(128),
                                        waits=[dq["sub"]], sem="s_dq")
                        dsub = R.op("gpsimd", lambda E: E.tensor_tensor(
                            dqt, qfb, zbcb, op=ALU.subtract),
                            waits=[tq, dq["z"], dq["s"]], inc="s_gp")
                        dq["sub"] = dsub
                        par = qv % 2
                        dmul = R.op("gpsimd", lambda E, par=par: E.tensor_tensor(
                            wdst[:, par * TH:(par + 1) * TH], dqt, sbcb, op=ALU.mult),
                            waits=[dq["out"][par]], inc="s_gp")
                        dq["mul"][par] = dmul
                        dq["out"][par] = R.dma(
                            "gpsimd", WD[c * 128:(c + 1) * 128, qv * TH:(qv + 1) * TH],
                            wdst[:, par * TH:(par + 1) * TH], waits=[dmul], sem="s_wd")

                def emit_pv(h, qh, c, vs, exp_tick):
                    cp = c % 2
                    last = None
                    for nt in range(2):
                        w = []
                        if nt == 0:
                            w = [exp_tick[c]]
                            if c == 0:
                                w += [prev["ao_dve"], prev["ln"]]
                        R.op("tensor", lambda E, c=c, nt=nt, cp=cp, vs=vs: E.matmul(
                            ps_o[:, ts2(nt)], vs[:, c * 128:(c + 1) * 128],
                            ptb[:, cp * TH + nt * 512: cp * TH + nt * 512 + 512],
                            start=(c == 0), stop=(c == 15)), waits=w)
                        last = R.op("tensor", lambda E, c=c, nt=nt, cp=cp: E.matmul(
                            ps_l[:, ts2(nt)], ones128b,
                            ptb[:, cp * TH + nt * 512: cp * TH + nt * 512 + 512],
                            start=(c == 0), stop=(c == 15)),
                            inc=("s_pe" if (c == 15 and nt == 1) else None))
                    return last

                load_head(0)
                load_head(1)
                for h in range(NH):
                    par = h % 2
                    dequant_stripe(h)
                    qn = qnb[:, par * S:(par + 1) * S]
                    qr = qrb2[:, par * S:(par + 1) * S]
                    kn = knb[:, par * S:(par + 1) * S]
                    vs = vsb[:, par * 2048:(par + 1) * 2048]
                    last_pe = None
                    for qh in range(2):
                        exp_tick = [None] * 16
                        pvl_last = None
                        for c in range(16):
                            cp = c % 2
                            sc_last = None
                            for nt in range(2):
                                w = []
                                if nt == 0:
                                    w = [ld_tick[par], ps_s_free[cp]]
                                colsl = slice(qh * TH + nt * 512, qh * TH + nt * 512 + 512)
                                R.op("tensor", lambda E, cp=cp, c=c, kn=kn, qn=qn,
                                     colsl=colsl, nt=nt: E.matmul(
                                         ps_s[cp][:, ts2(nt)],
                                         kn[:, c * 128:(c + 1) * 128],
                                         qn[:, colsl], start=True, stop=False),
                                     waits=w)
                                sc_last = R.op("tensor", lambda E, cp=cp, c=c, qr=qr,
                                               colsl=colsl, nt=nt: E.matmul(
                                                   ps_s[cp][:, ts2(nt)],
                                                   krop[:, c * 128:(c + 1) * 128],
                                                   qr[:, colsl], start=False, stop=True),
                                               inc=("s_pe" if nt == 1 else None))
                            if c >= 1:
                                pvl_last = emit_pv(h, qh, c - 1, vs, exp_tick)
                            exp_tick[c] = R.op("scalar", lambda E, cp=cp, h=h: E.activation(
                                ptb[:, cp * TH:(cp + 1) * TH], ps_s[cp], AF.Exp,
                                bias=nmhat[:, h:h + 1], scale=SCALE),
                                waits=[sc_last], inc="s_act")
                            ps_s_free[cp] = exp_tick[c]
                        pvl_last = emit_pv(h, qh, 15, vs, exp_tick)
                        a1 = R.op("scalar", lambda E: E.activation(
                            lnl, ps_l, AF.Ln), waits=[pvl_last], inc="s_act")
                        a2 = R.op("scalar", lambda E: E.activation(
                            recl, lnl, AF.Exp, scale=-1.0), inc="s_act")
                        dv = R.op("vector", lambda E, qh=qh: E.tensor_tensor(
                            aostg[:, qh * TH:(qh + 1) * TH], ps_o, recl, op=ALU.mult),
                            waits=[a2, ao_tick[qh]], inc="s_dve")
                        ao_tick[qh] = R.dma(
                            "sync", ASC[h * 128:(h + 1) * 128, qh * TH:(qh + 1) * TH],
                            aostg[:, qh * TH:(qh + 1) * TH], waits=[dv],
                            sem=("s_wA", "s_wB")[qh])
                        prev["ao_dve"] = dv
                        prev["ln"] = a1
                        last_pe = pvl_last
                    head_pe_end[par] = last_pe
                    if h + 2 < NH:
                        load_head(h + 2)
                R.flush()

    def phase3():
        with ExitStack() as pst:
            def psb(name, shape, dt):
                return pst.enter_context(nc.sbuf_tensor(name, shape, dt)).ap()
            hg0 = pst.enter_context(nc.psum_tensor("ps_g0d", [128, TH], F32))
            hg1 = pst.enter_context(nc.psum_tensor("ps_g1d", [128, TH], F32))
            hsx = pst.enter_context(nc.psum_tensor("ps_sx", [1, TH], F32))
            hsq = pst.enter_context(nc.psum_tensor("ps_sq", [1, TH], F32))
            ps_g = (hg0.ap(), hg1.ap())
            ps_sx, ps_sq = hsx.ap(), hsq.ap()
            strC3 = psb("strC3", [128, 32 * 128], BF16)
            str3 = (strA, strB, strC3)
            cx4 = psb("cx4", [128, 2 * 4 * TH], F32)
            ox4 = psb("ox4", [128, 2 * 4 * TH], BF16)
            ouf = psb("ouf", [128, 2 * 2 * TH], F32)
            oev = evC
            mu1_bc = psb("mu1bc", [128, TH], F32)
            rs1_bc = psb("rs1bc", [128, TH], F32)
            mu2_bc = psb("mu2bc", [128, TH], F32)
            rs2_bc = psb("rs2bc", [128, TH], F32)

            xf_tick = [None, None]
            xf_free = [None, None]
            res_tick = [None, None]
            stg_free = [None, None]

            def ln_rows_and_bcast(final_tick, mu_bc, rstd_bc):
                R.op("vector", lambda E: E.tensor_scalar(
                    ROW["r0"], ps_sx, 1.0 / H, None, op0=ALU.mult), waits=[final_tick])
                R.op("vector", lambda E: E.tensor_scalar(
                    ROW["t1"], ps_sq, 1.0 / H, None, op0=ALU.mult))
                R.op("vector", lambda E: E.tensor_tensor(
                    ROW["t0"], ROW["r0"], ROW["r0"], op=ALU.mult))
                dv = R.op("vector", lambda E: E.tensor_tensor(
                    ROW["t1"], ROW["t1"], ROW["t0"], op=ALU.subtract), inc="s_dve")
                R.op("scalar", lambda E: E.activation(
                    ROW["t0"], ROW["t1"], AF.Ln, bias=EPS_LN), waits=[dv], inc="s_act")
                ar = R.op("scalar", lambda E: E.activation(
                    ROW["r1"], ROW["t0"], AF.Exp, scale=-0.5), inc="s_act")
                out = {}
                for (row, psb_, bc, aw) in ((ROW["r0"], ps_g[0], mu_bc, dv),
                                            (ROW["r1"], ps_g[1], rstd_bc, ar)):
                    lastb = None
                    for nt in range(2):
                        lastb = R.op("tensor", lambda E, psb_=psb_, row=row, nt=nt: E.matmul(
                            psb_[:, ts2(nt)], ones1f, row[:, ts2(nt)],
                            start=True, stop=True),
                            waits=[aw] + R.fence(("s_dve", "s_act")),
                            inc=("s_pe" if nt == 1 else None))
                    out[id(bc)] = R.op("vector", lambda E, bc=bc, psb_=psb_: E.tensor_copy(bc, psb_),
                                       waits=[lastb], inc="s_dve")
                return out[id(mu_bc)], out[id(rstd_bc)]

            def wo_fetch_x(m, thc):
                par = m % 2
                w = [xf_free[par]]
                xf_tick[par] = R.dma("sync", (xfA, xfB)[par],
                                     xT[m * 128:(m + 1) * 128, thc], waits=w,
                                     sem=("s_xa", "s_xb")[par])

            def run_wo(th, thc, at, prefetched=False):
                st = {"first": True, "final": None,
                      "tail_tick": [("s_pe", R.cnt["s_pe"])] * 2}

                def fetch_x(m):
                    wo_fetch_x(m, thc)

                if not prefetched:
                    fetch_x(0)
                    fetch_x(1)

                def epi(m, ps, pe_w):
                    par = m % 2
                    rr = (evA, evB)[par]
                    xf = (xfA, xfB)[par]
                    d1 = R.op("vector", lambda E, ps=ps, rr=rr, xf=xf: E.tensor_tensor(
                        rr, ps, xf, op=ALU.add),
                        waits=[pe_w, xf_tick[par], res_tick[par]], inc="s_dve")
                    xf_free[par] = d1
                    if m + 2 < 32:
                        fetch_x(m + 2)
                    res_tick[par] = R.dma("gpsimd", R1[m * 128:(m + 1) * 128, thc],
                                          rr, waits=[d1], sem=("s_resA", "s_resB")[par])
                    stg = (stgA, stgB)[par]
                    a1 = R.op("scalar", lambda E, rr=rr, stg=stg: E.activation(
                        stg, rr, AF.Copy),
                        waits=[d1, st["tail_tick"][par]], inc="s_act")
                    sq = (sqA, sqB)[par]
                    d2 = R.op("vector", lambda E, stg=stg, sq=sq: E.tensor_tensor(
                        sq, stg, stg, op=ALU.mult), waits=[a1], inc="s_dve")

                    def tail(m=m, stg=stg, sq=sq, a1=a1, d2=d2, par=par):
                        stt = st["first"]
                        st["first"] = False
                        w_extra = R.fence(("s_dve",)) if stt else []
                        for nt in range(2):
                            R.op("tensor", lambda E, stg=stg, nt=nt, stt=stt, m=m: E.matmul(
                                ps_sx[:, ts2(nt)], onesCb, stg[:, ts2(nt)],
                                start=stt, stop=(m == 31)),
                                waits=[a1] + w_extra)
                            t = R.op("tensor", lambda E, sq=sq, nt=nt, stt=stt, m=m: E.matmul(
                                ps_sq[:, ts2(nt)], onesCb, sq[:, ts2(nt)],
                                start=stt, stop=(m == 31)), waits=[d2],
                                inc="s_pe")
                            if nt == 1:
                                st["tail_tick"][par] = t
                                if m == 31:
                                    st["final"] = t
                    return [d1], tail

                gemm(32, 32, mk_stripe(w_o, 32),
                     lambda k, nt: ABF[:, k * TH + nt * 512: k * TH + nt * 512 + 512],
                     ps_g, epi, rhs_wait_of=lambda k: at[k // 8], stripes=str3)
                return st["final"]

            def corr_pass(th, thc, mu_tick):
                ticks = [None] * 32
                dv = [None, None]
                r1f = R.fence(("s_resA", "s_resB", "s_pe"))
                for b in range(8):
                    par = b % 2
                    cx = cx4[:, par * 4 * TH:(par + 1) * 4 * TH]
                    w = [dv[par]] + (list(r1f) if b < 2 else [])
                    ld = R.dma("sync", cx.rearrange("p (c t) -> p c t", c=4),
                               R1[b * 512:(b + 1) * 512, thc]
                               .rearrange("(c p) t -> p c t", p=128),
                               waits=w, sem=("s_ca", "s_cb")[par])
                    for j in range(4):
                        k = 4 * b + j
                        dv[par] = R.op("vector", lambda E, cx=cx, j=j, k=k: E.tensor_tensor(
                            ABF[:, k * TH:(k + 1) * TH],
                            cx[:, j * TH:(j + 1) * TH], mu1_bc, op=ALU.subtract),
                            waits=[ld, mu_tick], inc="s_dve")
                        ticks[k] = dv[par]
                return ticks

            def run_ffn(th, thc, corr):
                st = {"first": True, "final": None,
                      "tail_tick": [("s_pe", R.cnt["s_pe"])] * 2}

                def epi(m, ps, pe_w):
                    par = m % 2
                    rr = (evA, evB)[par]
                    t1 = R.op("vector", lambda E, ps=ps, m=m: E.tensor_tensor(
                        evC, ps, ABF[:, m * TH:(m + 1) * TH], op=ALU.add),
                        waits=[pe_w], inc="s_dve")
                    R.op("vector", lambda E: E.tensor_tensor(
                        evC, evC, rs1_bc, op=ALU.mult))
                    d1 = R.op("vector", lambda E, rr=rr, m=m: E.tensor_scalar(
                        rr, evC, bias_c[:, m:m + 1], None, op0=ALU.add), inc="s_dve")
                    rb = (stgA, stgB)[par]
                    a1 = R.op("scalar", lambda E, rr=rr, rb=rb: E.activation(
                        rb, rr, AF.Copy),
                        waits=[d1, st["tail_tick"][par], res_tick[par]], inc="s_act")
                    res_tick[par] = R.dma("gpsimd", R2[m * 128:(m + 1) * 128, thc],
                                          rb, waits=[a1], sem=("s_resA", "s_resB")[par])
                    sq = (sqA, sqB)[par]
                    d2 = R.op("vector", lambda E, rb=rb, sq=sq: E.tensor_tensor(
                        sq, rb, rb, op=ALU.mult), waits=[a1], inc="s_dve")

                    def tail(m=m, rb=rb, sq=sq, a1=a1, d2=d2, par=par):
                        stt = st["first"]
                        st["first"] = False
                        w_extra = R.fence(("s_dve",)) if stt else []
                        for nt in range(2):
                            R.op("tensor", lambda E, rb=rb, nt=nt, stt=stt, m=m: E.matmul(
                                ps_sx[:, ts2(nt)], onesCb, rb[:, ts2(nt)],
                                start=stt, stop=(m == 31)),
                                waits=[a1] + w_extra)
                            t = R.op("tensor", lambda E, sq=sq, nt=nt, stt=stt, m=m: E.matmul(
                                ps_sq[:, ts2(nt)], onesCb, sq[:, ts2(nt)],
                                start=stt, stop=(m == 31)), waits=[d2],
                                inc="s_pe")
                            if nt == 1:
                                st["tail_tick"][par] = t
                                if m == 31:
                                    st["final"] = t
                    return [t1], tail

                gemm(32, 32,
                     mk_stripe(WD, 32, static_waits=lambda: R.fence(("s_wd",))),
                     lambda k, nt: ABF[:, k * TH + nt * 512: k * TH + nt * 512 + 512],
                     ps_g, epi, rhs_wait_of=lambda k: corr[k], stripes=str3)
                return st["final"]

            def out_loop(th, thc):
                tickO = [None, None]
                wfree = [None, None]
                r2_fence = R.fence(("s_resA", "s_resB", "s_dve"))
                for b in range(8):
                    par = b % 2
                    ox = ox4[:, par * 4 * TH:(par + 1) * 4 * TH]
                    w0 = [tickO[par]] + (list(r2_fence) if b < 2 else [])
                    t = R.dma("sync", ox.rearrange("p (c t) -> p c t", c=4),
                              R2[b * 512:(b + 1) * 512, thc]
                              .rearrange("(c p) t -> p c t", p=128),
                              waits=w0, sem=("s_ra", "s_rb")[par])
                    tickO[par] = t
                    for half in range(2):
                        ouw = ouf[:, half * 2 * TH:(half + 1) * 2 * TH]
                        d2 = None
                        for j in range(2):
                            oxj = ox[:, (half * 2 + j) * TH:(half * 2 + j + 1) * TH]
                            R.op("vector", lambda E, oxj=oxj: E.tensor_tensor(
                                oev, oxj, mu2_bc, op=ALU.subtract),
                                waits=[t, wfree[half]])
                            d2 = R.op("vector", lambda E, oxj=oxj, ouw=ouw, j=j:
                                      E.tensor_tensor(
                                          ouw[:, j * TH:(j + 1) * TH], oev, rs2_bc,
                                          op=ALU.mult), inc="s_dve")
                        wfree[half] = R.dma(
                            "sync",
                            outT[b * 512 + half * 256:(b * 512 + half * 256) + 256, thc]
                            .rearrange("(c p) t -> p c t", p=128),
                            ouw.rearrange("p (c t) -> p c t", c=2),
                            waits=[d2], sem=("s_oA", "s_oB")[half])

            def load_at(thc):
                at = []
                for q in range(4):
                    at.append(R.dma(
                        "sync",
                        ABF[:, q * 8 * TH:(q + 1) * 8 * TH].rearrange("p (k t) -> p k t", k=8),
                        ASC[q * 1024:(q + 1) * 1024, thc].rearrange("(k p) t -> p k t", p=128),
                        waits=R.fence(("s_pe", "s_dve", "s_act", "s_wA", "s_wB"))
                        if q == 0 else (), sem="s_abfs"))
                return at

            thc0 = slice(0, TH)
            thc1 = slice(TH, 2 * TH)
            at = load_at(thc0)
            prefetched = False
            for th in range(2):
                thc = (thc0, thc1)[th]
                fin = run_wo(th, thc, at, prefetched=prefetched)
                mu_t, rs_t = ln_rows_and_bcast(fin, mu1_bc, rs1_bc)
                corr = corr_pass(th, thc, mu_t)
                fin2 = run_ffn(th, thc, corr)
                mu2_t, rs2_t = ln_rows_and_bcast(fin2, mu2_bc, rs2_bc)
                if th == 0:
                    at = load_at(thc1)
                    wo_fetch_x(0, thc1)
                    wo_fetch_x(1, thc1)
                    prefetched = True
                out_loop(th, thc)
            R.flush()

    def finale():
        allsems = tuple(R.sems.keys())
        for name in debug_taps:
            R.dma("gpsimd", dbg[name], scratch[name], waits=R.fence(allsems),
                  sem="s_smg")
        R.op("gpsimd", lambda E: E.nop(), waits=R.fence(allsems))
        R.flush()

    phases = [("p0", phase0),
              ("p1t0", lambda: phase1(0)), ("p1t1", lambda: phase1(1)),
              ("p2", phase2),
              ("p3", phase3)]
    for name, fn in phases:
        fn()
        if name == "p1t1":
            trig_stack.close()
        if name == "p2":
            p12_stack.close()
        if last_phase is not None and name == last_phase:
            break
    finale()
    return nc


def _host_prep(inputs):
    x = np.asarray(inputs["L0_attn_hidden_states"])
    w_qa = np.ascontiguousarray(np.asarray(inputs["L0_attn_w_qa"]))
    g_qa = np.ascontiguousarray(np.asarray(inputs["L0_attn_g_qa"]))
    w_qb = np.asarray(inputs["L0_attn_w_qb"])
    w_kva = np.ascontiguousarray(np.asarray(inputs["L0_attn_w_kva"]))
    g_kva = np.ascontiguousarray(np.asarray(inputs["L0_attn_g_kva"]))
    w_kvb = np.asarray(inputs["L0_attn_w_kvb"])
    w_o = np.ascontiguousarray(np.asarray(inputs["L0_attn_w_o"]))
    qw = np.asarray(inputs["L0_ffn_qweight"])
    scales = np.asarray(inputs["L0_ffn_scales"]).reshape(H, 16)
    zeros = np.asarray(inputs["L0_ffn_zeros"]).reshape(H, 16)
    bias = np.ascontiguousarray(np.asarray(inputs["L0_ffn_bias"]))

    perm_q = np.empty(NH * QK, np.int64)
    for h in range(NH):
        perm_q[h * ND:(h + 1) * ND] = h * QK + np.arange(ND)
        perm_q[NH * ND + h * RD: NH * ND + (h + 1) * RD] = h * QK + ND + np.arange(RD)
    w_qb_r = np.ascontiguousarray(w_qb[:, perm_q])

    perm_k = np.empty(NH * (ND + VD), np.int64)
    for h in range(NH):
        perm_k[h * ND:(h + 1) * ND] = h * (ND + VD) + np.arange(ND)
        perm_k[NH * ND + h * VD: NH * ND + (h + 1) * VD] = h * (ND + VD) + ND + np.arange(VD)
    w_kvb_r = np.ascontiguousarray(w_kvb[:, perm_k])

    assert int(qw.min()) >= 0 and int(qw.max()) <= 255
    qwT = np.ascontiguousarray(qw.T).astype(np.uint8)
    sbcv = np.ascontiguousarray(scales.T).astype(np.float32)
    zbcv = np.ascontiguousarray(zeros.T).astype(np.float32)

    R64 = np.zeros((64, 64), np.float32)
    for j in range(32):
        R64[j, 32 + j] = -1.0
        R64[32 + j, j] = 1.0
    rotk = np.ascontiguousarray(R64.T)
    R128 = np.zeros((128, 128), np.float32)
    R128[0:64, 0:64] = R64
    R128[64:128, 64:128] = R64
    rotq = np.ascontiguousarray(R128.T)

    shared = dict(w_qa=w_qa, g_qa=g_qa, w_qb=w_qb_r, w_kva=w_kva, g_kva=g_kva,
                  w_kvb=w_kvb_r, w_o=w_o, qwT=qwT, sbc=sbcv, zbc=zbcv, bias=bias,
                  rotq=rotq, rotk=rotk)
    in_maps = []
    for b in range(x.shape[0]):
        m = dict(shared)
        m["xT"] = np.ascontiguousarray(x[b].T)
        in_maps.append(m)
    return in_maps


def kernel(**inputs):
    in_maps = _host_prep(inputs)
    nc = build_nc(tuple(DEBUG_TAPS), DEBUG_LAST_PHASE)
    res = run_bass_kernel_spmd(nc, in_maps, core_ids=list(range(len(in_maps))))
    out = np.stack([np.asarray(r["outT"]).T for r in res.results], axis=0)
    return out.astype(np.float32)
